# version 07
# baseline (speedup 1.0000x reference)
import numpy as np
from collections import defaultdict
from contextlib import ExitStack

import concourse.bass as bass
import concourse.mybir as mybir
from concourse.bass_utils import run_bass_kernel_spmd

F32 = mybir.dt.float32
BF16 = mybir.dt.bfloat16
I32 = mybir.dt.int32
I8 = mybir.dt.int8
AF = mybir.ActivationFunctionType
ALU = mybir.AluOpType
AX = mybir.AxisListType

NCORES = 8
B, S, D = 16, 2048, 2048
BL = B // NCORES
T = BL * S
NT = T // 128
HID = 5632
NHC = HID // 128
NH = 16
E3 = 3 * D
EPS1 = 1e-5
EPS2 = 1e-6
ISQ = float(1.0 / np.sqrt(128.0))
KB = 1024
ARENA_B = 204 * KB

ENGINES = ("sync", "gpsimd", "tensor", "vector", "scalar")
DT_SIZE = {F32: 4, BF16: 2, I32: 4, I8: 1}


def _prod(xs):
    p = 1
    for x in xs:
        p *= x
    return p


class Plan:
    def __init__(self, nc):
        self.nc = nc
        self.q = {e: [] for e in ENGINES}
        self.cnt = defaultdict(int)
        self.hwm = {e: defaultdict(int) for e in ENGINES}
        self.sems = {}
        self.pending = {e: [] for e in ENGINES}
        self.dma_sems = {}

    def tok(self, s):
        return (s, self.cnt[s])

    def dma(self, eng, key, fn, waits=()):
        name = "d_" + key
        if name not in self.dma_sems:
            self.dma_sems[name] = eng
        return self.op(eng, fn, waits, inc=name, amt=16)

    def op(self, eng, fn, waits=(), inc=None, amt=1):
        wl = []
        allw = list(self.pending[eng]) + [w for w in waits if w is not None]
        self.pending[eng] = []
        for (s, v) in allw:
            if v <= 0:
                continue
            if v > self.hwm[eng][s]:
                self.hwm[eng][s] = v
                wl.append((s, v))
        t = None
        if inc is not None:
            self.cnt[inc] += amt
            t = (inc, self.cnt[inc])
        self.q[eng].append((tuple(wl), fn, inc, amt))
        return t

    def barrier(self):
        nop = lambda e: e.nop()
        sync_w = [(n, self.cnt[n]) for n, eng in self.dma_sems.items() if eng == "sync"]
        gps_w = [(n, self.cnt[n]) for n, eng in self.dma_sems.items() if eng == "gpsimd"]
        self.op("sync", nop, waits=sync_w, inc="b")
        self.op("gpsimd", nop, waits=gps_w, inc="b")
        self.op("tensor", nop, inc="b")
        self.op("vector", nop, inc="b")
        self.op("scalar", nop, inc="b")
        tgt = self.cnt["b"]
        for e in ENGINES:
            self.pending[e].append(("b", tgt))

    def realize(self, ctx):
        for name in ("pe", "ve", "se", "ge", "b"):
            self.sems[name] = ctx.enter_context(self.nc.semaphore("ksem_" + name))
        for name in self.dma_sems:
            self.sems[name] = ctx.enter_context(self.nc.semaphore("ksem" + name))
        block = ctx.enter_context(self.nc.Block())
        sems = self.sems

        def runner(qlist):
            def run(e):
                for wl, fn, inc, amt in qlist:
                    for s, v in wl:
                        e.wait_ge(sems[s], v)
                    ins = fn(e)
                    if inc is not None:
                        ins.then_inc(sems[inc], amt)
            return run

        block.sync(runner(self.q["sync"]))
        block.gpsimd(runner(self.q["gpsimd"]))
        block.tensor(runner(self.q["tensor"]))
        block.vector(runner(self.q["vector"]))
        block.scalar(runner(self.q["scalar"]))


class Arena:
    def __init__(self, ap):
        self.ap = ap

    def at(self, off_b, shape, dtype):
        n = _prod(shape[1:]) * DT_SIZE[dtype]
        assert off_b % 64 == 0, off_b
        assert off_b + n <= ARENA_B, (off_b, n)
        v = self.ap[:, off_b:off_b + n].bitcast(dtype)
        if len(shape) == 2:
            return v
        if len(shape) == 3:
            return v.rearrange("p (a b) -> p a b", a=shape[1])
        if len(shape) == 4:
            return v.rearrange("p (a b c) -> p a b c", a=shape[1], b=shape[2])
        raise ValueError(shape)


DBG = False


def build_nc():
    nc = bass.Bass("TRN2", target_bir_lowering=False)

    X = nc.dram_tensor("k_x", [BL, S, D], F32, kind="ExternalInput").ap()
    WQKV = nc.dram_tensor("k_wqkv", [D, E3], F32, kind="ExternalInput").ap()
    QG = nc.dram_tensor("k_qg", [D], F32, kind="ExternalInput").ap()
    KG = nc.dram_tensor("k_kg", [D], F32, kind="ExternalInput").ap()
    WO = nc.dram_tensor("k_wo", [D, D], F32, kind="ExternalInput").ap()
    WQGATE = nc.dram_tensor("k_wqg", [HID, D], I32, kind="ExternalInput").ap()
    SG = nc.dram_tensor("k_sg", [HID], F32, kind="ExternalInput").ap()
    WQUP = nc.dram_tensor("k_wqu", [HID, D], I32, kind="ExternalInput").ap()
    SU = nc.dram_tensor("k_su", [HID], F32, kind="ExternalInput").ap()
    WQDOWN = nc.dram_tensor("k_wqd", [D, HID], I32, kind="ExternalInput").ap()
    SD = nc.dram_tensor("k_sd", [D], F32, kind="ExternalInput").ap()
    Y = nc.dram_tensor("k_y", [BL, S, D], F32, kind="ExternalOutput").ap()

    scr_hT = nc.dram_tensor("k_scr_ht", [16, 128, T], BF16, kind="Internal").ap()
    scr_qT = nc.dram_tensor("k_scr_qt", [BL, 16, 128, S], BF16, kind="Internal").ap()
    scr_kT = nc.dram_tensor("k_scr_kt", [BL, 16, 128, S], BF16, kind="Internal").ap()
    scr_v = nc.dram_tensor("k_scr_v", [BL, 16, 128, D], BF16, kind="Internal").ap()
    scr_cT = nc.dram_tensor("k_scr_ct", [BL, 16, 128, S], BF16, kind="Internal").ap()
    scr_xp = nc.dram_tensor("k_scr_xp", [T, D], F32, kind="Internal").ap()
    scr_h2T = nc.dram_tensor("k_scr_h2t", [16, 128, T], BF16, kind="Internal").ap()
    scr_tT = nc.dram_tensor("k_scr_tt", [NHC, 128, T], BF16, kind="Internal").ap()
    scr_wdT = nc.dram_tensor("k_scr_wdt", [NHC, 128, D], BF16, kind="Internal").ap()

    with ExitStack() as ctx:
        ident_bf = ctx.enter_context(nc.sbuf_tensor("k_identb", [128, 128], BF16))
        ident_f = ctx.enter_context(nc.sbuf_tensor("k_identf", [128, 128], F32))
        cmask = ctx.enter_context(nc.sbuf_tensor("k_cmask", [128, 128], F32))
        vecs = ctx.enter_context(nc.sbuf_tensor("k_vecs", [128, 240], F32))
        smalls = ctx.enter_context(nc.sbuf_tensor("k_smalls", [128, 64], F32))
        vraw = ctx.enter_context(nc.sbuf_tensor("k_vraw", [44, 128], F32))
        arena_t = ctx.enter_context(nc.sbuf_tensor("k_arena", [128, ARENA_B], I8))
        psA_t = ctx.enter_context(nc.psum_tensor("k_psA", [128, 2048], F32))
        psB_t = ctx.enter_context(nc.psum_tensor("k_psB", [128, 2048], F32))
        A = Arena(arena_t[:, :])
        psA = psA_t[:, :]
        psB = psB_t[:, :]
        psAb = psA.bitcast(BF16)
        psBb = psB.bitcast(BF16)

        qg_sb = vecs[:, 0:16]
        kg_sb = vecs[:, 16:32]
        sd_sb = vecs[:, 32:48]
        sg_sb = vecs[:, 64:108]
        su_sb = vecs[:, 152:196]

        s12 = [smalls[:, 0:2], smalls[:, 2:4]]
        meanex = [smalls[:, 4:6], smalls[:, 6:8]]
        m2 = [smalls[:, 8:9], smalls[:, 9:10]]
        varv = [smalls[:, 10:11], smalls[:, 11:12]]
        sqv = [smalls[:, 12:13], smalls[:, 13:14]]
        rstd = [smalls[:, 14:15], smalls[:, 15:16]]
        pmax = [smalls[:, 16 + i:17 + i] for i in range(4)]
        negmax = [smalls[:, 20 + i:21 + i] for i in range(4)]
        zacc = [smalls[:, 24 + i:25 + i] for i in range(8)]
        rz = [smalls[:, 32 + i:33 + i] for i in range(4)]
        eps_t = {EPS1: smalls[:, 36:37], EPS2: smalls[:, 37:38]}
        warm_in = smalls[:, 40:44]
        warm_out = smalls[:, 44:48]

        P = Plan(nc)

        ln_state = {}

        def ln_stage1(tag, sl, src_ap, junk_ap, eps, src_tok=None):
            st = ln_state.setdefault(tag, {
                "th": [None, None], "t5": [None, None], "t6": [None, None],
                "t7": [None, None], "junk": None})
            t1 = P.op("vector", lambda e: e.reduce_sum(
                s12[sl][:, 0:1], src_ap, axis=AX.X),
                waits=[src_tok, st["th"][sl]], inc="ve")
            t2 = P.op("vector", lambda e: e.scalar_tensor_tensor(
                out=junk_ap, in0=src_ap, scalar=1.0, in1=src_ap,
                op0=ALU.mult, op1=ALU.mult, accum_out=s12[sl][:, 1:2]),
                waits=[st["junk"]], inc="ve")
            st["junk"] = t2
            t3 = P.op("vector", lambda e: e.tensor_scalar(
                out=meanex[sl], in0=s12[sl], scalar1=1.0 / D, scalar2=None,
                op0=ALU.mult), waits=[t1, t2], inc="ve")
            t4 = P.op("vector", lambda e: e.tensor_mul(
                m2[sl], meanex[sl][:, 0:1], meanex[sl][:, 0:1]),
                waits=[t3, st["t5"][sl]], inc="ve")
            t5 = P.op("vector", lambda e: e.scalar_tensor_tensor(
                out=varv[sl], in0=m2[sl], scalar=-1.0, in1=meanex[sl][:, 1:2],
                op0=ALU.mult, op1=ALU.add), waits=[t4, st["t6"][sl]], inc="ve")
            st["t5"][sl] = t5
            t6 = P.op("scalar", lambda e, eps=eps: e.activation(
                out=sqv[sl], in_=varv[sl], func=AF.Sqrt, bias=eps_t[eps], scale=1.0),
                waits=[t5, st["th"][sl], st["t7"][sl]], inc="se")
            st["t6"][sl] = t6
            return t6

        def ln_stage2(tag, sl, src_ap, dst_ap, t_sqrt, extra=()):
            st = ln_state[tag]
            t7 = P.op("vector", lambda e: e.reciprocal(rstd[sl], sqv[sl]),
                      waits=[t_sqrt], inc="ve")
            st["t7"][sl] = t7
            th = P.op("vector", lambda e: e.tensor_scalar(
                out=dst_ap, in0=src_ap, scalar1=meanex[sl][:, 0:1],
                scalar2=rstd[sl], op0=ALU.subtract, op1=ALU.mult),
                waits=[t7] + [w for w in extra if w is not None], inc="ve")
            st["th"][sl] = th
            return th

        def emit_memsets(e):
            e.memset(smalls[:, 0:36], 1.0)
            e.memset(smalls[:, 38:64], 1.0)
            e.memset(eps_t[EPS1], EPS1)
            e.memset(eps_t[EPS2], EPS2)
            e.memset(ident_bf[:, :], 0.0)
            e.memset(ident_f[:, :], 0.0)
            return e.memset(cmask[:, :], 0.0)
        t_ms = P.op("gpsimd", emit_memsets, inc="ge")

        def emit_selects(e):
            e.affine_select(out=ident_bf[:, :], in_=ident_bf[:, :],
                            compare_op=ALU.not_equal, fill=1.0, base=0,
                            pattern=[[-1, 128]], channel_multiplier=1)
            e.affine_select(out=ident_f[:, :], in_=ident_f[:, :],
                            compare_op=ALU.not_equal, fill=1.0, base=0,
                            pattern=[[-1, 128]], channel_multiplier=1)
            return e.affine_select(out=cmask[:, :], in_=cmask[:, :],
                                   compare_op=ALU.is_ge, fill=-1e30, base=0,
                                   pattern=[[-1, 128]], channel_multiplier=1)
        t_ident = P.op("gpsimd", emit_selects, waits=[t_ms], inc="ge")
        P.pending["vector"].append(t_ident)
        P.pending["scalar"].append(t_ident)

        warm_prev = [None]

        def warm_act(func):
            def warm(e, func=func):
                ins = None
                for j in range(10):
                    ins = e.activation(out=smalls[:, 44 + j:45 + j],
                                       in_=smalls[:, 40:41],
                                       func=func, bias=0.0, scale=0.0)
                return ins
            t = P.op("scalar", warm, waits=[warm_prev[0]], inc="se")
            warm_prev[0] = t
            return t

        warm_act(AF.Sqrt)

        vec_tp_prev = [None]

        def load_vec(src_ap, nch, dst, scale=None):
            tl = P.dma("sync", "vr", lambda e, ap=src_ap, n=nch: e.dma_start(
                out=vraw[0:n, :], in_=ap.rearrange("(c p) -> c p", p=128)),
                waits=[vec_tp_prev[0]])
            tp = P.op("tensor", lambda e, n=nch: e.transpose(
                psB[:, 1536:1536 + n], vraw[0:n, :], ident_f[0:n, 0:n]),
                waits=[tl, t_ident, P.tok("ve")], inc="pe")
            vec_tp_prev[0] = tp
            if scale is None:
                tv = P.op("vector", lambda e, d=dst, n=nch: e.tensor_copy(
                    out=d, in_=psB[:, 1536:1536 + n]), waits=[tp], inc="ve")
            else:
                tv = P.op("vector", lambda e, d=dst, n=nch, sc=scale: e.tensor_scalar(
                    out=d, in0=psB[:, 1536:1536 + n], scalar1=sc, scalar2=None,
                    op0=ALU.mult), waits=[tp], inc="ve")
            return tv

        t_qg = load_vec(QG, 16, qg_sb, scale=ISQ)
        t_kg = load_vec(KG, 16, kg_sb)
        t_sd = load_vec(SD, 16, sd_sb)
        t_sg = load_vec(SG, 44, sg_sb)
        t_su = load_vec(SU, 44, su_sb)

        wd_i32 = A.at(132 * KB, [128, HID], I32)
        wd_bf = A.at(156 * KB, [128, HID], BF16)
        wdT_stage = A.at(168 * KB, [128, NHC, 128], BF16)
        war_wdi = None
        war_wdbf = None
        war_stage0 = None
        for dt in range(16):
            tl = P.dma("sync", "wd", lambda e, dt=dt: e.dma_start(
                out=wd_i32, in_=WQDOWN[dt * 128:(dt + 1) * 128, :]),
                waits=[war_wdi])
            tdq = P.op("vector", lambda e, dt=dt: e.tensor_scalar(
                out=wd_bf, in0=wd_i32, scalar1=-128.0, scalar2=sd_sb[:, dt:dt + 1],
                op0=ALU.add, op1=ALU.mult),
                waits=[tl, t_sd, war_wdbf], inc="ve")
            war_wdi = tdq
            tev_last = None
            tp_hist = []
            for g in range((NHC + 7) // 8):
                cnt = min(8, NHC - g * 8)
                gb = (g % 2) * 1024

                def tpf(e, g=g, cnt=cnt, gb=gb):
                    ins = None
                    for j in range(cnt):
                        ins = e.transpose(psBb[:, gb + j * 128: gb + (j + 1) * 128],
                                          wd_bf[:, (g * 8 + j) * 128:(g * 8 + j + 1) * 128],
                                          ident_bf[:, :])
                    return ins
                tp = P.op("tensor", tpf,
                          waits=[tdq, t_ident,
                                 (tp_hist[-2][1] if len(tp_hist) >= 2 else None),
                                 war_stage0], inc="pe")
                tev = P.op("vector", lambda e, g=g, cnt=cnt, gb=gb: e.tensor_copy(
                    out=wdT_stage[:, g * 8:g * 8 + cnt, :],
                    in_=psBb[:, gb:gb + cnt * 128].rearrange("p (c t) -> p c t", c=cnt)),
                    waits=[tp], inc="ve")
                tp_hist.append((tp, tev))
                tev_last = tev
            war_wdbf = tp_hist[-1][0]
            tst = P.dma("sync", "wds", lambda e, dt=dt: e.dma_start(
                out=scr_wdT[:, :, dt * 128:(dt + 1) * 128].rearrange("c p t -> p c t"),
                in_=wdT_stage[:, :, :]),
                waits=[tev_last])
            war_stage0 = tst
        t_wdT_done = war_stage0

        xa = [A.at(180 * KB, [128, D], F32), A.at(188 * KB, [128, D], F32)]
        h_bf = [A.at(196 * KB, [128, D], BF16), A.at(200 * KB, [128, D], BF16)]
        hT_stage = [A.at(68 * KB, [128, 16, 128], BF16), A.at(76 * KB, [128, 16, 128], BF16)]
        junk_bf = A.at(72 * KB, [128, D], BF16)
        war_x = [None, None]
        war_hstage = [None, None]
        pre_sqrt = {}
        pre_th = {}
        tp_pre = {}
        tev_pre = {}
        t_sthT = [None] * NT

        def pre_stage2(i):
            sl = i % 2
            th = ln_stage2("pre", sl, xa[sl], h_bf[sl], pre_sqrt[i],
                           extra=[tp_pre.get(i - 2)])
            pre_th[i] = th
            war_x[sl] = th

        def pre_T(i):
            sl = i % 2

            def tpf(e, sl=sl):
                ins = None
                for c in range(16):
                    ins = e.matmul(psA[:, c * 128:(c + 1) * 128],
                                   h_bf[sl][:, c * 128:(c + 1) * 128],
                                   ident_bf[:, :], start=True, stop=True)
                return ins
            tp = P.op("tensor", tpf, waits=[pre_th[i], t_ident, tev_pre.get(i - 1)],
                      inc="pe")
            tp_pre[i] = tp

        def pre_evac(i):
            sl = i % 2
            tev = P.op("vector", lambda e, sl=sl: e.tensor_copy(
                out=hT_stage[sl][:, :, :],
                in_=psA[:, 0:2048].rearrange("p (c t) -> p c t", c=16)),
                waits=[tp_pre[i], war_hstage[sl]], inc="ve")
            tev_pre[i] = tev
            tst = P.dma("sync", "hts%d" % sl, lambda e, i=i, sl=sl: e.dma_start(
                out=scr_hT[:, :, i * 128:(i + 1) * 128].rearrange("c p t -> p c t"),
                in_=hT_stage[sl][:, :, :]),
                waits=[tev])
            war_hstage[sl] = tst
            t_sthT[i] = tst

        for i in range(NT):
            sl = i % 2
            s, blk = divmod(i, 16)
            row0 = blk * 128
            tx = P.dma("sync", "xa%d" % sl, lambda e, s=s, row0=row0, sl=sl: e.dma_start(
                out=xa[sl], in_=X[s, row0:row0 + 128, :]),
                waits=[war_x[sl]])
            if i >= 2:
                pre_evac(i - 2)
            if i >= 1:
                pre_stage2(i - 1)
                pre_T(i - 1)
            P.pending["vector"].append(tx)
            pre_sqrt[i] = ln_stage1("pre", sl, xa[sl], junk_bf, EPS1)
        pre_evac(NT - 2)
        pre_stage2(NT - 1)
        pre_T(NT - 1)
        pre_evac(NT - 1)

        P.barrier()

        wr_sb = [A.at(4 * KB, [128, 16, 2048], BF16), A.at(68 * KB, [128, 16, 2048], BF16)]
        hT_t = [A.at(132 * KB, [128, 16, 256], BF16), A.at(140 * KB, [128, 16, 256], BF16)]
        q32s = [A.at(148 * KB, [128, D], F32), A.at(156 * KB, [128, D], F32)]
        qkvT_stage = [A.at(164 * KB, [128, 16, 128], BF16), A.at(168 * KB, [128, 16, 128], BF16)]
        junk2 = A.at(172 * KB, [128, D], BF16)
        q_bf = [A.at(176 * KB, [128, D], BF16), A.at(180 * KB, [128, D], BF16)]

        t_w = [None, None, None]
        t_w[0] = P.dma("gpsimd", "wr0", lambda e: e.dma_start(
            out=wr_sb[0], in_=WQKV[:, 0:2048].rearrange("(c p) e -> p c e", p=128)))
        t_w[1] = P.dma("gpsimd", "wr1", lambda e: e.dma_start(
            out=wr_sb[1], in_=WQKV[:, 2048:4096].rearrange("(c p) e -> p c e", p=128)))

        last_st_q = [None] * BL
        last_st_k = [None] * BL
        last_st_v = [None] * BL
        war_hT = [None, None]
        war_ps = [None, None]
        war_qstage = [None, None]
        war_vbf = [None, None]
        qk_tp = {}
        qk_th = {}
        qk_sqrt = {}
        t_ht = {}

        def qk_stage2(r, i):
            sl = i % 2
            th = ln_stage2("qk", sl, q32s[sl], q_bf[sl],
                           qk_sqrt[(r, i)], extra=[qk_tp.get((r, i - 2)),
                                                   qk_tp.get((r - 1, NT - 2 + sl)) if i < 2 else None])
            qk_th[(r, i)] = th

        def qk_T(r, i):
            sl = i % 2
            ps = psA if sl == 0 else psB

            def tpf(e, sl=sl, ps=ps):
                ins = None
                for c in range(16):
                    ins = e.matmul(ps[:, c * 128:(c + 1) * 128],
                                   q_bf[sl][:, c * 128:(c + 1) * 128],
                                   ident_bf[:, :], start=True, stop=True)
                return ins
            tp = P.op("tensor", tpf, waits=[qk_th[(r, i)]], inc="pe")
            qk_tp[(r, i)] = tp

        def qk_gevac(r, i):
            sl = i % 2
            ps = psA if sl == 0 else psB
            s, blk = divmod(i, 16)
            row0 = blk * 128
            gam = qg_sb if r == 0 else kg_sb
            gam_b = gam.unsqueeze(2).broadcast_to([128, 16, 128])

            def gevf(e, sl=sl, ps=ps, gam_b=gam_b):
                return e.tensor_tensor(
                    out=qkvT_stage[sl][:, :, :],
                    in0=ps[:, 0:2048].rearrange("p (c t) -> p c t", c=16),
                    in1=gam_b, op=ALU.mult)
            tgev = P.op("vector", gevf,
                        waits=[qk_tp[(r, i)], war_qstage[sl],
                               (t_qg if r == 0 else t_kg)], inc="ve")
            war_ps[sl] = tgev
            scr = scr_qT if r == 0 else scr_kT
            tst = P.dma("sync", "qks%d" % sl, lambda e, s=s, row0=row0, sl=sl, scr=scr: e.dma_start(
                out=scr[s, :, :, row0:row0 + 128].rearrange("c p t -> p c t"),
                in_=qkvT_stage[sl][:, :, :]),
                waits=[tgev])
            war_qstage[sl] = tst
            if r == 0:
                last_st_q[s] = tst
            else:
                last_st_k[s] = tst

        for r in range(3):
            for i in range(NT):
                sl = i % 2
                s, blk = divmod(i, 16)
                jp = i // 2
                hsl = jp % 2
                if i % 2 == 0:
                    w_pre = [t_sthT[i], t_sthT[i + 1]] if r == 0 else []
                    t_ht[(r, jp)] = P.dma("sync", "ht%d" % hsl, lambda e, i=i, hsl=hsl: e.dma_start(
                        out=hT_t[hsl], in_=scr_hT[:, :, i * 128:(i + 2) * 128]
                        .rearrange("c p t -> p c t")),
                        waits=w_pre + [war_hT[hsl]])
                ps = psA if sl == 0 else psB
                toff = (i % 2) * 128

                if r <= 1:
                    if i >= 1:
                        qk_stage2(r, i - 1)
                    if i >= 2:
                        qk_gevac(r, i - 2)

                def mmf(e, hsl=hsl, toff=toff, r=r, ps=ps):
                    ins = None
                    for d in range(16):
                        for es in range(4):
                            ins = e.matmul(ps[:, es * 512:(es + 1) * 512],
                                           hT_t[hsl][:, d, toff:toff + 128],
                                           wr_sb[r % 2][:, d, es * 512:(es + 1) * 512],
                                           start=(d == 0), stop=(d == 15))
                    return ins
                tmm = P.op("tensor", mmf,
                           waits=[t_ht[(r, jp)], t_w[r], war_ps[sl]], inc="pe")
                if i % 2 == 1:
                    war_hT[hsl] = tmm
                if r <= 1:
                    tcp = P.op("vector", lambda e, ps=ps, sl=sl: e.tensor_copy(
                        out=q32s[sl], in_=ps),
                        waits=[tmm, ln_state.get("qk", {}).get("th", [None, None])[sl]],
                        inc="ve")
                    qk_sqrt[(r, i)] = ln_stage1("qk", sl, q32s[sl], junk2, EPS2,
                                                src_tok=tcp)
                    if i >= 1:
                        qk_T(r, i - 1)
                else:
                    tcp = P.op("vector", lambda e, ps=ps, sl=sl: e.tensor_copy(
                        out=q_bf[sl], in_=ps),
                        waits=[tmm, war_vbf[sl], qk_tp.get((1, NT - 2 + sl))],
                        inc="ve")
                    war_ps[sl] = tcp
                    tstv = P.dma("sync", "vs%d" % sl, lambda e, s=s, blk=blk, sl=sl: e.dma_start(
                        out=scr_v[s, blk, :, :], in_=q_bf[sl]),
                        waits=[tcp])
                    war_vbf[sl] = tstv
                    last_st_v[s] = tstv
            if r <= 1:
                qk_stage2(r, NT - 1)
                qk_T(r, NT - 1)
                qk_gevac(r, NT - 2)
                qk_gevac(r, NT - 1)
            if r == 0:
                t_w[2] = P.dma("gpsimd", "wr0", lambda e: e.dma_start(
                    out=wr_sb[0], in_=WQKV[:, 4096:6144]
                    .rearrange("(c p) e -> p c e", p=128)),
                    waits=[P.tok("pe")])

        P.barrier()

        warm_act(AF.Exp)
        kT_sb = A.at(4 * KB, [128, 16, S], BF16)
        vT_sb = A.at(68 * KB, [128, 16, D], BF16)
        qT_sb = A.at(132 * KB, [128, 16, 512], BF16)
        pT_sb = A.at(148 * KB, [128, 16, 512], BF16)
        p_bf = [A.at(164 * KB, [128, S], BF16), A.at(168 * KB, [128, S], BF16)]
        psc_bf = [A.at(172 * KB, [128, S], BF16), A.at(176 * KB, [128, S], BF16),
                  A.at(196 * KB, [128, S], BF16), A.at(200 * KB, [128, S], BF16)]
        cT_h = [A.at(180 * KB, [128, 512], BF16), A.at(181 * KB, [128, 512], BF16)]
        sc_ps = [psA[:, 0:1024], psA[:, 1024:2048]]
        pt_ps = [psB[:, 0:512], psB[:, 512:1024]]
        ctx_ps = [psB[:, 1024:1536], psB[:, 1536:2048]]

        t_stctx = {}
        piece_ctr = [0]
        gctr = [0]
        war_sc = [None, None]
        war_pt = [None, None]
        war_ctx = [None, None]
        war_pbf = [None, None]
        war_psc = [None, None, None, None]
        war_negm = [None] * 4
        war_cth = [None, None]
        war_qT = None
        war_pT = None

        def emit_softmax(qt, h, sub, tqT, tkT):
            gsub = qt * 4 + sub
            k_len = (gsub + 1) * 128
            pieces = []
            k0 = 0
            while k0 < k_len:
                kw = min(1024, k_len - k0)
                pieces.append((k0, kw))
                k0 += kw
            ptoks = []
            for pidx, (k0, kw) in enumerate(pieces):
                psl = piece_ctr[0] % 2
                piece_ctr[0] += 1

                def scf(e, h=h, sub=sub, k0=k0, kw=kw, psl=psl):
                    ins = None
                    for ks in range(0, kw, 512):
                        w = min(512, kw - ks)
                        ins = e.matmul(
                            sc_ps[psl][:, ks:ks + w],
                            qT_sb[:, h, sub * 128:(sub + 1) * 128],
                            kT_sb[:, h, k0 + ks:k0 + ks + w],
                            start=True, stop=True)
                    return ins
                tsc = P.op("tensor", scf, waits=[tqT, tkT, war_sc[psl]], inc="pe")
                if k0 + kw == k_len:
                    tsc = P.op("vector", lambda e, psl=psl, kw=kw: e.tensor_tensor(
                        out=sc_ps[psl][:, kw - 128:kw],
                        in0=sc_ps[psl][:, kw - 128:kw],
                        in1=cmask[:, :], op=ALU.add),
                        waits=[tsc, t_ident], inc="ve")
                tmx = P.op("vector", lambda e, psl=psl, kw=kw, pidx=pidx: e.reduce_max(
                    pmax[pidx], sc_ps[psl][:, 0:kw], axis=AX.X),
                    waits=[tsc], inc="ve")
                ptoks.append((psl, k0, kw, tmx, pidx))
            if len(ptoks) == 1:
                tneg = P.op("vector", lambda e, sub=sub: e.tensor_scalar(
                    out=negmax[sub], in0=pmax[0], scalar1=-1.0,
                    scalar2=None, op0=ALU.mult),
                    waits=[ptoks[0][3], war_negm[sub]], inc="ve")
            else:
                tmxc = P.op("vector", lambda e: e.tensor_max(
                    pmax[0], pmax[0], pmax[1]),
                    waits=[ptoks[0][3], ptoks[1][3]], inc="ve")
                tneg = P.op("vector", lambda e, sub=sub: e.tensor_scalar(
                    out=negmax[sub], in0=pmax[0], scalar1=-1.0,
                    scalar2=None, op0=ALU.mult),
                    waits=[tmxc, war_negm[sub]], inc="ve")
            exp_toks = []
            for (psl, k0, kw, _tmx, pidx) in ptoks:
                tex = P.op("scalar", lambda e, psl=psl, k0=k0, kw=kw, sub=sub, pidx=pidx: e.activation(
                    out=p_bf[sub % 2][:, k0:k0 + kw], in_=sc_ps[psl][:, 0:kw],
                    func=AF.Exp, bias=negmax[sub], scale=1.0,
                    accum_out=zacc[sub * 2 + pidx]),
                    waits=[tneg, war_pbf[sub % 2]], inc="se")
                war_sc[psl] = tex
                exp_toks.append(tex)
            war_negm[sub] = exp_toks[-1]
            return (sub, k_len, len(ptoks), exp_toks)

        def emit_scale(info):
            sub, k_len, npieces, exp_toks = info
            zw = list(exp_toks)
            if npieces > 1:
                tza = P.op("vector", lambda e, sub=sub: e.tensor_add(
                    zacc[sub * 2], zacc[sub * 2], zacc[sub * 2 + 1]),
                    waits=exp_toks, inc="ve")
                zw = [tza]
            trz = P.op("vector", lambda e, sub=sub: e.reciprocal(
                rz[sub], zacc[sub * 2]), waits=zw, inc="ve")
            tps = P.op("vector", lambda e, sub=sub, k_len=k_len: e.tensor_scalar(
                out=psc_bf[sub][:, 0:k_len], in0=p_bf[sub % 2][:, 0:k_len],
                scalar1=rz[sub], scalar2=None, op0=ALU.mult),
                waits=[trz, war_psc[sub]], inc="ve")
            war_pbf[sub % 2] = tps
            return (sub, k_len, tps)

        def emit_T(info):
            sub, k_len, tps = info
            nkc = k_len // 128
            t_last = None
            t_last_v = None
            t_last_s = None
            for g in range(0, nkc, 4):
                cnt = min(4, nkc - g)
                gsl = gctr[0] % 2
                gctr[0] += 1

                def tpf(e, sub=sub, g=g, cnt=cnt, gsl=gsl):
                    ins = None
                    for j in range(cnt):
                        ins = e.matmul(
                            pt_ps[gsl][:, j * 128:(j + 1) * 128],
                            psc_bf[sub][:, (g + j) * 128:(g + j + 1) * 128],
                            ident_bf[:, :], start=True, stop=True)
                    return ins
                tp = P.op("tensor", tpf, waits=[tps, war_pt[gsl]], inc="pe")
                if gctr[0] % 2 == 0:
                    tev = P.op("vector", lambda e, sub=sub, g=g, cnt=cnt, gsl=gsl: e.tensor_copy(
                        out=pT_sb[:, g:g + cnt, sub * 128:(sub + 1) * 128],
                        in_=pt_ps[gsl][:, 0:cnt * 128]
                        .rearrange("p (c t) -> p c t", c=cnt)),
                        waits=[tp, war_pT], inc="ve")
                else:
                    tev = P.op("scalar", lambda e, sub=sub, g=g, cnt=cnt, gsl=gsl: e.activation(
                        out=pT_sb[:, g:g + cnt, sub * 128:(sub + 1) * 128],
                        in_=pt_ps[gsl][:, 0:cnt * 128]
                        .rearrange("p (c t) -> p c t", c=cnt),
                        func=AF.Identity), waits=[tp, war_pT], inc="se")
                war_pt[gsl] = tev
                t_last = tev
                if tev[0] == "ve":
                    t_last_v = tev
                else:
                    t_last_s = tev
            war_psc[sub] = t_last
            return (t_last_v, t_last_s)

        def emit_ctx(s, qt, h, ev_toks, tvT):
            nonlocal war_qT, war_pT
            nkc_tot = (qt + 1) * 4

            def ctxf(e, h=h, qt=qt, nkc_tot=nkc_tot):
                ins = None
                for kc in range(nkc_tot):
                    jlo = max(0, kc - qt * 4)
                    ins = e.matmul(
                        ctx_ps[h % 2][:, jlo * 128:512],
                        vT_sb[:, kc, h * 128:(h + 1) * 128],
                        pT_sb[:, kc, jlo * 128:512],
                        start=(kc == 0), stop=(kc == nkc_tot - 1))
                return ins
            tctx = P.op("tensor", ctxf, waits=list(ev_toks) + [tvT, war_ctx[h % 2]],
                        inc="pe")
            war_pT = tctx
            tcev = P.op("scalar", lambda e, h=h: e.activation(
                out=cT_h[h % 2], in_=ctx_ps[h % 2],
                func=AF.Identity), waits=[tctx, war_cth[h % 2]], inc="se")
            war_ctx[h % 2] = tcev
            q0 = qt * 512
            tstc = P.dma("sync", "cs%d" % (h % 2), lambda e, s=s, q0=q0, h=h: e.dma_start(
                out=scr_cT[s, h, :, q0:q0 + 512], in_=cT_h[h % 2]),
                waits=[tcev])
            war_cth[h % 2] = tstc
            t_stctx[(s, qt, h)] = tstc

        for s in range(BL):
            tkT = P.dma("sync", "kt", lambda e, s=s: e.dma_start(
                out=kT_sb, in_=scr_kT[s].rearrange("c p t -> p c t")),
                waits=[last_st_k[s], war_qT])
            tvT = P.dma("sync", "vt", lambda e, s=s: e.dma_start(
                out=vT_sb, in_=scr_v[s].rearrange("c p t -> p c t")),
                waits=[last_st_v[s]])
            for qt in range(4):
                q0 = qt * 512
                tqT = P.dma("sync", "qtl", lambda e, s=s, q0=q0: e.dma_start(
                    out=qT_sb, in_=scr_qT[s, :, :, q0:q0 + 512].rearrange("c p t -> p c t")),
                    waits=[last_st_q[s], war_qT])
                steps = [(h, sub) for h in range(NH) for sub in range(4)]
                sm_q = {}
                sc_q = {}
                tev_q = {}

                head_ev = {}

                def do_tail(k2):
                    h2, sub2 = steps[k2]
                    tv, tsx = emit_T(sc_q[k2])
                    cur = head_ev.setdefault(h2, [None, None])
                    if tv is not None:
                        cur[0] = tv
                    if tsx is not None:
                        cur[1] = tsx
                    if sub2 == 3:
                        emit_ctx(s, qt, h2, [t for t in cur if t is not None], tvT)

                for k, (h, sub) in enumerate(steps):
                    sm_q[k] = emit_softmax(qt, h, sub, tqT, tkT)
                    if k >= 1:
                        sc_q[k - 1] = emit_scale(sm_q[k - 1])
                    if k >= 2:
                        do_tail(k - 2)
                k = len(steps)
                sc_q[k - 1] = emit_scale(sm_q[k - 1])
                do_tail(k - 2)
                do_tail(k - 1)
                war_qT = P.tok("pe")

        P.barrier()

        warm_act(AF.Sqrt)
        wo_sb = A.at(4 * KB, [128, 16, D], BF16)
        cT_t = [A.at(68 * KB, [128, 16, 128], BF16), A.at(72 * KB, [128, 16, 128], BF16)]
        x2a = [A.at(76 * KB, [128, D], F32), A.at(84 * KB, [128, D], F32)]
        xpa = [A.at(92 * KB, [128, D], F32), A.at(100 * KB, [128, D], F32)]
        h2_bf = [A.at(108 * KB, [128, D], BF16), A.at(112 * KB, [128, D], BF16)]
        h2T_stage = [A.at(116 * KB, [128, 16, 128], BF16), A.at(120 * KB, [128, 16, 128], BF16)]
        junk3 = A.at(124 * KB, [128, D], BF16)

        t_wo = P.dma("gpsimd", "wo", lambda e: e.dma_start(
            out=wo_sb, in_=WO.rearrange("(c p) e -> p c e", p=128)))
        war_ct2 = [None, None]
        war_x2 = [None, None]
        war_xp = [None, None]
        war_h2stage = [None, None]
        war_pso = [None, None]
        b_tp = {}
        b_sqrt = {}
        b_th = {}
        b_txp = {}
        t_stxp = [None] * NT
        t_sth2 = [None] * NT

        def b_stage2(i):
            sl = i % 2
            th = ln_stage2("ln2", sl, xpa[sl], h2_bf[sl], b_sqrt[i],
                           extra=[b_tp.get(i - 2)])
            b_th[i] = th
            war_xp[sl] = th

        def b_T(i):
            sl = i % 2
            ps = psA if sl == 0 else psB

            def tpf(e, sl=sl, ps=ps):
                ins = None
                for c in range(16):
                    ins = e.matmul(ps[:, c * 128:(c + 1) * 128],
                                   h2_bf[sl][:, c * 128:(c + 1) * 128],
                                   ident_bf[:, :], start=True, stop=True)
                return ins
            tp = P.op("tensor", tpf, waits=[b_th[i], b_txp[i]], inc="pe")
            b_tp[i] = tp

        def b_evac(i):
            sl = i % 2
            ps = psA if sl == 0 else psB
            tev = P.op("vector", lambda e, sl=sl, ps=ps: e.tensor_copy(
                out=h2T_stage[sl][:, :, :],
                in_=ps[:, 0:2048].rearrange("p (c t) -> p c t", c=16)),
                waits=[b_tp[i], war_h2stage[sl]], inc="ve")
            war_pso[sl] = tev
            tst2 = P.dma("sync", "h2s%d" % sl, lambda e, i=i, sl=sl: e.dma_start(
                out=scr_h2T[:, :, i * 128:(i + 1) * 128].rearrange("c p t -> p c t"),
                in_=h2T_stage[sl][:, :, :]),
                waits=[tev])
            war_h2stage[sl] = tst2
            t_sth2[i] = tst2

        for i in range(NT):
            sl = i % 2
            s, blk = divmod(i, 16)
            row0 = blk * 128
            if i >= 1:
                b_stage2(i - 1)
            if i >= 2:
                b_evac(i - 2)
            tct = P.dma("sync", "ct%d" % sl, lambda e, s=s, row0=row0, sl=sl, blk=blk: e.dma_start(
                out=cT_t[sl], in_=scr_cT[s, :, :, row0:row0 + 128]
                .rearrange("c p t -> p c t")),
                waits=[t_stctx[(s, blk // 4, NH - 1)], t_stctx[(s, blk // 4, NH - 2)],
                       war_ct2[sl]])
            tx = P.dma("sync", "x2%d" % sl, lambda e, s=s, row0=row0, sl=sl: e.dma_start(
                out=x2a[sl], in_=X[s, row0:row0 + 128, :]),
                waits=[war_x2[sl]])
            ps = psA if sl == 0 else psB

            def mmf(e, sl=sl, ps=ps):
                ins = None
                for d in range(16):
                    for es in range(4):
                        ins = e.matmul(ps[:, es * 512:(es + 1) * 512],
                                       cT_t[sl][:, d, :],
                                       wo_sb[:, d, es * 512:(es + 1) * 512],
                                       start=(d == 0), stop=(d == 15))
                return ins
            tmm = P.op("tensor", mmf, waits=[tct, t_wo, war_pso[sl]], inc="pe")
            txp = P.op("vector", lambda e, sl=sl, ps=ps: e.scalar_tensor_tensor(
                out=xpa[sl], in0=ps, scalar=1.0, in1=x2a[sl],
                op0=ALU.mult, op1=ALU.add),
                waits=[tmm, tx, war_xp[sl], t_stxp[i - 2] if i >= 2 else None],
                inc="ve")
            b_txp[i] = txp
            war_ct2[sl] = tmm
            war_x2[sl] = txp
            tstx = P.dma("sync", "xps%d" % sl, lambda e, i=i, sl=sl: e.dma_start(
                out=scr_xp[i * 128:(i + 1) * 128, :], in_=xpa[sl]),
                waits=[txp])
            t_stxp[i] = tstx
            b_sqrt[i] = ln_stage1("ln2", sl, xpa[sl], junk3, EPS1, src_tok=txp)
            if i >= 1:
                b_T(i - 1)
        b_stage2(NT - 1)
        b_T(NT - 1)
        b_evac(NT - 2)
        b_evac(NT - 1)

        P.barrier()

        warm_act(AF.Silu)
        h2T_sb = A.at(4 * KB, [128, 16, 2048], BF16)
        wg_i32 = [A.at(68 * KB, [128, D], I32), A.at(76 * KB, [128, D], I32)]
        wu_i32 = [A.at(84 * KB, [128, D], I32), A.at(92 * KB, [128, D], I32)]
        wg_bf = [A.at(100 * KB, [128, D], BF16), A.at(104 * KB, [128, D], BF16)]
        wu_bf = [A.at(108 * KB, [128, D], BF16), A.at(112 * KB, [128, D], BF16)]
        wgT_sb = [A.at(116 * KB, [128, 16, 128], BF16), A.at(120 * KB, [128, 16, 128], BF16)]
        wuT_sb = [A.at(124 * KB, [128, 16, 128], BF16), A.at(128 * KB, [128, 16, 128], BF16)]
        silu_sb = A.at(132 * KB, [128, 2048], F32)
        tT_sb = [A.at(140 * KB, [128, 2048], BF16), A.at(144 * KB, [128, 2048], BF16)]

        war_wgT = [None, None]
        war_wuT = [None, None]
        war_gi = [None, None]
        war_ui = [None, None]
        war_gbf = [None, None]
        war_ubf = [None, None]
        war_tT = [None, None]
        war_h2T = None
        war_psff = [None, None]
        t_sttT = []
        dq_toks = {}

        def ffa_dq(c):
            cs = c % 2
            tgl = P.dma("sync", "wg%d" % cs, lambda e, c=c, cs=cs: e.dma_start(
                out=wg_i32[cs], in_=WQGATE[c * 128:(c + 1) * 128, :]),
                waits=[war_gi[cs]])
            tul = P.dma("sync", "wu%d" % cs, lambda e, c=c, cs=cs: e.dma_start(
                out=wu_i32[cs], in_=WQUP[c * 128:(c + 1) * 128, :]),
                waits=[war_ui[cs]])
            tdqg = P.op("vector", lambda e, c=c, cs=cs: e.tensor_scalar(
                out=wg_bf[cs], in0=wg_i32[cs], scalar1=-128.0, scalar2=sg_sb[:, c:c + 1],
                op0=ALU.add, op1=ALU.mult), waits=[tgl, t_sg, war_gbf[cs]], inc="ve")
            war_gi[cs] = tdqg
            tdqu = P.op("vector", lambda e, c=c, cs=cs: e.tensor_scalar(
                out=wu_bf[cs], in0=wu_i32[cs], scalar1=-128.0, scalar2=su_sb[:, c:c + 1],
                op0=ALU.add, op1=ALU.mult), waits=[tul, t_su, war_ubf[cs]], inc="ve")
            war_ui[cs] = tdqu
            dq_toks[c] = (tdqg, tdqu)

        for hf in range(2):
            th2T = P.dma("sync", "h2t", lambda e, hf=hf: e.dma_start(
                out=h2T_sb, in_=scr_h2T[:, :, hf * 2048:(hf + 1) * 2048]
                .rearrange("c p t -> p c t")),
                waits=[t_sth2[NT - 1], t_sth2[NT - 2], war_h2T])
            ffa_dq(0)
            for c in range(NHC):
                cs = c % 2
                if c + 1 < NHC:
                    ffa_dq(c + 1)
                tdqg, tdqu = dq_toks[c]

                def tpg(e, cs=cs):
                    ins = None
                    for j in range(16):
                        ins = e.transpose(psAb[:, j * 128:(j + 1) * 128],
                                          wg_bf[cs][:, j * 128:(j + 1) * 128], ident_bf[:, :])
                    return ins
                tpg_t = P.op("tensor", tpg, waits=[tdqg, war_psff[0]], inc="pe")
                tevg = P.op("vector", lambda e, cs=cs: e.tensor_copy(
                    out=wgT_sb[cs][:, :, :],
                    in_=psAb[:, 0:2048].rearrange("p (c t) -> p c t", c=16)),
                    waits=[tpg_t, war_wgT[cs]], inc="ve")
                war_gbf[cs] = tpg_t

                def tpu(e, cs=cs):
                    ins = None
                    for j in range(16):
                        ins = e.transpose(psAb[:, 2048 + j * 128:2048 + (j + 1) * 128],
                                          wu_bf[cs][:, j * 128:(j + 1) * 128], ident_bf[:, :])
                    return ins
                tpu_t = P.op("tensor", tpu, waits=[tdqu], inc="pe")
                tevu = P.op("vector", lambda e, cs=cs: e.tensor_copy(
                    out=wuT_sb[cs][:, :, :],
                    in_=psAb[:, 2048:4096].rearrange("p (c t) -> p c t", c=16)),
                    waits=[tpu_t, war_wuT[cs]], inc="ve")
                war_ubf[cs] = tpu_t

                tq_last = None
                for qq in range(2):
                    ps = psA if qq % 2 == 0 else psB

                    def mmf(e, cs=cs, ps=ps, qq=qq):
                        ins = None
                        for d in range(16):
                            for t2 in range(2):
                                ins = e.matmul(
                                    ps[:, t2 * 512:(t2 + 1) * 512],
                                    wgT_sb[cs][:, d, :],
                                    h2T_sb[:, d, qq * 1024 + t2 * 512:qq * 1024 + (t2 + 1) * 512],
                                    start=(d == 0), stop=(d == 15))
                            for t2 in range(2):
                                ins = e.matmul(
                                    ps[:, 1024 + t2 * 512:1024 + (t2 + 1) * 512],
                                    wuT_sb[cs][:, d, :],
                                    h2T_sb[:, d, qq * 1024 + t2 * 512:qq * 1024 + (t2 + 1) * 512],
                                    start=(d == 0), stop=(d == 15))
                        return ins
                    wq = tevu if qq == 0 else war_psff[1]
                    tmm = P.op("tensor", mmf,
                               waits=[tevg, tevu, th2T, wq], inc="pe")

                    def siluf(e, ps=ps, qq=qq):
                        ins = None
                        for t2 in range(2):
                            ins = e.activation(
                                out=silu_sb[:, qq * 1024 + t2 * 512:qq * 1024 + (t2 + 1) * 512],
                                in_=ps[:, t2 * 512:(t2 + 1) * 512],
                                func=AF.Silu)
                        return ins
                    tsi = P.op("scalar", siluf, waits=[tmm], inc="se")

                    def multf(e, cs=cs, ps=ps, qq=qq):
                        ins = None
                        for t2 in range(2):
                            ins = e.scalar_tensor_tensor(
                                out=tT_sb[cs][:, qq * 1024 + t2 * 512:qq * 1024 + (t2 + 1) * 512],
                                in0=ps[:, 1024 + t2 * 512:1024 + (t2 + 1) * 512],
                                scalar=1.0,
                                in1=silu_sb[:, qq * 1024 + t2 * 512:qq * 1024 + (t2 + 1) * 512],
                                op0=ALU.mult, op1=ALU.mult)
                        return ins
                    tmu = P.op("vector", multf, waits=[tsi, war_tT[cs]], inc="ve")
                    war_psff[qq % 2] = tmu
                    tq_last = tmu
                tst = P.dma("sync", "tts%d" % cs, lambda e, c=c, hf=hf, cs=cs: e.dma_start(
                    out=scr_tT[c, :, hf * 2048:(hf + 1) * 2048], in_=tT_sb[cs]),
                    waits=[tq_last])
                war_tT[cs] = tst
                war_wgT[cs] = P.tok("pe")
                war_wuT[cs] = P.tok("pe")
                t_sttT.append(tst)
            war_h2T = P.tok("pe")

        P.barrier()

        tT_t = [A.at(4 * KB, [128, 11, 512], BF16), A.at(16 * KB, [128, 11, 512], BF16)]
        wdT_t = [A.at(28 * KB, [128, 11, D], BF16), A.at(72 * KB, [128, 11, D], BF16)]
        xp_t = [A.at(116 * KB, [128, D], F32), A.at(124 * KB, [128, D], F32)]
        acc = A.at(132 * KB, [128, 4, D], F32)

        war_tTt = [None, None]
        war_wdt = [None, None]
        war_xpt = [None, None]
        war_psd = [None, None]
        war_acc = [None] * 4
        acc_last = [None] * 4
        t_lastw_d = [None] * 4
        for tt in range(8):
            tok0 = tt * 512
            for grp in range(4):
                gs = grp % 2
                ttl = P.dma("sync", "tt%d" % gs, lambda e, grp=grp, gs=gs, tok0=tok0: e.dma_start(
                    out=tT_t[gs],
                    in_=scr_tT[grp * 11:(grp + 1) * 11, :, tok0:tok0 + 512]
                    .rearrange("c p t -> p c t")),
                    waits=[t_sttT[-1], t_sttT[-2], war_tTt[gs]])
                twd = P.dma("sync", "wdl%d" % gs, lambda e, grp=grp, gs=gs: e.dma_start(
                    out=wdT_t[gs],
                    in_=scr_wdT[grp * 11:(grp + 1) * 11, :, :]
                    .rearrange("c p t -> p c t")),
                    waits=[t_wdT_done, war_wdt[gs]])
                for sub in range(4):
                    sl = sub % 2
                    txl = None
                    if grp == 0:
                        txl = P.dma("sync", "xpl%d" % sl, lambda e, tok0=tok0, sub=sub, sl=sl: e.dma_start(
                            out=xp_t[sl],
                            in_=scr_xp[tok0 + sub * 128:tok0 + (sub + 1) * 128, :]),
                            waits=[t_stxp[NT - 1], t_stxp[NT - 2], war_xpt[sl]])
                    ps = psA if sl == 0 else psB

                    def mmf(e, sub=sub, gs=gs, ps=ps):
                        ins = None
                        for j in range(11):
                            for es in range(4):
                                ins = e.matmul(ps[:, es * 512:(es + 1) * 512],
                                               tT_t[gs][:, j, sub * 128:(sub + 1) * 128],
                                               wdT_t[gs][:, j, es * 512:(es + 1) * 512],
                                               start=(j == 0), stop=(j == 10))
                        return ins
                    tmm = P.op("tensor", mmf,
                               waits=[ttl, twd, war_psd[sl]], inc="pe")
                    if grp == 0:
                        tev = P.op("vector", lambda e, sub=sub, sl=sl, ps=ps: e.scalar_tensor_tensor(
                            out=acc[:, sub, :], in0=ps, scalar=1.0, in1=xp_t[sl],
                            op0=ALU.mult, op1=ALU.add),
                            waits=[tmm, txl, war_acc[sub], acc_last[sub]], inc="ve")
                        war_xpt[sl] = tev
                    else:
                        tev = P.op("vector", lambda e, sub=sub, ps=ps: e.tensor_tensor(
                            out=acc[:, sub, :], in0=acc[:, sub, :], in1=ps, op=ALU.add),
                            waits=[tmm, acc_last[sub]], inc="ve")
                    acc_last[sub] = tev
                    war_psd[sl] = tev
                    if grp == 3:
                        i = (tok0 // 128) + sub
                        s, blk = divmod(i, 16)
                        tw = P.dma("sync", "ys%d" % sub, lambda e, s=s, blk=blk, sub=sub: e.dma_start(
                            out=Y[s, blk * 128:(blk + 1) * 128, :], in_=acc[:, sub, :]),
                            waits=[tev])
                        war_acc[sub] = tw
                        t_lastw_d[sub] = tw
                war_tTt[gs] = P.tok("pe")
                war_wdt[gs] = P.tok("pe")

        P.op("sync", lambda e: e.nop(), waits=t_lastw_d)

        P.realize(ctx)
    return nc


_NC_CACHE = {}


def kernel(L0_attn_x, L0_attn_Wqkv, L0_attn_q_ln_scale, L0_attn_k_ln_scale,
           L0_attn_out_W, L0_ffn_wq_gate, L0_ffn_s_gate, L0_ffn_wq_up,
           L0_ffn_s_up, L0_ffn_wq_down, L0_ffn_s_down):
    x = np.ascontiguousarray(np.asarray(L0_attn_x, dtype=np.float32))
    wqkv = np.ascontiguousarray(np.asarray(L0_attn_Wqkv, dtype=np.float32))
    qg = np.ascontiguousarray(np.asarray(L0_attn_q_ln_scale, dtype=np.float32))
    kg = np.ascontiguousarray(np.asarray(L0_attn_k_ln_scale, dtype=np.float32))
    wo = np.ascontiguousarray(np.asarray(L0_attn_out_W, dtype=np.float32))
    wqg = np.ascontiguousarray(np.asarray(L0_ffn_wq_gate, dtype=np.int32))
    sg = np.ascontiguousarray(np.asarray(L0_ffn_s_gate, dtype=np.float32))
    wqu = np.ascontiguousarray(np.asarray(L0_ffn_wq_up, dtype=np.int32))
    su = np.ascontiguousarray(np.asarray(L0_ffn_s_up, dtype=np.float32))
    wqd = np.ascontiguousarray(np.asarray(L0_ffn_wq_down, dtype=np.int32))
    sd = np.ascontiguousarray(np.asarray(L0_ffn_s_down, dtype=np.float32))

    if "nc" not in _NC_CACHE:
        _NC_CACHE["nc"] = build_nc()
    nc = _NC_CACHE["nc"]

    in_maps = []
    for c in range(NCORES):
        in_maps.append({
            "k_x": x[c * BL:(c + 1) * BL],
            "k_wqkv": wqkv, "k_qg": qg, "k_kg": kg, "k_wo": wo,
            "k_wqg": wqg, "k_sg": sg, "k_wqu": wqu, "k_su": su,
            "k_wqd": wqd, "k_sd": sd,
        })
    res = run_bass_kernel_spmd(nc, in_maps, core_ids=list(range(NCORES)))
    out = np.concatenate([np.asarray(r["k_y"], dtype=np.float32)
                          for r in res.results], axis=0)
    return out


# version 09
# speedup vs baseline: 1.0273x; 1.0273x over previous
import numpy as np
from collections import defaultdict
from contextlib import ExitStack

import concourse.bass as bass
import concourse.mybir as mybir
from concourse.bass_utils import run_bass_kernel_spmd

F32 = mybir.dt.float32
BF16 = mybir.dt.bfloat16
I32 = mybir.dt.int32
I8 = mybir.dt.int8
AF = mybir.ActivationFunctionType
ALU = mybir.AluOpType
AX = mybir.AxisListType

NCORES = 8
B, S, D = 16, 2048, 2048
BL = B // NCORES
T = BL * S
NT = T // 128
HID = 5632
NHC = HID // 128
NH = 16
E3 = 3 * D
EPS1 = 1e-5
EPS2 = 1e-6
ISQ = float(1.0 / np.sqrt(128.0))
KB = 1024
ARENA_B = 204 * KB

ENGINES = ("sync", "gpsimd", "tensor", "vector", "scalar")
DT_SIZE = {F32: 4, BF16: 2, I32: 4, I8: 1}


def _prod(xs):
    p = 1
    for x in xs:
        p *= x
    return p


class Plan:
    def __init__(self, nc):
        self.nc = nc
        self.q = {e: [] for e in ENGINES}
        self.cnt = defaultdict(int)
        self.hwm = {e: defaultdict(int) for e in ENGINES}
        self.sems = {}
        self.pending = {e: [] for e in ENGINES}
        self.dma_sems = {}

    def tok(self, s):
        return (s, self.cnt[s])

    def dma(self, eng, key, fn, waits=()):
        name = "d_" + key
        if name not in self.dma_sems:
            self.dma_sems[name] = eng
        return self.op(eng, fn, waits, inc=name, amt=16)

    def op(self, eng, fn, waits=(), inc=None, amt=1):
        wl = []
        allw = list(self.pending[eng]) + [w for w in waits if w is not None]
        self.pending[eng] = []
        for (s, v) in allw:
            if v <= 0:
                continue
            if v > self.hwm[eng][s]:
                self.hwm[eng][s] = v
                wl.append((s, v))
        t = None
        if inc is not None:
            self.cnt[inc] += amt
            t = (inc, self.cnt[inc])
        self.q[eng].append((tuple(wl), fn, inc, amt))
        return t

    def barrier(self):
        nop = lambda e: e.nop()
        sync_w = [(n, self.cnt[n]) for n, eng in self.dma_sems.items() if eng == "sync"]
        gps_w = [(n, self.cnt[n]) for n, eng in self.dma_sems.items() if eng == "gpsimd"]
        self.op("sync", nop, waits=sync_w, inc="b")
        self.op("gpsimd", nop, waits=gps_w, inc="b")
        self.op("tensor", nop, inc="b")
        self.op("vector", nop, inc="b")
        self.op("scalar", nop, inc="b")
        tgt = self.cnt["b"]
        for e in ENGINES:
            self.pending[e].append(("b", tgt))

    def realize(self, ctx):
        for name in ("pe", "ve", "se", "ge", "b"):
            self.sems[name] = ctx.enter_context(self.nc.semaphore("ksem_" + name))
        for name in self.dma_sems:
            self.sems[name] = ctx.enter_context(self.nc.semaphore("ksem" + name))
        block = ctx.enter_context(self.nc.Block())
        sems = self.sems

        def runner(qlist):
            def run(e):
                for wl, fn, inc, amt in qlist:
                    for s, v in wl:
                        e.wait_ge(sems[s], v)
                    ins = fn(e)
                    if inc is not None:
                        ins.then_inc(sems[inc], amt)
            return run

        block.sync(runner(self.q["sync"]))
        block.gpsimd(runner(self.q["gpsimd"]))
        block.tensor(runner(self.q["tensor"]))
        block.vector(runner(self.q["vector"]))
        block.scalar(runner(self.q["scalar"]))


class Arena:
    def __init__(self, ap):
        self.ap = ap

    def at(self, off_b, shape, dtype):
        n = _prod(shape[1:]) * DT_SIZE[dtype]
        assert off_b % 64 == 0, off_b
        assert off_b + n <= ARENA_B, (off_b, n)
        v = self.ap[:, off_b:off_b + n].bitcast(dtype)
        if len(shape) == 2:
            return v
        if len(shape) == 3:
            return v.rearrange("p (a b) -> p a b", a=shape[1])
        if len(shape) == 4:
            return v.rearrange("p (a b c) -> p a b c", a=shape[1], b=shape[2])
        raise ValueError(shape)


DBG = False


def build_nc():
    nc = bass.Bass("TRN2", target_bir_lowering=False)

    X = nc.dram_tensor("k_x", [BL, S, D], F32, kind="ExternalInput").ap()
    WQKV = nc.dram_tensor("k_wqkv", [D, E3], F32, kind="ExternalInput").ap()
    QG = nc.dram_tensor("k_qg", [D], F32, kind="ExternalInput").ap()
    KG = nc.dram_tensor("k_kg", [D], F32, kind="ExternalInput").ap()
    WO = nc.dram_tensor("k_wo", [D, D], F32, kind="ExternalInput").ap()
    WQGATE = nc.dram_tensor("k_wqg", [HID, D], I32, kind="ExternalInput").ap()
    SG = nc.dram_tensor("k_sg", [HID], F32, kind="ExternalInput").ap()
    WQUP = nc.dram_tensor("k_wqu", [HID, D], I32, kind="ExternalInput").ap()
    SU = nc.dram_tensor("k_su", [HID], F32, kind="ExternalInput").ap()
    WQDOWN = nc.dram_tensor("k_wqd", [D, HID], I32, kind="ExternalInput").ap()
    SD = nc.dram_tensor("k_sd", [D], F32, kind="ExternalInput").ap()
    Y = nc.dram_tensor("k_y", [BL, S, D], F32, kind="ExternalOutput").ap()

    scr_hT = nc.dram_tensor("k_scr_ht", [16, 128, T], BF16, kind="Internal").ap()
    scr_qT = nc.dram_tensor("k_scr_qt", [BL, 16, 128, S], BF16, kind="Internal").ap()
    scr_kT = nc.dram_tensor("k_scr_kt", [BL, 16, 128, S], BF16, kind="Internal").ap()
    scr_v = nc.dram_tensor("k_scr_v", [BL, 16, 128, D], BF16, kind="Internal").ap()
    scr_cT = nc.dram_tensor("k_scr_ct", [BL, 16, 128, S], BF16, kind="Internal").ap()
    scr_xp = nc.dram_tensor("k_scr_xp", [T, D], F32, kind="Internal").ap()
    scr_h2T = nc.dram_tensor("k_scr_h2t", [16, 128, T], BF16, kind="Internal").ap()
    scr_tT = nc.dram_tensor("k_scr_tt", [NHC, 128, T], BF16, kind="Internal").ap()
    scr_wdT = nc.dram_tensor("k_scr_wdt", [NHC, 128, D], BF16, kind="Internal").ap()
    scr_wgT = nc.dram_tensor("k_scr_wgt", [NHC, 128, D], BF16, kind="Internal").ap()
    scr_wuT = nc.dram_tensor("k_scr_wut", [NHC, 128, D], BF16, kind="Internal").ap()

    with ExitStack() as ctx:
        ident_bf = ctx.enter_context(nc.sbuf_tensor("k_identb", [128, 128], BF16))
        ident_f = ctx.enter_context(nc.sbuf_tensor("k_identf", [128, 128], F32))
        cmask = ctx.enter_context(nc.sbuf_tensor("k_cmask", [128, 128], F32))
        vecs = ctx.enter_context(nc.sbuf_tensor("k_vecs", [128, 240], F32))
        smalls = ctx.enter_context(nc.sbuf_tensor("k_smalls", [128, 64], F32))
        vraw = ctx.enter_context(nc.sbuf_tensor("k_vraw", [44, 128], F32))
        arena_t = ctx.enter_context(nc.sbuf_tensor("k_arena", [128, ARENA_B], I8))
        psA_t = ctx.enter_context(nc.psum_tensor("k_psA", [128, 2048], F32))
        psB_t = ctx.enter_context(nc.psum_tensor("k_psB", [128, 2048], F32))
        A = Arena(arena_t[:, :])
        psA = psA_t[:, :]
        psB = psB_t[:, :]
        psAb = psA.bitcast(BF16)
        psBb = psB.bitcast(BF16)

        qg_sb = vecs[:, 0:16]
        kg_sb = vecs[:, 16:32]
        sd_sb = vecs[:, 32:48]
        sg_sb = vecs[:, 64:108]
        su_sb = vecs[:, 152:196]

        s12 = [smalls[:, 0:2], smalls[:, 2:4]]
        meanex = [smalls[:, 4:6], smalls[:, 6:8]]
        m2 = [smalls[:, 8:9], smalls[:, 9:10]]
        varv = [smalls[:, 10:11], smalls[:, 11:12]]
        sqv = [smalls[:, 12:13], smalls[:, 13:14]]
        rstd = [smalls[:, 14:15], smalls[:, 15:16]]
        pmax = [smalls[:, 16 + i:17 + i] for i in range(4)]
        negmax = [smalls[:, 20 + i:21 + i] for i in range(4)]
        zacc = [smalls[:, 24 + i:25 + i] for i in range(8)]
        rz = [smalls[:, 32 + i:33 + i] for i in range(4)]
        eps_t = {EPS1: smalls[:, 36:37], EPS2: smalls[:, 37:38]}
        warm_in = smalls[:, 40:44]
        warm_out = smalls[:, 44:48]

        P = Plan(nc)

        ln_state = {}

        def ln_stage1(tag, sl, src_ap, junk_ap, eps, src_tok=None):
            st = ln_state.setdefault(tag, {
                "th": [None, None], "t5": [None, None], "t6": [None, None],
                "t7": [None, None], "junk": None})
            t1 = P.op("vector", lambda e: e.reduce_sum(
                s12[sl][:, 0:1], src_ap, axis=AX.X),
                waits=[src_tok, st["th"][sl]], inc="ve")
            t2 = P.op("vector", lambda e: e.scalar_tensor_tensor(
                out=junk_ap, in0=src_ap, scalar=1.0, in1=src_ap,
                op0=ALU.mult, op1=ALU.mult, accum_out=s12[sl][:, 1:2]),
                waits=[st["junk"]], inc="ve")
            st["junk"] = t2
            t3 = P.op("vector", lambda e: e.tensor_scalar(
                out=meanex[sl], in0=s12[sl], scalar1=1.0 / D, scalar2=None,
                op0=ALU.mult), waits=[t1, t2], inc="ve")
            t4 = P.op("vector", lambda e: e.tensor_mul(
                m2[sl], meanex[sl][:, 0:1], meanex[sl][:, 0:1]),
                waits=[t3, st["t5"][sl]], inc="ve")
            t5 = P.op("vector", lambda e: e.scalar_tensor_tensor(
                out=varv[sl], in0=m2[sl], scalar=-1.0, in1=meanex[sl][:, 1:2],
                op0=ALU.mult, op1=ALU.add), waits=[t4, st["t6"][sl]], inc="ve")
            st["t5"][sl] = t5
            t6 = P.op("scalar", lambda e, eps=eps: e.activation(
                out=sqv[sl], in_=varv[sl], func=AF.Sqrt, bias=eps_t[eps], scale=1.0),
                waits=[t5, st["th"][sl], st["t7"][sl]], inc="se")
            st["t6"][sl] = t6
            return t6

        def ln_stage2(tag, sl, src_ap, dst_ap, t_sqrt, extra=()):
            st = ln_state[tag]
            t7 = P.op("vector", lambda e: e.reciprocal(rstd[sl], sqv[sl]),
                      waits=[t_sqrt], inc="ve")
            st["t7"][sl] = t7
            th = P.op("vector", lambda e: e.tensor_scalar(
                out=dst_ap, in0=src_ap, scalar1=meanex[sl][:, 0:1],
                scalar2=rstd[sl], op0=ALU.subtract, op1=ALU.mult),
                waits=[t7] + [w for w in extra if w is not None], inc="ve")
            st["th"][sl] = th
            return th

        def emit_memsets(e):
            e.memset(smalls[:, 0:36], 1.0)
            e.memset(smalls[:, 38:64], 1.0)
            e.memset(eps_t[EPS1], EPS1)
            e.memset(eps_t[EPS2], EPS2)
            e.memset(ident_bf[:, :], 0.0)
            e.memset(ident_f[:, :], 0.0)
            return e.memset(cmask[:, :], 0.0)
        t_ms = P.op("gpsimd", emit_memsets, inc="ge")

        def emit_selects(e):
            e.affine_select(out=ident_bf[:, :], in_=ident_bf[:, :],
                            compare_op=ALU.not_equal, fill=1.0, base=0,
                            pattern=[[-1, 128]], channel_multiplier=1)
            e.affine_select(out=ident_f[:, :], in_=ident_f[:, :],
                            compare_op=ALU.not_equal, fill=1.0, base=0,
                            pattern=[[-1, 128]], channel_multiplier=1)
            return e.affine_select(out=cmask[:, :], in_=cmask[:, :],
                                   compare_op=ALU.is_ge, fill=-1e30, base=0,
                                   pattern=[[-1, 128]], channel_multiplier=1)
        t_ident = P.op("gpsimd", emit_selects, waits=[t_ms], inc="ge")
        P.pending["vector"].append(t_ident)
        P.pending["scalar"].append(t_ident)

        warm_prev = [None]

        def warm_act(func):
            def warm(e, func=func):
                ins = None
                for j in range(10):
                    ins = e.activation(out=smalls[:, 44 + j:45 + j],
                                       in_=smalls[:, 40:41],
                                       func=func, bias=0.0, scale=0.0)
                return ins
            t = P.op("scalar", warm, waits=[warm_prev[0]], inc="se")
            warm_prev[0] = t
            return t

        warm_act(AF.Sqrt)

        vec_tp_prev = [None]

        def load_vec(src_ap, nch, dst, scale=None):
            tl = P.dma("sync", "vr", lambda e, ap=src_ap, n=nch: e.dma_start(
                out=vraw[0:n, :], in_=ap.rearrange("(c p) -> c p", p=128)),
                waits=[vec_tp_prev[0]])
            tp = P.op("tensor", lambda e, n=nch: e.transpose(
                psB[:, 1536:1536 + n], vraw[0:n, :], ident_f[0:n, 0:n]),
                waits=[tl, t_ident, P.tok("ve")], inc="pe")
            vec_tp_prev[0] = tp
            if scale is None:
                tv = P.op("vector", lambda e, d=dst, n=nch: e.tensor_copy(
                    out=d, in_=psB[:, 1536:1536 + n]), waits=[tp], inc="ve")
            else:
                tv = P.op("vector", lambda e, d=dst, n=nch, sc=scale: e.tensor_scalar(
                    out=d, in0=psB[:, 1536:1536 + n], scalar1=sc, scalar2=None,
                    op0=ALU.mult), waits=[tp], inc="ve")
            return tv

        t_qg = load_vec(QG, 16, qg_sb, scale=ISQ)
        t_kg = load_vec(KG, 16, kg_sb)
        t_sd = load_vec(SD, 16, sd_sb)
        t_sg = load_vec(SG, 44, sg_sb)
        t_su = load_vec(SU, 44, su_sb)

        w_i32 = A.at(132 * KB, [128, HID], I32)
        w_bf = A.at(156 * KB, [128, HID], BF16)
        wT_stage = A.at(168 * KB, [128, NHC, 128], BF16)
        pp_pt = [psB[:, 0:512], psB[:, 512:1024]]
        war_ppi = None
        war_ppbf = None
        war_ppstage = None
        pp_gctr = [0]
        pp_pt_war = [None, None]
        t_ppst = {"wd": None, "wg": None, "wu": None}

        def preprocess_w(tag, W_ap, ncols, scale_sb, t_scale, ntiles, store_fn):
            nonlocal war_ppi, war_ppbf, war_ppstage
            nblk = ncols // 128
            for tl_i in range(ntiles):
                tl = P.dma("sync", "ppw", lambda e, tl_i=tl_i, ncols=ncols: e.dma_start(
                    out=w_i32[:, 0:ncols], in_=W_ap[tl_i * 128:(tl_i + 1) * 128, :]),
                    waits=[war_ppi])
                tdq = P.op("gpsimd", lambda e, tl_i=tl_i, ncols=ncols: e.tensor_scalar(
                    out=w_bf[:, 0:ncols], in0=w_i32[:, 0:ncols], scalar1=-128.0,
                    scalar2=scale_sb[:, tl_i:tl_i + 1], op0=ALU.add, op1=ALU.mult),
                    waits=[tl, t_scale, war_ppbf], inc="ge")
                war_ppi = tdq
                tev_toks = []
                tp_last = None
                for g in range(0, nblk, 4):
                    cnt = min(4, nblk - g)
                    gsl = pp_gctr[0] % 2
                    pp_gctr[0] += 1

                    def tpf(e, g=g, cnt=cnt, gsl=gsl):
                        ins = None
                        for j in range(cnt):
                            ins = e.matmul(pp_pt[gsl][:, j * 128:(j + 1) * 128],
                                           w_bf[:, (g + j) * 128:(g + j + 1) * 128],
                                           ident_bf[:, :], start=True, stop=True)
                        return ins
                    tp = P.op("tensor", tpf,
                              waits=[tdq, t_ident, pp_pt_war[gsl]], inc="pe")
                    tp_last = tp
                    eng = "vector" if pp_gctr[0] % 2 == 0 else "scalar"
                    if eng == "vector":
                        tev = P.op("vector", lambda e, g=g, cnt=cnt, gsl=gsl: e.tensor_copy(
                            out=wT_stage[:, g:g + cnt, :],
                            in_=pp_pt[gsl][:, 0:cnt * 128]
                            .rearrange("p (c t) -> p c t", c=cnt)),
                            waits=[tp, war_ppstage], inc="ve")
                    else:
                        tev = P.op("scalar", lambda e, g=g, cnt=cnt, gsl=gsl: e.activation(
                            out=wT_stage[:, g:g + cnt, :],
                            in_=pp_pt[gsl][:, 0:cnt * 128]
                            .rearrange("p (c t) -> p c t", c=cnt),
                            func=AF.Identity),
                            waits=[tp, war_ppstage], inc="se")
                    pp_pt_war[gsl] = tev
                    tev_toks.append(tev)
                war_ppbf = tp_last
                tst = P.dma("sync", "pps", store_fn(tl_i), waits=tev_toks)
                war_ppstage = tst
                t_ppst[tag] = tst

        def wd_store(tl_i):
            return lambda e, tl_i=tl_i: e.dma_start(
                out=scr_wdT[:, :, tl_i * 128:(tl_i + 1) * 128].rearrange("c p t -> p c t"),
                in_=wT_stage[:, 0:NHC, :])

        def wg_store(tl_i):
            return lambda e, tl_i=tl_i: e.dma_start(
                out=scr_wgT[tl_i], in_=wT_stage[:, 0:16, :].rearrange("p c t -> p (c t)"))

        def wu_store(tl_i):
            return lambda e, tl_i=tl_i: e.dma_start(
                out=scr_wuT[tl_i], in_=wT_stage[:, 0:16, :].rearrange("p c t -> p (c t)"))

        preprocess_w("wd", WQDOWN, HID, sd_sb, t_sd, 16, wd_store)
        preprocess_w("wg", WQGATE, D, sg_sb, t_sg, NHC, wg_store)
        preprocess_w("wu", WQUP, D, su_sb, t_su, NHC, wu_store)
        t_wdT_done = t_ppst["wd"]
        t_wgT_done = t_ppst["wg"]
        t_wuT_done = t_ppst["wu"]

        xa = [A.at(180 * KB, [128, D], F32), A.at(188 * KB, [128, D], F32)]
        h_bf = [A.at(196 * KB, [128, D], BF16), A.at(200 * KB, [128, D], BF16)]
        hT_stage = [A.at(68 * KB, [128, 16, 128], BF16), A.at(76 * KB, [128, 16, 128], BF16)]
        junk_bf = A.at(72 * KB, [128, D], BF16)
        war_x = [None, None]
        war_hstage = [None, None]
        pre_sqrt = {}
        pre_th = {}
        tp_pre = {}
        tev_pre = {}
        t_sthT = [None] * NT

        def pre_stage2(i):
            sl = i % 2
            th = ln_stage2("pre", sl, xa[sl], h_bf[sl], pre_sqrt[i],
                           extra=[tp_pre.get(i - 2)])
            pre_th[i] = th
            war_x[sl] = th

        def pre_T(i):
            sl = i % 2

            def tpf(e, sl=sl):
                ins = None
                for c in range(16):
                    ins = e.matmul(psA[:, c * 128:(c + 1) * 128],
                                   h_bf[sl][:, c * 128:(c + 1) * 128],
                                   ident_bf[:, :], start=True, stop=True)
                return ins
            tp = P.op("tensor", tpf, waits=[pre_th[i], t_ident, tev_pre.get(i - 1)],
                      inc="pe")
            tp_pre[i] = tp

        def pre_evac(i):
            sl = i % 2
            tev = P.op("vector", lambda e, sl=sl: e.tensor_copy(
                out=hT_stage[sl][:, :, :],
                in_=psA[:, 0:2048].rearrange("p (c t) -> p c t", c=16)),
                waits=[tp_pre[i], war_hstage[sl]], inc="ve")
            tev_pre[i] = tev
            tst = P.dma("sync", "hts%d" % sl, lambda e, i=i, sl=sl: e.dma_start(
                out=scr_hT[:, :, i * 128:(i + 1) * 128].rearrange("c p t -> p c t"),
                in_=hT_stage[sl][:, :, :]),
                waits=[tev])
            war_hstage[sl] = tst
            t_sthT[i] = tst

        for i in range(NT):
            sl = i % 2
            s, blk = divmod(i, 16)
            row0 = blk * 128
            tx = P.dma("sync", "xa%d" % sl, lambda e, s=s, row0=row0, sl=sl: e.dma_start(
                out=xa[sl], in_=X[s, row0:row0 + 128, :]),
                waits=[war_x[sl]])
            if i >= 2:
                pre_evac(i - 2)
            if i >= 1:
                pre_stage2(i - 1)
                pre_T(i - 1)
            P.pending["vector"].append(tx)
            pre_sqrt[i] = ln_stage1("pre", sl, xa[sl], junk_bf, EPS1)
        pre_evac(NT - 2)
        pre_stage2(NT - 1)
        pre_T(NT - 1)
        pre_evac(NT - 1)

        P.barrier()

        wr_sb = [A.at(4 * KB, [128, 16, 2048], BF16), A.at(68 * KB, [128, 16, 2048], BF16)]
        hT_t = [A.at(132 * KB, [128, 16, 256], BF16), A.at(140 * KB, [128, 16, 256], BF16)]
        q32s = [A.at(148 * KB, [128, D], F32), A.at(156 * KB, [128, D], F32)]
        qkvT_stage = [A.at(164 * KB, [128, 16, 128], BF16), A.at(168 * KB, [128, 16, 128], BF16)]
        junk2 = A.at(172 * KB, [128, D], BF16)
        q_bf = [A.at(176 * KB, [128, D], BF16), A.at(180 * KB, [128, D], BF16)]

        t_w = [None, None, None]
        t_w[0] = P.dma("gpsimd", "wr0", lambda e: e.dma_start(
            out=wr_sb[0], in_=WQKV[:, 0:2048].rearrange("(c p) e -> p c e", p=128)))
        t_w[1] = P.dma("gpsimd", "wr1", lambda e: e.dma_start(
            out=wr_sb[1], in_=WQKV[:, 2048:4096].rearrange("(c p) e -> p c e", p=128)))

        last_st_q = [None] * BL
        last_st_k = [None] * BL
        last_st_v = [None] * BL
        war_hT = [None, None]
        war_ps = [None, None]
        war_qstage = [None, None]
        war_vbf = [None, None]
        qk_tp = {}
        qk_th = {}
        qk_sqrt = {}
        t_ht = {}

        def qk_stage2(r, i):
            sl = i % 2
            th = ln_stage2("qk", sl, q32s[sl], q_bf[sl],
                           qk_sqrt[(r, i)], extra=[qk_tp.get((r, i - 2)),
                                                   qk_tp.get((r - 1, NT - 2 + sl)) if i < 2 else None])
            qk_th[(r, i)] = th

        def qk_T(r, i):
            sl = i % 2
            ps = psA if sl == 0 else psB

            def tpf(e, sl=sl, ps=ps):
                ins = None
                for c in range(16):
                    ins = e.matmul(ps[:, c * 128:(c + 1) * 128],
                                   q_bf[sl][:, c * 128:(c + 1) * 128],
                                   ident_bf[:, :], start=True, stop=True)
                return ins
            tp = P.op("tensor", tpf, waits=[qk_th[(r, i)]], inc="pe")
            qk_tp[(r, i)] = tp

        def qk_gevac(r, i):
            sl = i % 2
            ps = psA if sl == 0 else psB
            s, blk = divmod(i, 16)
            row0 = blk * 128
            gam = qg_sb if r == 0 else kg_sb
            gam_b = gam.unsqueeze(2).broadcast_to([128, 16, 128])

            def gevf(e, sl=sl, ps=ps, gam_b=gam_b):
                return e.tensor_tensor(
                    out=qkvT_stage[sl][:, :, :],
                    in0=ps[:, 0:2048].rearrange("p (c t) -> p c t", c=16),
                    in1=gam_b, op=ALU.mult)
            tgev = P.op("vector", gevf,
                        waits=[qk_tp[(r, i)], war_qstage[sl],
                               (t_qg if r == 0 else t_kg)], inc="ve")
            war_ps[sl] = tgev
            scr = scr_qT if r == 0 else scr_kT
            tst = P.dma("sync", "qks%d" % sl, lambda e, s=s, row0=row0, sl=sl, scr=scr: e.dma_start(
                out=scr[s, :, :, row0:row0 + 128].rearrange("c p t -> p c t"),
                in_=qkvT_stage[sl][:, :, :]),
                waits=[tgev])
            war_qstage[sl] = tst
            if r == 0:
                last_st_q[s] = tst
            else:
                last_st_k[s] = tst

        for r in range(3):
            for i in range(NT):
                sl = i % 2
                s, blk = divmod(i, 16)
                jp = i // 2
                hsl = jp % 2
                if i % 2 == 0:
                    w_pre = [t_sthT[i], t_sthT[i + 1]] if r == 0 else []
                    t_ht[(r, jp)] = P.dma("sync", "ht%d" % hsl, lambda e, i=i, hsl=hsl: e.dma_start(
                        out=hT_t[hsl], in_=scr_hT[:, :, i * 128:(i + 2) * 128]
                        .rearrange("c p t -> p c t")),
                        waits=w_pre + [war_hT[hsl]])
                ps = psA if sl == 0 else psB
                toff = (i % 2) * 128

                if r <= 1:
                    if i >= 2:
                        qk_gevac(r, i - 2)
                    if i >= 1:
                        qk_stage2(r, i - 1)

                def mmf(e, hsl=hsl, toff=toff, r=r, ps=ps):
                    ins = None
                    for d in range(16):
                        for es in range(4):
                            ins = e.matmul(ps[:, es * 512:(es + 1) * 512],
                                           hT_t[hsl][:, d, toff:toff + 128],
                                           wr_sb[r % 2][:, d, es * 512:(es + 1) * 512],
                                           start=(d == 0), stop=(d == 15))
                    return ins
                tmm = P.op("tensor", mmf,
                           waits=[t_ht[(r, jp)], t_w[r], war_ps[sl]], inc="pe")
                if i % 2 == 1:
                    war_hT[hsl] = tmm
                if r <= 1:
                    tcp = P.op("vector", lambda e, ps=ps, sl=sl: e.tensor_copy(
                        out=q32s[sl], in_=ps),
                        waits=[tmm, ln_state.get("qk", {}).get("th", [None, None])[sl]],
                        inc="ve")
                    qk_sqrt[(r, i)] = ln_stage1("qk", sl, q32s[sl], junk2, EPS2,
                                                src_tok=tcp)
                    if i >= 1:
                        qk_T(r, i - 1)
                else:
                    tcp = P.op("vector", lambda e, ps=ps, sl=sl: e.tensor_copy(
                        out=q_bf[sl], in_=ps),
                        waits=[tmm, war_vbf[sl], qk_tp.get((1, NT - 2 + sl))],
                        inc="ve")
                    war_ps[sl] = tcp
                    tstv = P.dma("sync", "vs%d" % sl, lambda e, s=s, blk=blk, sl=sl: e.dma_start(
                        out=scr_v[s, blk, :, :], in_=q_bf[sl]),
                        waits=[tcp])
                    war_vbf[sl] = tstv
                    last_st_v[s] = tstv
            if r <= 1:
                qk_stage2(r, NT - 1)
                qk_T(r, NT - 1)
                qk_gevac(r, NT - 2)
                qk_gevac(r, NT - 1)
            if r == 0:
                t_w[2] = P.dma("gpsimd", "wr0", lambda e: e.dma_start(
                    out=wr_sb[0], in_=WQKV[:, 4096:6144]
                    .rearrange("(c p) e -> p c e", p=128)),
                    waits=[P.tok("pe")])

        P.barrier()

        warm_act(AF.Exp)
        kT_sb = A.at(4 * KB, [128, 16, S], BF16)
        vT_sb = A.at(68 * KB, [128, 16, D], BF16)
        qT_sb = A.at(132 * KB, [128, 16, 512], BF16)
        pT_sb = A.at(148 * KB, [128, 16, 512], BF16)
        p_bf = [A.at(164 * KB, [128, S], BF16), A.at(168 * KB, [128, S], BF16)]
        psc_bf = [A.at(172 * KB, [128, S], BF16), A.at(176 * KB, [128, S], BF16),
                  A.at(196 * KB, [128, S], BF16), A.at(200 * KB, [128, S], BF16)]
        cT_h = [A.at(180 * KB, [128, 512], BF16), A.at(181 * KB, [128, 512], BF16)]
        sc_ps = [psA[:, 0:1024], psA[:, 1024:2048]]
        pt_ps = [psB[:, 0:512], psB[:, 512:1024]]
        ctx_ps = [psB[:, 1024:1536], psB[:, 1536:2048]]

        t_stctx = {}
        piece_ctr = [0]
        gctr = [0]
        war_sc = [None, None]
        war_pt = [None, None]
        war_ctx = [None, None]
        war_pbf = [None, None]
        war_psc = [None, None, None, None]
        war_negm = [None] * 4
        war_cth = [None, None]
        war_qT = None
        war_pT = None

        def emit_softmax(qt, h, sub, tqT, tkT):
            gsub = qt * 4 + sub
            k_len = (gsub + 1) * 128
            pieces = []
            k0 = 0
            while k0 < k_len:
                kw = min(1024, k_len - k0)
                pieces.append((k0, kw))
                k0 += kw
            ptoks = []
            for pidx, (k0, kw) in enumerate(pieces):
                psl = piece_ctr[0] % 2
                piece_ctr[0] += 1

                def scf(e, h=h, sub=sub, k0=k0, kw=kw, psl=psl):
                    ins = None
                    for ks in range(0, kw, 512):
                        w = min(512, kw - ks)
                        ins = e.matmul(
                            sc_ps[psl][:, ks:ks + w],
                            qT_sb[:, h, sub * 128:(sub + 1) * 128],
                            kT_sb[:, h, k0 + ks:k0 + ks + w],
                            start=True, stop=True)
                    return ins
                tsc = P.op("tensor", scf, waits=[tqT, tkT, war_sc[psl]], inc="pe")
                if k0 + kw == k_len:
                    tsc = P.op("vector", lambda e, psl=psl, kw=kw: e.tensor_tensor(
                        out=sc_ps[psl][:, kw - 128:kw],
                        in0=sc_ps[psl][:, kw - 128:kw],
                        in1=cmask[:, :], op=ALU.add),
                        waits=[tsc, t_ident], inc="ve")
                tmx = P.op("vector", lambda e, psl=psl, kw=kw, pidx=pidx: e.reduce_max(
                    pmax[pidx], sc_ps[psl][:, 0:kw], axis=AX.X),
                    waits=[tsc], inc="ve")
                ptoks.append((psl, k0, kw, tmx, pidx))
            if len(ptoks) == 1:
                tneg = P.op("vector", lambda e, sub=sub: e.tensor_scalar(
                    out=negmax[sub], in0=pmax[0], scalar1=-1.0,
                    scalar2=None, op0=ALU.mult),
                    waits=[ptoks[0][3], war_negm[sub]], inc="ve")
            else:
                tmxc = P.op("vector", lambda e: e.tensor_max(
                    pmax[0], pmax[0], pmax[1]),
                    waits=[ptoks[0][3], ptoks[1][3]], inc="ve")
                tneg = P.op("vector", lambda e, sub=sub: e.tensor_scalar(
                    out=negmax[sub], in0=pmax[0], scalar1=-1.0,
                    scalar2=None, op0=ALU.mult),
                    waits=[tmxc, war_negm[sub]], inc="ve")
            exp_toks = []
            for (psl, k0, kw, _tmx, pidx) in ptoks:
                tex = P.op("scalar", lambda e, psl=psl, k0=k0, kw=kw, sub=sub, pidx=pidx: e.activation(
                    out=p_bf[sub % 2][:, k0:k0 + kw], in_=sc_ps[psl][:, 0:kw],
                    func=AF.Exp, bias=negmax[sub], scale=1.0,
                    accum_out=zacc[sub * 2 + pidx]),
                    waits=[tneg, war_pbf[sub % 2]], inc="se")
                war_sc[psl] = tex
                exp_toks.append(tex)
            war_negm[sub] = exp_toks[-1]
            return (sub, k_len, len(ptoks), exp_toks)

        def emit_scale(info):
            sub, k_len, npieces, exp_toks = info
            zw = list(exp_toks)
            if npieces > 1:
                tza = P.op("vector", lambda e, sub=sub: e.tensor_add(
                    zacc[sub * 2], zacc[sub * 2], zacc[sub * 2 + 1]),
                    waits=exp_toks, inc="ve")
                zw = [tza]
            trz = P.op("vector", lambda e, sub=sub: e.reciprocal(
                rz[sub], zacc[sub * 2]), waits=zw, inc="ve")
            tps = P.op("vector", lambda e, sub=sub, k_len=k_len: e.tensor_scalar(
                out=psc_bf[sub][:, 0:k_len], in0=p_bf[sub % 2][:, 0:k_len],
                scalar1=rz[sub], scalar2=None, op0=ALU.mult),
                waits=[trz, war_psc[sub]], inc="ve")
            war_pbf[sub % 2] = tps
            return (sub, k_len, tps)

        def emit_T(info):
            sub, k_len, tps = info
            nkc = k_len // 128
            t_last = None
            t_last_v = None
            t_last_s = None
            for g in range(0, nkc, 4):
                cnt = min(4, nkc - g)
                gsl = gctr[0] % 2
                gctr[0] += 1

                def tpf(e, sub=sub, g=g, cnt=cnt, gsl=gsl):
                    ins = None
                    for j in range(cnt):
                        ins = e.matmul(
                            pt_ps[gsl][:, j * 128:(j + 1) * 128],
                            psc_bf[sub][:, (g + j) * 128:(g + j + 1) * 128],
                            ident_bf[:, :], start=True, stop=True)
                    return ins
                tp = P.op("tensor", tpf, waits=[tps, war_pt[gsl]], inc="pe")
                if gctr[0] % 2 == 0:
                    tev = P.op("vector", lambda e, sub=sub, g=g, cnt=cnt, gsl=gsl: e.tensor_copy(
                        out=pT_sb[:, g:g + cnt, sub * 128:(sub + 1) * 128],
                        in_=pt_ps[gsl][:, 0:cnt * 128]
                        .rearrange("p (c t) -> p c t", c=cnt)),
                        waits=[tp, war_pT], inc="ve")
                else:
                    tev = P.op("scalar", lambda e, sub=sub, g=g, cnt=cnt, gsl=gsl: e.activation(
                        out=pT_sb[:, g:g + cnt, sub * 128:(sub + 1) * 128],
                        in_=pt_ps[gsl][:, 0:cnt * 128]
                        .rearrange("p (c t) -> p c t", c=cnt),
                        func=AF.Identity), waits=[tp, war_pT], inc="se")
                war_pt[gsl] = tev
                t_last = tev
                if tev[0] == "ve":
                    t_last_v = tev
                else:
                    t_last_s = tev
            war_psc[sub] = t_last
            return (t_last_v, t_last_s)

        def emit_ctx(s, qt, h, ev_toks, tvT):
            nonlocal war_qT, war_pT
            nkc_tot = (qt + 1) * 4

            def ctxf(e, h=h, qt=qt, nkc_tot=nkc_tot):
                ins = None
                for kc in range(nkc_tot):
                    jlo = max(0, kc - qt * 4)
                    ins = e.matmul(
                        ctx_ps[h % 2][:, jlo * 128:512],
                        vT_sb[:, kc, h * 128:(h + 1) * 128],
                        pT_sb[:, kc, jlo * 128:512],
                        start=(kc == 0), stop=(kc == nkc_tot - 1))
                return ins
            tctx = P.op("tensor", ctxf, waits=list(ev_toks) + [tvT, war_ctx[h % 2]],
                        inc="pe")
            war_pT = tctx
            tcev = P.op("scalar", lambda e, h=h: e.activation(
                out=cT_h[h % 2], in_=ctx_ps[h % 2],
                func=AF.Identity), waits=[tctx, war_cth[h % 2]], inc="se")
            war_ctx[h % 2] = tcev
            q0 = qt * 512
            tstc = P.dma("sync", "cs%d" % (h % 2), lambda e, s=s, q0=q0, h=h: e.dma_start(
                out=scr_cT[s, h, :, q0:q0 + 512], in_=cT_h[h % 2]),
                waits=[tcev])
            war_cth[h % 2] = tstc
            t_stctx[(s, qt, h)] = tstc

        for s in range(BL):
            tkT = P.dma("sync", "kt", lambda e, s=s: e.dma_start(
                out=kT_sb, in_=scr_kT[s].rearrange("c p t -> p c t")),
                waits=[last_st_k[s], war_qT])
            tvT = P.dma("sync", "vt", lambda e, s=s: e.dma_start(
                out=vT_sb, in_=scr_v[s].rearrange("c p t -> p c t")),
                waits=[last_st_v[s]])
            for qt in range(4):
                q0 = qt * 512
                tqT = P.dma("sync", "qtl", lambda e, s=s, q0=q0: e.dma_start(
                    out=qT_sb, in_=scr_qT[s, :, :, q0:q0 + 512].rearrange("c p t -> p c t")),
                    waits=[last_st_q[s], war_qT])
                steps = [(h, sub) for h in range(NH) for sub in range(4)]
                sm_q = {}
                sc_q = {}
                tev_q = {}

                head_ev = {}

                def do_tail(k2):
                    h2, sub2 = steps[k2]
                    tv, tsx = emit_T(sc_q[k2])
                    cur = head_ev.setdefault(h2, [None, None])
                    if tv is not None:
                        cur[0] = tv
                    if tsx is not None:
                        cur[1] = tsx
                    if sub2 == 3:
                        emit_ctx(s, qt, h2, [t for t in cur if t is not None], tvT)

                for k, (h, sub) in enumerate(steps):
                    sm_q[k] = emit_softmax(qt, h, sub, tqT, tkT)
                    if k >= 1:
                        sc_q[k - 1] = emit_scale(sm_q[k - 1])
                    if k >= 3:
                        do_tail(k - 3)
                k = len(steps)
                sc_q[k - 1] = emit_scale(sm_q[k - 1])
                do_tail(k - 3)
                do_tail(k - 2)
                do_tail(k - 1)
                war_qT = P.tok("pe")

        P.barrier()

        warm_act(AF.Sqrt)
        wo_sb = A.at(4 * KB, [128, 16, D], BF16)
        cT_t = [A.at(68 * KB, [128, 16, 128], BF16), A.at(72 * KB, [128, 16, 128], BF16)]
        x2a = [A.at(76 * KB, [128, D], F32), A.at(84 * KB, [128, D], F32)]
        xpa = [A.at(92 * KB, [128, D], F32), A.at(100 * KB, [128, D], F32)]
        h2_bf = [A.at(108 * KB, [128, D], BF16), A.at(112 * KB, [128, D], BF16)]
        h2T_stage = [A.at(116 * KB, [128, 16, 128], BF16), A.at(120 * KB, [128, 16, 128], BF16)]
        junk3 = A.at(124 * KB, [128, D], BF16)

        t_wo = P.dma("gpsimd", "wo", lambda e: e.dma_start(
            out=wo_sb, in_=WO.rearrange("(c p) e -> p c e", p=128)))
        war_ct2 = [None, None]
        war_x2 = [None, None]
        war_xp = [None, None]
        war_h2stage = [None, None]
        war_pso = [None, None]
        b_tp = {}
        b_sqrt = {}
        b_th = {}
        b_txp = {}
        t_stxp = [None] * NT
        t_sth2 = [None] * NT

        def b_stage2(i):
            sl = i % 2
            th = ln_stage2("ln2", sl, xpa[sl], h2_bf[sl], b_sqrt[i],
                           extra=[b_tp.get(i - 2)])
            b_th[i] = th
            war_xp[sl] = th

        def b_T(i):
            sl = i % 2
            ps = psA if sl == 0 else psB

            def tpf(e, sl=sl, ps=ps):
                ins = None
                for c in range(16):
                    ins = e.matmul(ps[:, c * 128:(c + 1) * 128],
                                   h2_bf[sl][:, c * 128:(c + 1) * 128],
                                   ident_bf[:, :], start=True, stop=True)
                return ins
            tp = P.op("tensor", tpf, waits=[b_th[i], b_txp[i]], inc="pe")
            b_tp[i] = tp

        def b_evac(i):
            sl = i % 2
            ps = psA if sl == 0 else psB
            tev = P.op("vector", lambda e, sl=sl, ps=ps: e.tensor_copy(
                out=h2T_stage[sl][:, :, :],
                in_=ps[:, 0:2048].rearrange("p (c t) -> p c t", c=16)),
                waits=[b_tp[i], war_h2stage[sl]], inc="ve")
            war_pso[sl] = tev
            tst2 = P.dma("sync", "h2s%d" % sl, lambda e, i=i, sl=sl: e.dma_start(
                out=scr_h2T[:, :, i * 128:(i + 1) * 128].rearrange("c p t -> p c t"),
                in_=h2T_stage[sl][:, :, :]),
                waits=[tev])
            war_h2stage[sl] = tst2
            t_sth2[i] = tst2

        for i in range(NT):
            sl = i % 2
            s, blk = divmod(i, 16)
            row0 = blk * 128
            if i >= 2:
                b_evac(i - 2)
            if i >= 1:
                b_stage2(i - 1)
            tct = P.dma("sync", "ct%d" % sl, lambda e, s=s, row0=row0, sl=sl, blk=blk: e.dma_start(
                out=cT_t[sl], in_=scr_cT[s, :, :, row0:row0 + 128]
                .rearrange("c p t -> p c t")),
                waits=[t_stctx[(s, blk // 4, NH - 1)], t_stctx[(s, blk // 4, NH - 2)],
                       war_ct2[sl]])
            tx = P.dma("sync", "x2%d" % sl, lambda e, s=s, row0=row0, sl=sl: e.dma_start(
                out=x2a[sl], in_=X[s, row0:row0 + 128, :]),
                waits=[war_x2[sl]])
            ps = psA if sl == 0 else psB

            def mmf(e, sl=sl, ps=ps):
                ins = None
                for d in range(16):
                    for es in range(4):
                        ins = e.matmul(ps[:, es * 512:(es + 1) * 512],
                                       cT_t[sl][:, d, :],
                                       wo_sb[:, d, es * 512:(es + 1) * 512],
                                       start=(d == 0), stop=(d == 15))
                return ins
            tmm = P.op("tensor", mmf, waits=[tct, t_wo, war_pso[sl]], inc="pe")
            txp = P.op("vector", lambda e, sl=sl, ps=ps: e.scalar_tensor_tensor(
                out=xpa[sl], in0=ps, scalar=1.0, in1=x2a[sl],
                op0=ALU.mult, op1=ALU.add),
                waits=[tmm, tx, war_xp[sl], t_stxp[i - 2] if i >= 2 else None],
                inc="ve")
            b_txp[i] = txp
            war_ct2[sl] = tmm
            war_x2[sl] = txp
            tstx = P.dma("sync", "xps%d" % sl, lambda e, i=i, sl=sl: e.dma_start(
                out=scr_xp[i * 128:(i + 1) * 128, :], in_=xpa[sl]),
                waits=[txp])
            t_stxp[i] = tstx
            b_sqrt[i] = ln_stage1("ln2", sl, xpa[sl], junk3, EPS1, src_tok=txp)
            if i >= 1:
                b_T(i - 1)
        b_stage2(NT - 1)
        b_T(NT - 1)
        b_evac(NT - 2)
        b_evac(NT - 1)

        P.barrier()

        warm_act(AF.Silu)
        h2T_sb = A.at(4 * KB, [128, 16, 2048], BF16)
        wgT_t = [A.at(68 * KB, [128, 16, 128], BF16), A.at(72 * KB, [128, 16, 128], BF16)]
        wuT_t = [A.at(76 * KB, [128, 16, 128], BF16), A.at(80 * KB, [128, 16, 128], BF16)]
        silu_sb = A.at(84 * KB, [128, 2048], F32)
        tT_sb = [A.at(92 * KB, [128, 2048], BF16), A.at(96 * KB, [128, 2048], BF16)]

        war_wgl = [None, None]
        war_wul = [None, None]
        war_tT = [None, None]
        war_h2T = None
        war_psff = [None, None]
        t_sttT = []
        wld = {}

        def ffa_load(c):
            cs = c % 2
            tg = P.dma("sync", "wgl%d" % cs, lambda e, c=c, cs=cs: e.dma_start(
                out=wgT_t[cs][:, :, :].rearrange("p c t -> p (c t)"), in_=scr_wgT[c]),
                waits=[t_wgT_done, war_wgl[cs]])
            tu = P.dma("sync", "wul%d" % cs, lambda e, c=c, cs=cs: e.dma_start(
                out=wuT_t[cs][:, :, :].rearrange("p c t -> p (c t)"), in_=scr_wuT[c]),
                waits=[t_wuT_done, war_wul[cs]])
            wld[c] = (tg, tu)

        for hf in range(2):
            th2T = P.dma("sync", "h2t", lambda e, hf=hf: e.dma_start(
                out=h2T_sb, in_=scr_h2T[:, :, hf * 2048:(hf + 1) * 2048]
                .rearrange("c p t -> p c t")),
                waits=[t_sth2[NT - 1], t_sth2[NT - 2], war_h2T])
            ffa_load(0)
            ffa_load(1)
            for c in range(NHC):
                cs = c % 2
                tg, tu = wld[c]
                tq_last = None
                for qq in range(2):
                    ps = psA if qq % 2 == 0 else psB

                    def mmf(e, cs=cs, ps=ps, qq=qq):
                        ins = None
                        for d in range(16):
                            for t2 in range(2):
                                ins = e.matmul(
                                    ps[:, t2 * 512:(t2 + 1) * 512],
                                    wgT_t[cs][:, d, :],
                                    h2T_sb[:, d, qq * 1024 + t2 * 512:qq * 1024 + (t2 + 1) * 512],
                                    start=(d == 0), stop=(d == 15))
                            for t2 in range(2):
                                ins = e.matmul(
                                    ps[:, 1024 + t2 * 512:1024 + (t2 + 1) * 512],
                                    wuT_t[cs][:, d, :],
                                    h2T_sb[:, d, qq * 1024 + t2 * 512:qq * 1024 + (t2 + 1) * 512],
                                    start=(d == 0), stop=(d == 15))
                        return ins
                    tmm = P.op("tensor", mmf,
                               waits=[tg, tu, th2T, war_psff[qq % 2]], inc="pe")

                    def siluf(e, ps=ps, qq=qq):
                        ins = None
                        for t2 in range(2):
                            ins = e.activation(
                                out=silu_sb[:, qq * 1024 + t2 * 512:qq * 1024 + (t2 + 1) * 512],
                                in_=ps[:, t2 * 512:(t2 + 1) * 512],
                                func=AF.Silu)
                        return ins
                    tsi = P.op("scalar", siluf, waits=[tmm], inc="se")

                    def multf(e, cs=cs, ps=ps, qq=qq):
                        ins = None
                        for t2 in range(2):
                            ins = e.scalar_tensor_tensor(
                                out=tT_sb[cs][:, qq * 1024 + t2 * 512:qq * 1024 + (t2 + 1) * 512],
                                in0=ps[:, 1024 + t2 * 512:1024 + (t2 + 1) * 512],
                                scalar=1.0,
                                in1=silu_sb[:, qq * 1024 + t2 * 512:qq * 1024 + (t2 + 1) * 512],
                                op0=ALU.mult, op1=ALU.mult)
                        return ins
                    tmu = P.op("vector", multf, waits=[tsi, war_tT[cs]], inc="ve")
                    war_psff[qq % 2] = tmu
                    tq_last = tmu
                tst = P.dma("sync", "tts%d" % cs, lambda e, c=c, hf=hf, cs=cs: e.dma_start(
                    out=scr_tT[c, :, hf * 2048:(hf + 1) * 2048], in_=tT_sb[cs]),
                    waits=[tq_last])
                war_tT[cs] = tst
                war_wgl[cs] = tmm
                war_wul[cs] = tmm
                if c + 2 < NHC:
                    ffa_load(c + 2)
                t_sttT.append(tst)
            war_h2T = P.tok("pe")

        P.barrier()

        tT_t = [A.at(4 * KB, [128, 11, 512], BF16), A.at(16 * KB, [128, 11, 512], BF16)]
        wdT_t = [A.at(28 * KB, [128, 11, D], BF16), A.at(72 * KB, [128, 11, D], BF16)]
        xp_t = [A.at(116 * KB, [128, D], F32), A.at(124 * KB, [128, D], F32)]
        acc = A.at(132 * KB, [128, 4, D], F32)

        war_tTt = [None, None]
        war_wdt = [None, None]
        war_xpt = [None, None]
        war_psd = [None, None]
        war_acc = [None] * 4
        acc_last = [None] * 4
        t_lastw_d = [None] * 4
        for tt in range(8):
            tok0 = tt * 512
            for grp in range(4):
                gs = grp % 2
                ttl = P.dma("sync", "tt%d" % gs, lambda e, grp=grp, gs=gs, tok0=tok0: e.dma_start(
                    out=tT_t[gs],
                    in_=scr_tT[grp * 11:(grp + 1) * 11, :, tok0:tok0 + 512]
                    .rearrange("c p t -> p c t")),
                    waits=[t_sttT[-1], t_sttT[-2], war_tTt[gs]])
                twd = P.dma("sync", "wdl%d" % gs, lambda e, grp=grp, gs=gs: e.dma_start(
                    out=wdT_t[gs],
                    in_=scr_wdT[grp * 11:(grp + 1) * 11, :, :]
                    .rearrange("c p t -> p c t")),
                    waits=[t_wdT_done, war_wdt[gs]])
                for sub in range(4):
                    sl = sub % 2
                    txl = None
                    if grp == 0:
                        txl = P.dma("sync", "xpl%d" % sl, lambda e, tok0=tok0, sub=sub, sl=sl: e.dma_start(
                            out=xp_t[sl],
                            in_=scr_xp[tok0 + sub * 128:tok0 + (sub + 1) * 128, :]),
                            waits=[t_stxp[NT - 1], t_stxp[NT - 2], war_xpt[sl]])
                    ps = psA if sl == 0 else psB

                    def mmf(e, sub=sub, gs=gs, ps=ps):
                        ins = None
                        for j in range(11):
                            for es in range(4):
                                ins = e.matmul(ps[:, es * 512:(es + 1) * 512],
                                               tT_t[gs][:, j, sub * 128:(sub + 1) * 128],
                                               wdT_t[gs][:, j, es * 512:(es + 1) * 512],
                                               start=(j == 0), stop=(j == 10))
                        return ins
                    tmm = P.op("tensor", mmf,
                               waits=[ttl, twd, war_psd[sl]], inc="pe")
                    if grp == 0:
                        tev = P.op("vector", lambda e, sub=sub, sl=sl, ps=ps: e.scalar_tensor_tensor(
                            out=acc[:, sub, :], in0=ps, scalar=1.0, in1=xp_t[sl],
                            op0=ALU.mult, op1=ALU.add),
                            waits=[tmm, txl, war_acc[sub], acc_last[sub]], inc="ve")
                        war_xpt[sl] = tev
                    else:
                        tev = P.op("vector", lambda e, sub=sub, ps=ps: e.tensor_tensor(
                            out=acc[:, sub, :], in0=acc[:, sub, :], in1=ps, op=ALU.add),
                            waits=[tmm, acc_last[sub]], inc="ve")
                    acc_last[sub] = tev
                    war_psd[sl] = tev
                    if grp == 3:
                        i = (tok0 // 128) + sub
                        s, blk = divmod(i, 16)
                        tw = P.dma("sync", "ys%d" % sub, lambda e, s=s, blk=blk, sub=sub: e.dma_start(
                            out=Y[s, blk * 128:(blk + 1) * 128, :], in_=acc[:, sub, :]),
                            waits=[tev])
                        war_acc[sub] = tw
                        t_lastw_d[sub] = tw
                war_tTt[gs] = P.tok("pe")
                war_wdt[gs] = P.tok("pe")

        P.op("sync", lambda e: e.nop(), waits=t_lastw_d)

        P.realize(ctx)
    return nc


_NC_CACHE = {}


def kernel(L0_attn_x, L0_attn_Wqkv, L0_attn_q_ln_scale, L0_attn_k_ln_scale,
           L0_attn_out_W, L0_ffn_wq_gate, L0_ffn_s_gate, L0_ffn_wq_up,
           L0_ffn_s_up, L0_ffn_wq_down, L0_ffn_s_down):
    x = np.ascontiguousarray(np.asarray(L0_attn_x, dtype=np.float32))
    wqkv = np.ascontiguousarray(np.asarray(L0_attn_Wqkv, dtype=np.float32))
    qg = np.ascontiguousarray(np.asarray(L0_attn_q_ln_scale, dtype=np.float32))
    kg = np.ascontiguousarray(np.asarray(L0_attn_k_ln_scale, dtype=np.float32))
    wo = np.ascontiguousarray(np.asarray(L0_attn_out_W, dtype=np.float32))
    wqg = np.ascontiguousarray(np.asarray(L0_ffn_wq_gate, dtype=np.int32))
    sg = np.ascontiguousarray(np.asarray(L0_ffn_s_gate, dtype=np.float32))
    wqu = np.ascontiguousarray(np.asarray(L0_ffn_wq_up, dtype=np.int32))
    su = np.ascontiguousarray(np.asarray(L0_ffn_s_up, dtype=np.float32))
    wqd = np.ascontiguousarray(np.asarray(L0_ffn_wq_down, dtype=np.int32))
    sd = np.ascontiguousarray(np.asarray(L0_ffn_s_down, dtype=np.float32))

    if "nc" not in _NC_CACHE:
        _NC_CACHE["nc"] = build_nc()
    nc = _NC_CACHE["nc"]

    in_maps = []
    for c in range(NCORES):
        in_maps.append({
            "k_x": x[c * BL:(c + 1) * BL],
            "k_wqkv": wqkv, "k_qg": qg, "k_kg": kg, "k_wo": wo,
            "k_wqg": wqg, "k_sg": sg, "k_wqu": wqu, "k_su": su,
            "k_wqd": wqd, "k_sd": sd,
        })
    res = run_bass_kernel_spmd(nc, in_maps, core_ids=list(range(NCORES)))
    out = np.concatenate([np.asarray(r["k_y"], dtype=np.float32)
                          for r in res.results], axis=0)
    return out


# version 11
# speedup vs baseline: 1.3020x; 1.2673x over previous
import numpy as np
from collections import defaultdict
from contextlib import ExitStack

import concourse.bass as bass
import concourse.mybir as mybir
from concourse.bass_utils import run_bass_kernel_spmd

F32 = mybir.dt.float32
BF16 = mybir.dt.bfloat16
I32 = mybir.dt.int32
I8 = mybir.dt.int8
AF = mybir.ActivationFunctionType
ALU = mybir.AluOpType
AX = mybir.AxisListType

NCORES = 8
B, S, D = 16, 2048, 2048
BL = B // NCORES
T = BL * S
NT = T // 128
HID = 5632
NHC = HID // 128
NH = 16
E3 = 3 * D
EPS1 = 1e-5
EPS2 = 1e-6
ISQ = float(1.0 / np.sqrt(128.0))
KB = 1024
ARENA_B = 204 * KB

ENGINES = ("sync", "gpsimd", "tensor", "vector", "scalar")
DT_SIZE = {F32: 4, BF16: 2, I32: 4, I8: 1}


def _prod(xs):
    p = 1
    for x in xs:
        p *= x
    return p


class Plan:
    def __init__(self, nc):
        self.nc = nc
        self.q = {e: [] for e in ENGINES}
        self.cnt = defaultdict(int)
        self.hwm = {e: defaultdict(int) for e in ENGINES}
        self.sems = {}
        self.pending = {e: [] for e in ENGINES}
        self.dma_sems = {}

    def tok(self, s):
        return (s, self.cnt[s])

    def dma(self, eng, key, fn, waits=()):
        name = "d_" + key
        if name not in self.dma_sems:
            self.dma_sems[name] = eng
        return self.op(eng, fn, waits, inc=name, amt=16)

    def op(self, eng, fn, waits=(), inc=None, amt=1):
        wl = []
        allw = list(self.pending[eng]) + [w for w in waits if w is not None]
        self.pending[eng] = []
        for (s, v) in allw:
            if v <= 0:
                continue
            if v > self.hwm[eng][s]:
                self.hwm[eng][s] = v
                wl.append((s, v))
        t = None
        if inc is not None:
            self.cnt[inc] += amt
            t = (inc, self.cnt[inc])
        self.q[eng].append((tuple(wl), fn, inc, amt))
        return t

    def barrier(self):
        nop = lambda e: e.nop()
        sync_w = [(n, self.cnt[n]) for n, eng in self.dma_sems.items() if eng == "sync"]
        gps_w = [(n, self.cnt[n]) for n, eng in self.dma_sems.items() if eng == "gpsimd"]
        self.op("sync", nop, waits=sync_w, inc="b")
        self.op("gpsimd", nop, waits=gps_w, inc="b")
        self.op("tensor", nop, inc="b")
        self.op("vector", nop, inc="b")
        self.op("scalar", nop, inc="b")
        tgt = self.cnt["b"]
        for e in ENGINES:
            self.pending[e].append(("b", tgt))

    def realize(self, ctx):
        for name in ("pe", "ve", "se", "ge", "b"):
            self.sems[name] = ctx.enter_context(self.nc.semaphore("ksem_" + name))
        for name in self.dma_sems:
            self.sems[name] = ctx.enter_context(self.nc.semaphore("ksem" + name))
        block = ctx.enter_context(self.nc.Block())
        sems = self.sems

        def runner(qlist):
            def run(e):
                for wl, fn, inc, amt in qlist:
                    for s, v in wl:
                        e.wait_ge(sems[s], v)
                    ins = fn(e)
                    if inc is not None:
                        ins.then_inc(sems[inc], amt)
            return run

        block.sync(runner(self.q["sync"]))
        block.gpsimd(runner(self.q["gpsimd"]))
        block.tensor(runner(self.q["tensor"]))
        block.vector(runner(self.q["vector"]))
        block.scalar(runner(self.q["scalar"]))


class Arena:
    def __init__(self, ap):
        self.ap = ap

    def at(self, off_b, shape, dtype):
        n = _prod(shape[1:]) * DT_SIZE[dtype]
        assert off_b % 64 == 0, off_b
        assert off_b + n <= ARENA_B, (off_b, n)
        v = self.ap[:, off_b:off_b + n].bitcast(dtype)
        if len(shape) == 2:
            return v
        if len(shape) == 3:
            return v.rearrange("p (a b) -> p a b", a=shape[1])
        if len(shape) == 4:
            return v.rearrange("p (a b c) -> p a b c", a=shape[1], b=shape[2])
        raise ValueError(shape)


DBG = False


def build_nc():
    nc = bass.Bass("TRN2", target_bir_lowering=False)

    X = nc.dram_tensor("k_x", [BL, S, D], F32, kind="ExternalInput").ap()
    WQKV = nc.dram_tensor("k_wqkv", [D, E3], F32, kind="ExternalInput").ap()
    QG = nc.dram_tensor("k_qg", [D], F32, kind="ExternalInput").ap()
    KG = nc.dram_tensor("k_kg", [D], F32, kind="ExternalInput").ap()
    WO = nc.dram_tensor("k_wo", [D, D], F32, kind="ExternalInput").ap()
    WQGATE = nc.dram_tensor("k_wqg", [HID, D], I32, kind="ExternalInput").ap()
    SG = nc.dram_tensor("k_sg", [HID], F32, kind="ExternalInput").ap()
    WQUP = nc.dram_tensor("k_wqu", [HID, D], I32, kind="ExternalInput").ap()
    SU = nc.dram_tensor("k_su", [HID], F32, kind="ExternalInput").ap()
    WQDOWN = nc.dram_tensor("k_wqd", [D, HID], I32, kind="ExternalInput").ap()
    SD = nc.dram_tensor("k_sd", [D], F32, kind="ExternalInput").ap()
    Y = nc.dram_tensor("k_y", [BL, S, D], F32, kind="ExternalOutput").ap()

    scr_hT = nc.dram_tensor("k_scr_ht", [16, 128, T], BF16, kind="Internal").ap()
    scr_qT = nc.dram_tensor("k_scr_qt", [BL, 16, 128, S], BF16, kind="Internal").ap()
    scr_kT = nc.dram_tensor("k_scr_kt", [BL, 16, 128, S], BF16, kind="Internal").ap()
    scr_v = nc.dram_tensor("k_scr_v", [BL, 16, 128, D], BF16, kind="Internal").ap()
    scr_cT = nc.dram_tensor("k_scr_ct", [BL, 16, 128, S], BF16, kind="Internal").ap()
    scr_xp = nc.dram_tensor("k_scr_xp", [T, D], F32, kind="Internal").ap()
    scr_h2T = nc.dram_tensor("k_scr_h2t", [16, 128, T], BF16, kind="Internal").ap()
    scr_tT = nc.dram_tensor("k_scr_tt", [NHC, 128, T], BF16, kind="Internal").ap()
    scr_wdT = nc.dram_tensor("k_scr_wdt", [NHC, 128, D], BF16, kind="Internal").ap()
    scr_wgT = nc.dram_tensor("k_scr_wgt", [NHC, 128, D], BF16, kind="Internal").ap()
    scr_wuT = nc.dram_tensor("k_scr_wut", [NHC, 128, D], BF16, kind="Internal").ap()

    with ExitStack() as ctx:
        ident_bf = ctx.enter_context(nc.sbuf_tensor("k_identb", [128, 128], BF16))
        ident_f = ctx.enter_context(nc.sbuf_tensor("k_identf", [128, 128], F32))
        cmask = ctx.enter_context(nc.sbuf_tensor("k_cmask", [128, 128], F32))
        vecs = ctx.enter_context(nc.sbuf_tensor("k_vecs", [128, 240], F32))
        smalls = ctx.enter_context(nc.sbuf_tensor("k_smalls", [128, 64], F32))
        vraw = ctx.enter_context(nc.sbuf_tensor("k_vraw", [44, 128], F32))
        arena_t = ctx.enter_context(nc.sbuf_tensor("k_arena", [128, ARENA_B], I8))
        psA_t = ctx.enter_context(nc.psum_tensor("k_psA", [128, 2048], F32))
        psB_t = ctx.enter_context(nc.psum_tensor("k_psB", [128, 2048], F32))
        A = Arena(arena_t[:, :])
        psA = psA_t[:, :]
        psB = psB_t[:, :]
        psAb = psA.bitcast(BF16)
        psBb = psB.bitcast(BF16)

        qg_sb = vecs[:, 0:16]
        kg_sb = vecs[:, 16:32]
        sd_sb = vecs[:, 32:48]
        sg_sb = vecs[:, 64:108]
        su_sb = vecs[:, 152:196]

        s12 = [smalls[:, 0:2], smalls[:, 2:4]]
        meanex = [smalls[:, 4:6], smalls[:, 6:8]]
        m2 = [smalls[:, 8:9], smalls[:, 9:10]]
        varv = [smalls[:, 10:11], smalls[:, 11:12]]
        sqv = [smalls[:, 12:13], smalls[:, 13:14]]
        rstd = [smalls[:, 14:15], smalls[:, 15:16]]
        pmax = [smalls[:, 16 + i:17 + i] for i in range(4)]
        negmax = [smalls[:, 20 + i:21 + i] for i in range(4)]
        zacc = [smalls[:, 24 + i:25 + i] for i in range(8)]
        rz = [smalls[:, 32 + i:33 + i] for i in range(4)]
        eps_t = {EPS1: smalls[:, 36:37], EPS2: smalls[:, 37:38]}
        warm_in = smalls[:, 40:44]
        warm_out = smalls[:, 44:48]

        P = Plan(nc)

        ln_state = {}

        def ln_stage1(tag, sl, src_ap, junk_ap, eps, src_tok=None):
            st = ln_state.setdefault(tag, {
                "th": [None, None], "t5": [None, None], "t6": [None, None],
                "t7": [None, None], "junk": None})
            t1 = P.op("vector", lambda e: e.reduce_sum(
                s12[sl][:, 0:1], src_ap, axis=AX.X),
                waits=[src_tok, st["th"][sl]], inc="ve")
            t2 = P.op("vector", lambda e: e.scalar_tensor_tensor(
                out=junk_ap, in0=src_ap, scalar=1.0, in1=src_ap,
                op0=ALU.mult, op1=ALU.mult, accum_out=s12[sl][:, 1:2]),
                waits=[st["junk"]], inc="ve")
            st["junk"] = t2
            t3 = P.op("vector", lambda e: e.tensor_scalar(
                out=meanex[sl], in0=s12[sl], scalar1=1.0 / D, scalar2=None,
                op0=ALU.mult), waits=[t1, t2], inc="ve")
            t4 = P.op("vector", lambda e: e.tensor_mul(
                m2[sl], meanex[sl][:, 0:1], meanex[sl][:, 0:1]),
                waits=[t3, st["t5"][sl]], inc="ve")
            t5 = P.op("vector", lambda e: e.scalar_tensor_tensor(
                out=varv[sl], in0=m2[sl], scalar=-1.0, in1=meanex[sl][:, 1:2],
                op0=ALU.mult, op1=ALU.add), waits=[t4, st["t6"][sl]], inc="ve")
            st["t5"][sl] = t5
            t6 = P.op("scalar", lambda e, eps=eps: e.activation(
                out=sqv[sl], in_=varv[sl], func=AF.Sqrt, bias=eps_t[eps], scale=1.0),
                waits=[t5, st["th"][sl], st["t7"][sl]], inc="se")
            st["t6"][sl] = t6
            return t6

        def ln_stage2(tag, sl, src_ap, dst_ap, t_sqrt, extra=()):
            st = ln_state[tag]
            t7 = P.op("vector", lambda e: e.reciprocal(rstd[sl], sqv[sl]),
                      waits=[t_sqrt], inc="ve")
            st["t7"][sl] = t7
            th = P.op("vector", lambda e: e.tensor_scalar(
                out=dst_ap, in0=src_ap, scalar1=meanex[sl][:, 0:1],
                scalar2=rstd[sl], op0=ALU.subtract, op1=ALU.mult),
                waits=[t7] + [w for w in extra if w is not None], inc="ve")
            st["th"][sl] = th
            return th

        def emit_memsets(e):
            e.memset(smalls[:, 0:36], 1.0)
            e.memset(smalls[:, 38:64], 1.0)
            e.memset(eps_t[EPS1], EPS1)
            e.memset(eps_t[EPS2], EPS2)
            e.memset(ident_bf[:, :], 0.0)
            e.memset(ident_f[:, :], 0.0)
            return e.memset(cmask[:, :], 0.0)
        t_ms = P.op("gpsimd", emit_memsets, inc="ge")

        def emit_selects(e):
            e.affine_select(out=ident_bf[:, :], in_=ident_bf[:, :],
                            compare_op=ALU.not_equal, fill=1.0, base=0,
                            pattern=[[-1, 128]], channel_multiplier=1)
            e.affine_select(out=ident_f[:, :], in_=ident_f[:, :],
                            compare_op=ALU.not_equal, fill=1.0, base=0,
                            pattern=[[-1, 128]], channel_multiplier=1)
            return e.affine_select(out=cmask[:, :], in_=cmask[:, :],
                                   compare_op=ALU.is_ge, fill=-1e30, base=0,
                                   pattern=[[-1, 128]], channel_multiplier=1)
        t_ident = P.op("gpsimd", emit_selects, waits=[t_ms], inc="ge")
        P.pending["vector"].append(t_ident)
        P.pending["scalar"].append(t_ident)

        warm_prev = [None]

        def warm_act(func):
            def warm(e, func=func):
                ins = None
                for j in range(10):
                    ins = e.activation(out=smalls[:, 44 + j:45 + j],
                                       in_=smalls[:, 40:41],
                                       func=func, bias=0.0, scale=0.0)
                return ins
            t = P.op("scalar", warm, waits=[warm_prev[0]], inc="se")
            warm_prev[0] = t
            return t

        warm_act(AF.Sqrt)

        vec_tp_prev = [None]

        def load_vec(src_ap, nch, dst, scale=None):
            tl = P.dma("sync", "vr", lambda e, ap=src_ap, n=nch: e.dma_start(
                out=vraw[0:n, :], in_=ap.rearrange("(c p) -> c p", p=128)),
                waits=[vec_tp_prev[0]])
            tp = P.op("tensor", lambda e, n=nch: e.transpose(
                psB[:, 1536:1536 + n], vraw[0:n, :], ident_f[0:n, 0:n]),
                waits=[tl, t_ident, P.tok("ve")], inc="pe")
            vec_tp_prev[0] = tp
            if scale is None:
                tv = P.op("vector", lambda e, d=dst, n=nch: e.tensor_copy(
                    out=d, in_=psB[:, 1536:1536 + n]), waits=[tp], inc="ve")
            else:
                tv = P.op("vector", lambda e, d=dst, n=nch, sc=scale: e.tensor_scalar(
                    out=d, in0=psB[:, 1536:1536 + n], scalar1=sc, scalar2=None,
                    op0=ALU.mult), waits=[tp], inc="ve")
            return tv

        t_qg = load_vec(QG, 16, qg_sb, scale=ISQ)
        t_kg = load_vec(KG, 16, kg_sb)
        t_sd = load_vec(SD, 16, sd_sb)
        t_sg = load_vec(SG, 44, sg_sb)
        t_su = load_vec(SU, 44, su_sb)

        w_i32 = [A.at(132 * KB, [128, HID], I32), A.at(80 * KB, [128, HID], I32)]
        w_bf = [A.at(156 * KB, [128, HID], BF16), A.at(104 * KB, [128, HID], BF16)]
        wT_stage = [A.at(168 * KB, [128, NHC, 128], BF16),
                    A.at(116 * KB, [128, NHC, 128], BF16)]
        pp_pt = [psB[:, 0:512], psB[:, 512:1024], psB[:, 1024:1536], psB[:, 1536:2048]]
        war_ppi = [None, None]
        war_ppbf = [None, None]
        war_ppstage = [None, None]
        pp_ctr = [0]
        pp_gctr = [0]
        pp_pt_war = [None, None, None, t_su]
        t_ppst = {"wd": None, "wg": None, "wu": None}

        def preprocess_w(tag, W_ap, ncols, scale_sb, t_scale, ntiles, store_fn):
            nblk = ncols // 128
            for tl_i in range(ntiles):
                ts_ = pp_ctr[0] % 2
                pp_ctr[0] += 1
                tl = P.dma("sync", "ppw%d" % ts_, lambda e, tl_i=tl_i, ncols=ncols, ts_=ts_: e.dma_start(
                    out=w_i32[ts_][:, 0:ncols], in_=W_ap[tl_i * 128:(tl_i + 1) * 128, :]),
                    waits=[war_ppi[ts_]])
                tdq = P.op("gpsimd", lambda e, tl_i=tl_i, ncols=ncols, ts_=ts_: e.tensor_scalar(
                    out=w_bf[ts_][:, 0:ncols], in0=w_i32[ts_][:, 0:ncols], scalar1=-128.0,
                    scalar2=scale_sb[:, tl_i:tl_i + 1], op0=ALU.add, op1=ALU.mult),
                    waits=[tl, t_scale, war_ppbf[ts_]], inc="ge")
                war_ppi[ts_] = tdq
                tev_toks = []
                tp_last = None
                for g in range(0, nblk, 4):
                    cnt = min(4, nblk - g)
                    gsl = pp_gctr[0] % 4
                    pp_gctr[0] += 1

                    def tpf(e, g=g, cnt=cnt, gsl=gsl, ts_=ts_):
                        ins = None
                        for j in range(cnt):
                            ins = e.matmul(pp_pt[gsl][:, j * 128:(j + 1) * 128],
                                           w_bf[ts_][:, (g + j) * 128:(g + j + 1) * 128],
                                           ident_bf[:, :], start=True, stop=True)
                        return ins
                    tp = P.op("tensor", tpf,
                              waits=[tdq, t_ident, pp_pt_war[gsl]], inc="pe")
                    tp_last = tp
                    eng = "vector" if pp_gctr[0] % 2 == 0 else "scalar"
                    if eng == "vector":
                        tev = P.op("vector", lambda e, g=g, cnt=cnt, gsl=gsl, ts_=ts_: e.tensor_copy(
                            out=wT_stage[ts_][:, g:g + cnt, :],
                            in_=pp_pt[gsl][:, 0:cnt * 128]
                            .rearrange("p (c t) -> p c t", c=cnt)),
                            waits=[tp, war_ppstage[ts_]], inc="ve")
                    else:
                        tev = P.op("scalar", lambda e, g=g, cnt=cnt, gsl=gsl, ts_=ts_: e.activation(
                            out=wT_stage[ts_][:, g:g + cnt, :],
                            in_=pp_pt[gsl][:, 0:cnt * 128]
                            .rearrange("p (c t) -> p c t", c=cnt),
                            func=AF.Identity),
                            waits=[tp, war_ppstage[ts_]], inc="se")
                    pp_pt_war[gsl] = tev
                    tev_toks.append(tev)
                war_ppbf[ts_] = tp_last
                tst = P.dma("gpsimd", "pps%d" % ts_, store_fn(tl_i, ts_), waits=tev_toks)
                war_ppstage[ts_] = tst
                t_ppst[tag] = tst

        def wd_store(tl_i, ts_):
            return lambda e, tl_i=tl_i, ts_=ts_: e.dma_start(
                out=scr_wdT[:, :, tl_i * 128:(tl_i + 1) * 128].rearrange("c p t -> p c t"),
                in_=wT_stage[ts_][:, 0:NHC, :])

        def wg_store(tl_i, ts_):
            return lambda e, tl_i=tl_i, ts_=ts_: e.dma_start(
                out=scr_wgT[tl_i],
                in_=wT_stage[ts_][:, 0:16, :].rearrange("p c t -> p (c t)"))

        def wu_store(tl_i, ts_):
            return lambda e, tl_i=tl_i, ts_=ts_: e.dma_start(
                out=scr_wuT[tl_i],
                in_=wT_stage[ts_][:, 0:16, :].rearrange("p c t -> p (c t)"))

        preprocess_w("wd", WQDOWN, HID, sd_sb, t_sd, 16, wd_store)
        preprocess_w("wg", WQGATE, D, sg_sb, t_sg, NHC, wg_store)
        preprocess_w("wu", WQUP, D, su_sb, t_su, NHC, wu_store)
        t_wdT_done = t_ppst["wd"]
        t_wgT_done = t_ppst["wg"]
        t_wuT_done = t_ppst["wu"]

        xa = [A.at(180 * KB, [128, D], F32), A.at(188 * KB, [128, D], F32)]
        h_bf = [A.at(196 * KB, [128, D], BF16), A.at(200 * KB, [128, D], BF16)]
        hT_stage = [A.at(68 * KB, [128, 16, 128], BF16), A.at(76 * KB, [128, 16, 128], BF16)]
        junk_bf = A.at(72 * KB, [128, D], BF16)
        war_x = [None, None]
        war_hstage = [None, None]
        pre_sqrt = {}
        pre_th = {}
        tp_pre = {}
        tev_pre = {}
        t_sthT = [None] * NT

        def pre_stage2(i):
            sl = i % 2
            th = ln_stage2("pre", sl, xa[sl], h_bf[sl], pre_sqrt[i],
                           extra=[tp_pre.get(i - 2)])
            pre_th[i] = th
            war_x[sl] = th

        def pre_T(i):
            sl = i % 2

            def tpf(e, sl=sl):
                ins = None
                for c in range(16):
                    ins = e.matmul(psA[:, c * 128:(c + 1) * 128],
                                   h_bf[sl][:, c * 128:(c + 1) * 128],
                                   ident_bf[:, :], start=True, stop=True)
                return ins
            tp = P.op("tensor", tpf, waits=[pre_th[i], t_ident, tev_pre.get(i - 1)],
                      inc="pe")
            tp_pre[i] = tp

        def pre_evac(i):
            sl = i % 2
            tev = P.op("vector", lambda e, sl=sl: e.tensor_copy(
                out=hT_stage[sl][:, :, :],
                in_=psA[:, 0:2048].rearrange("p (c t) -> p c t", c=16)),
                waits=[tp_pre[i], war_hstage[sl]], inc="ve")
            tev_pre[i] = tev
            tst = P.dma("gpsimd", "hts%d" % sl, lambda e, i=i, sl=sl: e.dma_start(
                out=scr_hT[:, :, i * 128:(i + 1) * 128].rearrange("c p t -> p c t"),
                in_=hT_stage[sl][:, :, :]),
                waits=[tev])
            war_hstage[sl] = tst
            t_sthT[i] = tst

        for i in range(NT):
            sl = i % 2
            s, blk = divmod(i, 16)
            row0 = blk * 128
            tx = P.dma("sync", "xa%d" % sl, lambda e, s=s, row0=row0, sl=sl: e.dma_start(
                out=xa[sl], in_=X[s, row0:row0 + 128, :]),
                waits=[war_x[sl]])
            if i >= 2:
                pre_evac(i - 2)
            if i >= 1:
                pre_stage2(i - 1)
                pre_T(i - 1)
            P.pending["vector"].append(tx)
            pre_sqrt[i] = ln_stage1("pre", sl, xa[sl], junk_bf, EPS1)
        pre_evac(NT - 2)
        pre_stage2(NT - 1)
        pre_T(NT - 1)
        pre_evac(NT - 1)

        P.barrier()

        wr_sb = [A.at(4 * KB, [128, 16, 2048], BF16), A.at(68 * KB, [128, 16, 2048], BF16)]
        hT_t = [A.at(132 * KB, [128, 16, 256], BF16), A.at(140 * KB, [128, 16, 256], BF16)]
        q32s = [A.at(148 * KB, [128, D], F32), A.at(156 * KB, [128, D], F32)]
        qkvT_stage = [A.at(164 * KB, [128, 16, 128], BF16), A.at(168 * KB, [128, 16, 128], BF16)]
        junk2 = A.at(172 * KB, [128, D], BF16)
        q_bf = [A.at(176 * KB, [128, D], BF16), A.at(180 * KB, [128, D], BF16)]

        t_w = [None, None, None]
        t_w[0] = P.dma("gpsimd", "wr0", lambda e: e.dma_start(
            out=wr_sb[0], in_=WQKV[:, 0:2048].rearrange("(c p) e -> p c e", p=128)))
        t_w[1] = P.dma("gpsimd", "wr1", lambda e: e.dma_start(
            out=wr_sb[1], in_=WQKV[:, 2048:4096].rearrange("(c p) e -> p c e", p=128)))

        last_st_q = [None] * BL
        last_st_k = [None] * BL
        last_st_v = [None] * BL
        war_hT = [None, None]
        war_ps = [None, None]
        war_qstage = [None, None]
        war_vbf = [None, None]
        qk_tp = {}
        qk_th = {}
        qk_sqrt = {}
        t_ht = {}

        def qk_stage2(r, i):
            sl = i % 2
            th = ln_stage2("qk", sl, q32s[sl], q_bf[sl],
                           qk_sqrt[(r, i)], extra=[qk_tp.get((r, i - 2)),
                                                   qk_tp.get((r - 1, NT - 2 + sl)) if i < 2 else None])
            qk_th[(r, i)] = th

        def qk_T(r, i):
            sl = i % 2
            ps = psA if sl == 0 else psB

            def tpf(e, sl=sl, ps=ps):
                ins = None
                for c in range(16):
                    ins = e.matmul(ps[:, c * 128:(c + 1) * 128],
                                   q_bf[sl][:, c * 128:(c + 1) * 128],
                                   ident_bf[:, :], start=True, stop=True)
                return ins
            tp = P.op("tensor", tpf, waits=[qk_th[(r, i)]], inc="pe")
            qk_tp[(r, i)] = tp

        def qk_gevac(r, i):
            sl = i % 2
            ps = psA if sl == 0 else psB
            s, blk = divmod(i, 16)
            row0 = blk * 128
            gam = qg_sb if r == 0 else kg_sb
            gam_b = gam.unsqueeze(2).broadcast_to([128, 16, 128])

            def gevf(e, sl=sl, ps=ps, gam_b=gam_b):
                return e.tensor_tensor(
                    out=qkvT_stage[sl][:, :, :],
                    in0=ps[:, 0:2048].rearrange("p (c t) -> p c t", c=16),
                    in1=gam_b, op=ALU.mult)
            tgev = P.op("vector", gevf,
                        waits=[qk_tp[(r, i)], war_qstage[sl],
                               (t_qg if r == 0 else t_kg)], inc="ve")
            war_ps[sl] = tgev
            scr = scr_qT if r == 0 else scr_kT
            tst = P.dma("gpsimd", "qks%d" % sl, lambda e, s=s, row0=row0, sl=sl, scr=scr: e.dma_start(
                out=scr[s, :, :, row0:row0 + 128].rearrange("c p t -> p c t"),
                in_=qkvT_stage[sl][:, :, :]),
                waits=[tgev])
            war_qstage[sl] = tst
            if r == 0:
                last_st_q[s] = tst
            else:
                last_st_k[s] = tst

        for r in range(3):
            for i in range(NT):
                sl = i % 2
                s, blk = divmod(i, 16)
                jp = i // 2
                hsl = jp % 2
                if i % 2 == 0:
                    w_pre = [t_sthT[i], t_sthT[i + 1]] if r == 0 else []
                    t_ht[(r, jp)] = P.dma("sync", "ht%d" % hsl, lambda e, i=i, hsl=hsl: e.dma_start(
                        out=hT_t[hsl], in_=scr_hT[:, :, i * 128:(i + 2) * 128]
                        .rearrange("c p t -> p c t")),
                        waits=w_pre + [war_hT[hsl]])
                ps = psA if sl == 0 else psB
                toff = (i % 2) * 128

                if r <= 1:
                    if i >= 1:
                        qk_stage2(r, i - 1)

                def mmf(e, hsl=hsl, toff=toff, r=r, ps=ps):
                    ins = None
                    for d in range(16):
                        for es in range(4):
                            ins = e.matmul(ps[:, es * 512:(es + 1) * 512],
                                           hT_t[hsl][:, d, toff:toff + 128],
                                           wr_sb[r % 2][:, d, es * 512:(es + 1) * 512],
                                           start=(d == 0), stop=(d == 15))
                    return ins
                tmm = P.op("tensor", mmf,
                           waits=[t_ht[(r, jp)], t_w[r], war_ps[sl]], inc="pe")
                if i % 2 == 1:
                    war_hT[hsl] = tmm
                if r <= 1:
                    if i >= 1:
                        qk_T(r, i - 1)
                        qk_gevac(r, i - 1)
                    tcp = P.op("vector", lambda e, ps=ps, sl=sl: e.tensor_copy(
                        out=q32s[sl], in_=ps),
                        waits=[tmm, ln_state.get("qk", {}).get("th", [None, None])[sl]],
                        inc="ve")
                    qk_sqrt[(r, i)] = ln_stage1("qk", sl, q32s[sl], junk2, EPS2,
                                                src_tok=tcp)
                else:
                    tcp = P.op("vector", lambda e, ps=ps, sl=sl: e.tensor_copy(
                        out=q_bf[sl], in_=ps),
                        waits=[tmm, war_vbf[sl], qk_tp.get((1, NT - 2 + sl))],
                        inc="ve")
                    war_ps[sl] = tcp
                    tstv = P.dma("gpsimd", "vs%d" % sl, lambda e, s=s, blk=blk, sl=sl: e.dma_start(
                        out=scr_v[s, blk, :, :], in_=q_bf[sl]),
                        waits=[tcp])
                    war_vbf[sl] = tstv
                    last_st_v[s] = tstv
            if r <= 1:
                qk_stage2(r, NT - 1)
                qk_T(r, NT - 1)
                qk_gevac(r, NT - 1)
            if r == 0:
                t_w[2] = P.dma("gpsimd", "wr0", lambda e: e.dma_start(
                    out=wr_sb[0], in_=WQKV[:, 4096:6144]
                    .rearrange("(c p) e -> p c e", p=128)),
                    waits=[P.tok("pe")])

        P.barrier()

        warm_act(AF.Exp)
        kT_sb = A.at(4 * KB, [128, 16, S], BF16)
        vT_sb = A.at(68 * KB, [128, 16, D], BF16)
        qT_sb = A.at(132 * KB, [128, 16, 512], BF16)
        pT_sb = A.at(148 * KB, [128, 16, 512], BF16)
        p_bf = [A.at(164 * KB, [128, S], BF16), A.at(168 * KB, [128, S], BF16)]
        psc_bf = [A.at(172 * KB, [128, S], BF16), A.at(176 * KB, [128, S], BF16),
                  A.at(196 * KB, [128, S], BF16), A.at(200 * KB, [128, S], BF16)]
        cT_h = [A.at(180 * KB, [128, 512], BF16), A.at(181 * KB, [128, 512], BF16)]
        sc_ps = [psA[:, 0:1024], psA[:, 1024:2048]]
        pt_ps = [psB[:, 0:512], psB[:, 512:1024]]
        ctx_ps = [psB[:, 1024:1536], psB[:, 1536:2048]]

        t_stctx = {}
        piece_ctr = [0]
        gctr = [0]
        war_sc = [None, None]
        war_pt = [None, None]
        war_ctx = [None, None]
        war_pbf = [None, None]
        war_psc = [None, None, None, None]
        war_negm = [None] * 4
        war_cth = [None, None]
        war_qT = None
        war_pT = None

        def emit_softmax(qt, h, sub, tqT, tkT):
            gsub = qt * 4 + sub
            k_len = (gsub + 1) * 128
            pieces = []
            k0 = 0
            while k0 < k_len:
                kw = min(1024, k_len - k0)
                pieces.append((k0, kw))
                k0 += kw
            ptoks = []
            for pidx, (k0, kw) in enumerate(pieces):
                psl = piece_ctr[0] % 2
                piece_ctr[0] += 1

                def scf(e, h=h, sub=sub, k0=k0, kw=kw, psl=psl):
                    ins = None
                    for ks in range(0, kw, 512):
                        w = min(512, kw - ks)
                        ins = e.matmul(
                            sc_ps[psl][:, ks:ks + w],
                            qT_sb[:, h, sub * 128:(sub + 1) * 128],
                            kT_sb[:, h, k0 + ks:k0 + ks + w],
                            start=True, stop=True)
                    return ins
                tsc = P.op("tensor", scf, waits=[tqT, tkT, war_sc[psl]], inc="pe")
                if k0 + kw == k_len:
                    tsc = P.op("vector", lambda e, psl=psl, kw=kw: e.tensor_tensor(
                        out=sc_ps[psl][:, kw - 128:kw],
                        in0=sc_ps[psl][:, kw - 128:kw],
                        in1=cmask[:, :], op=ALU.add),
                        waits=[tsc, t_ident], inc="ve")
                if len(pieces) == 1:
                    tmx = P.op("vector", lambda e, psl=psl, kw=kw, sub=sub: e.reduce_max(
                        negmax[sub], sc_ps[psl][:, 0:kw], axis=AX.X, negate=True),
                        waits=[tsc, war_negm[sub]], inc="ve")
                else:
                    tmx = P.op("vector", lambda e, psl=psl, kw=kw, pidx=pidx: e.reduce_max(
                        pmax[pidx], sc_ps[psl][:, 0:kw], axis=AX.X, negate=True),
                        waits=[tsc], inc="ve")
                ptoks.append((psl, k0, kw, tmx, pidx))
            if len(ptoks) == 1:
                tneg = ptoks[0][3]
            else:
                tneg = P.op("vector", lambda e, sub=sub: e.tensor_tensor(
                    out=negmax[sub], in0=pmax[0], in1=pmax[1], op=ALU.min),
                    waits=[ptoks[0][3], ptoks[1][3], war_negm[sub]], inc="ve")
            exp_toks = []
            for (psl, k0, kw, _tmx, pidx) in ptoks:
                tex = P.op("scalar", lambda e, psl=psl, k0=k0, kw=kw, sub=sub, pidx=pidx: e.activation(
                    out=p_bf[sub % 2][:, k0:k0 + kw], in_=sc_ps[psl][:, 0:kw],
                    func=AF.Exp, bias=negmax[sub], scale=1.0,
                    accum_out=zacc[sub * 2 + pidx]),
                    waits=[tneg, war_pbf[sub % 2]], inc="se")
                war_sc[psl] = tex
                exp_toks.append(tex)
            war_negm[sub] = exp_toks[-1]
            return (sub, k_len, len(ptoks), exp_toks)

        def emit_scale(info):
            sub, k_len, npieces, exp_toks = info
            zw = list(exp_toks)
            if npieces > 1:
                tza = P.op("vector", lambda e, sub=sub: e.tensor_add(
                    zacc[sub * 2], zacc[sub * 2], zacc[sub * 2 + 1]),
                    waits=exp_toks, inc="ve")
                zw = [tza]
            trz = P.op("vector", lambda e, sub=sub: e.reciprocal(
                rz[sub], zacc[sub * 2]), waits=zw, inc="ve")
            tps = P.op("vector", lambda e, sub=sub, k_len=k_len: e.tensor_scalar(
                out=psc_bf[sub][:, 0:k_len], in0=p_bf[sub % 2][:, 0:k_len],
                scalar1=rz[sub], scalar2=None, op0=ALU.mult),
                waits=[trz, war_psc[sub]], inc="ve")
            war_pbf[sub % 2] = tps
            return (sub, k_len, tps)

        def emit_T(info):
            sub, k_len, tps = info
            nkc = k_len // 128
            t_last = None
            t_last_v = None
            t_last_s = None
            for g in range(0, nkc, 4):
                cnt = min(4, nkc - g)
                gsl = gctr[0] % 2
                gctr[0] += 1

                def tpf(e, sub=sub, g=g, cnt=cnt, gsl=gsl):
                    ins = None
                    for j in range(cnt):
                        ins = e.matmul(
                            pt_ps[gsl][:, j * 128:(j + 1) * 128],
                            psc_bf[sub][:, (g + j) * 128:(g + j + 1) * 128],
                            ident_bf[:, :], start=True, stop=True)
                    return ins
                tp = P.op("tensor", tpf, waits=[tps, war_pt[gsl]], inc="pe")
                tev = P.op("scalar", lambda e, sub=sub, g=g, cnt=cnt, gsl=gsl: e.activation(
                    out=pT_sb[:, g:g + cnt, sub * 128:(sub + 1) * 128],
                    in_=pt_ps[gsl][:, 0:cnt * 128]
                    .rearrange("p (c t) -> p c t", c=cnt),
                    func=AF.Identity), waits=[tp, war_pT], inc="se")
                war_pt[gsl] = tev
                t_last = tev
                if tev[0] == "ve":
                    t_last_v = tev
                else:
                    t_last_s = tev
            war_psc[sub] = t_last
            return (t_last_v, t_last_s)

        def emit_ctx(s, qt, h, ev_toks, tvT):
            nonlocal war_qT, war_pT
            nkc_tot = (qt + 1) * 4

            def ctxf(e, h=h, qt=qt, nkc_tot=nkc_tot):
                ins = None
                for kc in range(nkc_tot):
                    jlo = max(0, kc - qt * 4)
                    ins = e.matmul(
                        ctx_ps[h % 2][:, jlo * 128:512],
                        vT_sb[:, kc, h * 128:(h + 1) * 128],
                        pT_sb[:, kc, jlo * 128:512],
                        start=(kc == 0), stop=(kc == nkc_tot - 1))
                return ins
            tctx = P.op("tensor", ctxf, waits=list(ev_toks) + [tvT, war_ctx[h % 2]],
                        inc="pe")
            war_pT = tctx
            tcev = P.op("scalar", lambda e, h=h: e.activation(
                out=cT_h[h % 2], in_=ctx_ps[h % 2],
                func=AF.Identity), waits=[tctx, war_cth[h % 2]], inc="se")
            war_ctx[h % 2] = tcev
            q0 = qt * 512
            tstc = P.dma("gpsimd", "cs%d" % (h % 2), lambda e, s=s, q0=q0, h=h: e.dma_start(
                out=scr_cT[s, h, :, q0:q0 + 512], in_=cT_h[h % 2]),
                waits=[tcev])
            war_cth[h % 2] = tstc
            t_stctx[(s, qt, h)] = tstc

        for s in range(BL):
            tkT = P.dma("sync", "kt", lambda e, s=s: e.dma_start(
                out=kT_sb, in_=scr_kT[s].rearrange("c p t -> p c t")),
                waits=[last_st_k[s], war_qT])
            tvT = P.dma("sync", "vt", lambda e, s=s: e.dma_start(
                out=vT_sb, in_=scr_v[s].rearrange("c p t -> p c t")),
                waits=[last_st_v[s]])
            for qt in range(4):
                q0 = qt * 512
                tqT = P.dma("sync", "qtl", lambda e, s=s, q0=q0: e.dma_start(
                    out=qT_sb, in_=scr_qT[s, :, :, q0:q0 + 512].rearrange("c p t -> p c t")),
                    waits=[last_st_q[s], war_qT])
                steps = [(h, sub) for h in range(NH) for sub in range(4)]
                sm_q = {}
                sc_q = {}
                tev_q = {}

                head_ev = {}

                def do_tail(k2):
                    h2, sub2 = steps[k2]
                    tv, tsx = emit_T(sc_q[k2])
                    cur = head_ev.setdefault(h2, [None, None])
                    if tv is not None:
                        cur[0] = tv
                    if tsx is not None:
                        cur[1] = tsx
                    if sub2 == 3:
                        emit_ctx(s, qt, h2, [t for t in cur if t is not None], tvT)

                for k, (h, sub) in enumerate(steps):
                    sm_q[k] = emit_softmax(qt, h, sub, tqT, tkT)
                    if k >= 1:
                        sc_q[k - 1] = emit_scale(sm_q[k - 1])
                    if k >= 3:
                        do_tail(k - 3)
                k = len(steps)
                sc_q[k - 1] = emit_scale(sm_q[k - 1])
                do_tail(k - 3)
                do_tail(k - 2)
                do_tail(k - 1)
                war_qT = P.tok("pe")

        P.barrier()

        warm_act(AF.Sqrt)
        wo_sb = A.at(4 * KB, [128, 16, D], BF16)
        cT_t = [A.at(68 * KB, [128, 16, 128], BF16), A.at(72 * KB, [128, 16, 128], BF16)]
        x2a = [A.at(76 * KB, [128, D], F32), A.at(84 * KB, [128, D], F32)]
        xpa = [A.at(92 * KB, [128, D], F32), A.at(100 * KB, [128, D], F32)]
        h2_bf = [A.at(108 * KB, [128, D], BF16), A.at(112 * KB, [128, D], BF16)]
        h2T_stage = [A.at(116 * KB, [128, 16, 128], BF16), A.at(120 * KB, [128, 16, 128], BF16)]
        junk3 = A.at(124 * KB, [128, D], BF16)

        t_wo = P.dma("gpsimd", "wo", lambda e: e.dma_start(
            out=wo_sb, in_=WO.rearrange("(c p) e -> p c e", p=128)))
        war_ct2 = [None, None]
        war_x2 = [None, None]
        war_xp = [None, None]
        war_h2stage = [None, None]
        war_pso = [None, None]
        b_tp = {}
        b_sqrt = {}
        b_th = {}
        b_txp = {}
        t_stxp = [None] * NT
        t_sth2 = [None] * NT

        def b_stage2(i):
            sl = i % 2
            th = ln_stage2("ln2", sl, xpa[sl], h2_bf[sl], b_sqrt[i],
                           extra=[b_tp.get(i - 2)])
            b_th[i] = th
            war_xp[sl] = th

        def b_T(i):
            sl = i % 2
            ps = psA if sl == 0 else psB

            def tpf(e, sl=sl, ps=ps):
                ins = None
                for c in range(16):
                    ins = e.matmul(ps[:, c * 128:(c + 1) * 128],
                                   h2_bf[sl][:, c * 128:(c + 1) * 128],
                                   ident_bf[:, :], start=True, stop=True)
                return ins
            tp = P.op("tensor", tpf, waits=[b_th[i], b_txp[i]], inc="pe")
            b_tp[i] = tp

        def b_evac(i):
            sl = i % 2
            ps = psA if sl == 0 else psB
            tev = P.op("scalar", lambda e, sl=sl, ps=ps: e.activation(
                out=h2T_stage[sl][:, :, :],
                in_=ps[:, 0:2048].rearrange("p (c t) -> p c t", c=16),
                func=AF.Identity),
                waits=[b_tp[i], war_h2stage[sl]], inc="se")
            war_pso[sl] = tev
            tst2 = P.dma("gpsimd", "h2s%d" % sl, lambda e, i=i, sl=sl: e.dma_start(
                out=scr_h2T[:, :, i * 128:(i + 1) * 128].rearrange("c p t -> p c t"),
                in_=h2T_stage[sl][:, :, :]),
                waits=[tev])
            war_h2stage[sl] = tst2
            t_sth2[i] = tst2

        for i in range(NT):
            sl = i % 2
            s, blk = divmod(i, 16)
            row0 = blk * 128
            if i >= 1:
                b_stage2(i - 1)
            tct = P.dma("sync", "ct%d" % sl, lambda e, s=s, row0=row0, sl=sl, blk=blk: e.dma_start(
                out=cT_t[sl], in_=scr_cT[s, :, :, row0:row0 + 128]
                .rearrange("c p t -> p c t")),
                waits=[t_stctx[(s, blk // 4, NH - 1)], t_stctx[(s, blk // 4, NH - 2)],
                       war_ct2[sl]])
            tx = P.dma("sync", "x2%d" % sl, lambda e, s=s, row0=row0, sl=sl: e.dma_start(
                out=x2a[sl], in_=X[s, row0:row0 + 128, :]),
                waits=[war_x2[sl]])
            ps = psA if sl == 0 else psB

            def mmf(e, sl=sl, ps=ps):
                ins = None
                for d in range(16):
                    for es in range(4):
                        ins = e.matmul(ps[:, es * 512:(es + 1) * 512],
                                       cT_t[sl][:, d, :],
                                       wo_sb[:, d, es * 512:(es + 1) * 512],
                                       start=(d == 0), stop=(d == 15))
                return ins
            tmm = P.op("tensor", mmf, waits=[tct, t_wo, war_pso[sl]], inc="pe")
            if i >= 1:
                b_T(i - 1)
                b_evac(i - 1)
            txp = P.op("vector", lambda e, sl=sl, ps=ps: e.scalar_tensor_tensor(
                out=xpa[sl], in0=ps, scalar=1.0, in1=x2a[sl],
                op0=ALU.mult, op1=ALU.add),
                waits=[tmm, tx, war_xp[sl], t_stxp[i - 2] if i >= 2 else None],
                inc="ve")
            b_txp[i] = txp
            war_ct2[sl] = tmm
            war_x2[sl] = txp
            tstx = P.dma("gpsimd", "xps%d" % sl, lambda e, i=i, sl=sl: e.dma_start(
                out=scr_xp[i * 128:(i + 1) * 128, :], in_=xpa[sl]),
                waits=[txp])
            t_stxp[i] = tstx
            b_sqrt[i] = ln_stage1("ln2", sl, xpa[sl], junk3, EPS1, src_tok=txp)
        b_stage2(NT - 1)
        b_T(NT - 1)
        b_evac(NT - 1)

        P.barrier()

        warm_act(AF.Silu)
        h2T_sb = A.at(4 * KB, [128, 16, 2048], BF16)
        wgT_t = [A.at(68 * KB, [128, 16, 128], BF16), A.at(72 * KB, [128, 16, 128], BF16)]
        wuT_t = [A.at(76 * KB, [128, 16, 128], BF16), A.at(80 * KB, [128, 16, 128], BF16)]
        silu_sb = A.at(84 * KB, [128, 2048], F32)
        tT_sb = [A.at(92 * KB, [128, 2048], BF16), A.at(96 * KB, [128, 2048], BF16)]

        war_wgl = [None, None]
        war_wul = [None, None]
        war_tT = [None, None]
        war_h2T = None
        war_psff = [None, None]
        t_sttT = []
        wld = {}

        def ffa_load(c):
            cs = c % 2
            tg = P.dma("sync", "wgl%d" % cs, lambda e, c=c, cs=cs: e.dma_start(
                out=wgT_t[cs][:, :, :].rearrange("p c t -> p (c t)"), in_=scr_wgT[c]),
                waits=[t_wgT_done, war_wgl[cs]])
            tu = P.dma("sync", "wul%d" % cs, lambda e, c=c, cs=cs: e.dma_start(
                out=wuT_t[cs][:, :, :].rearrange("p c t -> p (c t)"), in_=scr_wuT[c]),
                waits=[t_wuT_done, war_wul[cs]])
            wld[c] = (tg, tu)

        for hf in range(2):
            th2T = P.dma("sync", "h2t", lambda e, hf=hf: e.dma_start(
                out=h2T_sb, in_=scr_h2T[:, :, hf * 2048:(hf + 1) * 2048]
                .rearrange("c p t -> p c t")),
                waits=[t_sth2[NT - 1], t_sth2[NT - 2], war_h2T])
            ffa_load(0)
            ffa_load(1)
            for c in range(NHC):
                cs = c % 2
                tg, tu = wld[c]
                tq_last = None
                for qq in range(2):
                    ps = psA if qq % 2 == 0 else psB

                    def mmf(e, cs=cs, ps=ps, qq=qq):
                        ins = None
                        for d in range(16):
                            for t2 in range(2):
                                ins = e.matmul(
                                    ps[:, t2 * 512:(t2 + 1) * 512],
                                    wgT_t[cs][:, d, :],
                                    h2T_sb[:, d, qq * 1024 + t2 * 512:qq * 1024 + (t2 + 1) * 512],
                                    start=(d == 0), stop=(d == 15))
                            for t2 in range(2):
                                ins = e.matmul(
                                    ps[:, 1024 + t2 * 512:1024 + (t2 + 1) * 512],
                                    wuT_t[cs][:, d, :],
                                    h2T_sb[:, d, qq * 1024 + t2 * 512:qq * 1024 + (t2 + 1) * 512],
                                    start=(d == 0), stop=(d == 15))
                        return ins
                    tmm = P.op("tensor", mmf,
                               waits=[tg, tu, th2T, war_psff[qq % 2]], inc="pe")

                    def siluf(e, ps=ps, qq=qq):
                        ins = None
                        for t2 in range(2):
                            ins = e.activation(
                                out=silu_sb[:, qq * 1024 + t2 * 512:qq * 1024 + (t2 + 1) * 512],
                                in_=ps[:, t2 * 512:(t2 + 1) * 512],
                                func=AF.Silu)
                        return ins
                    tsi = P.op("scalar", siluf, waits=[tmm], inc="se")

                    def multf(e, cs=cs, ps=ps, qq=qq):
                        ins = None
                        for t2 in range(2):
                            ins = e.scalar_tensor_tensor(
                                out=tT_sb[cs][:, qq * 1024 + t2 * 512:qq * 1024 + (t2 + 1) * 512],
                                in0=ps[:, 1024 + t2 * 512:1024 + (t2 + 1) * 512],
                                scalar=1.0,
                                in1=silu_sb[:, qq * 1024 + t2 * 512:qq * 1024 + (t2 + 1) * 512],
                                op0=ALU.mult, op1=ALU.mult)
                        return ins
                    tmu = P.op("vector", multf, waits=[tsi, war_tT[cs]], inc="ve")
                    war_psff[qq % 2] = tmu
                    tq_last = tmu
                tst = P.dma("gpsimd", "tts%d" % cs, lambda e, c=c, hf=hf, cs=cs: e.dma_start(
                    out=scr_tT[c, :, hf * 2048:(hf + 1) * 2048], in_=tT_sb[cs]),
                    waits=[tq_last])
                war_tT[cs] = tst
                war_wgl[cs] = tmm
                war_wul[cs] = tmm
                if c + 2 < NHC:
                    ffa_load(c + 2)
                t_sttT.append(tst)
            war_h2T = P.tok("pe")

        P.barrier()

        tT_t = [A.at(4 * KB, [128, 11, 512], BF16), A.at(16 * KB, [128, 11, 512], BF16)]
        wdT_t = [A.at(28 * KB, [128, 11, D], BF16), A.at(72 * KB, [128, 11, D], BF16)]
        xp_t = [A.at(116 * KB, [128, D], F32), A.at(124 * KB, [128, D], F32)]
        acc = A.at(132 * KB, [128, 4, D], F32)

        war_tTt = [None, None]
        war_wdt = [None, None]
        war_xpt = [None, None]
        war_psd = [None, None]
        war_acc = [None] * 4
        acc_last = [None] * 4
        t_lastw_d = [None] * 4
        for tt in range(8):
            tok0 = tt * 512
            for grp in range(4):
                gs = grp % 2
                ttl = P.dma("sync", "tt%d" % gs, lambda e, grp=grp, gs=gs, tok0=tok0: e.dma_start(
                    out=tT_t[gs],
                    in_=scr_tT[grp * 11:(grp + 1) * 11, :, tok0:tok0 + 512]
                    .rearrange("c p t -> p c t")),
                    waits=[t_sttT[-1], t_sttT[-2], war_tTt[gs]])
                twd = P.dma("sync", "wdl%d" % gs, lambda e, grp=grp, gs=gs: e.dma_start(
                    out=wdT_t[gs],
                    in_=scr_wdT[grp * 11:(grp + 1) * 11, :, :]
                    .rearrange("c p t -> p c t")),
                    waits=[t_wdT_done, war_wdt[gs]])
                for sub in range(4):
                    sl = sub % 2
                    txl = None
                    if grp == 0:
                        txl = P.dma("sync", "xpl%d" % sl, lambda e, tok0=tok0, sub=sub, sl=sl: e.dma_start(
                            out=xp_t[sl],
                            in_=scr_xp[tok0 + sub * 128:tok0 + (sub + 1) * 128, :]),
                            waits=[t_stxp[NT - 1], t_stxp[NT - 2], war_xpt[sl]])
                    ps = psA if sl == 0 else psB

                    def mmf(e, sub=sub, gs=gs, ps=ps):
                        ins = None
                        for j in range(11):
                            for es in range(4):
                                ins = e.matmul(ps[:, es * 512:(es + 1) * 512],
                                               tT_t[gs][:, j, sub * 128:(sub + 1) * 128],
                                               wdT_t[gs][:, j, es * 512:(es + 1) * 512],
                                               start=(j == 0), stop=(j == 10))
                        return ins
                    tmm = P.op("tensor", mmf,
                               waits=[ttl, twd, war_psd[sl]], inc="pe")
                    if grp == 0:
                        tev = P.op("vector", lambda e, sub=sub, sl=sl, ps=ps: e.scalar_tensor_tensor(
                            out=acc[:, sub, :], in0=ps, scalar=1.0, in1=xp_t[sl],
                            op0=ALU.mult, op1=ALU.add),
                            waits=[tmm, txl, war_acc[sub], acc_last[sub]], inc="ve")
                        war_xpt[sl] = tev
                    else:
                        tev = P.op("vector", lambda e, sub=sub, ps=ps: e.tensor_tensor(
                            out=acc[:, sub, :], in0=acc[:, sub, :], in1=ps, op=ALU.add),
                            waits=[tmm, acc_last[sub]], inc="ve")
                    acc_last[sub] = tev
                    war_psd[sl] = tev
                    if grp == 3:
                        i = (tok0 // 128) + sub
                        s, blk = divmod(i, 16)
                        tw = P.dma("gpsimd", "ys%d" % sub, lambda e, s=s, blk=blk, sub=sub: e.dma_start(
                            out=Y[s, blk * 128:(blk + 1) * 128, :], in_=acc[:, sub, :]),
                            waits=[tev])
                        war_acc[sub] = tw
                        t_lastw_d[sub] = tw
                war_tTt[gs] = P.tok("pe")
                war_wdt[gs] = P.tok("pe")

        P.op("sync", lambda e: e.nop(), waits=t_lastw_d)

        P.realize(ctx)
    return nc


_NC_CACHE = {}


def kernel(L0_attn_x, L0_attn_Wqkv, L0_attn_q_ln_scale, L0_attn_k_ln_scale,
           L0_attn_out_W, L0_ffn_wq_gate, L0_ffn_s_gate, L0_ffn_wq_up,
           L0_ffn_s_up, L0_ffn_wq_down, L0_ffn_s_down):
    x = np.ascontiguousarray(np.asarray(L0_attn_x, dtype=np.float32))
    wqkv = np.ascontiguousarray(np.asarray(L0_attn_Wqkv, dtype=np.float32))
    qg = np.ascontiguousarray(np.asarray(L0_attn_q_ln_scale, dtype=np.float32))
    kg = np.ascontiguousarray(np.asarray(L0_attn_k_ln_scale, dtype=np.float32))
    wo = np.ascontiguousarray(np.asarray(L0_attn_out_W, dtype=np.float32))
    wqg = np.ascontiguousarray(np.asarray(L0_ffn_wq_gate, dtype=np.int32))
    sg = np.ascontiguousarray(np.asarray(L0_ffn_s_gate, dtype=np.float32))
    wqu = np.ascontiguousarray(np.asarray(L0_ffn_wq_up, dtype=np.int32))
    su = np.ascontiguousarray(np.asarray(L0_ffn_s_up, dtype=np.float32))
    wqd = np.ascontiguousarray(np.asarray(L0_ffn_wq_down, dtype=np.int32))
    sd = np.ascontiguousarray(np.asarray(L0_ffn_s_down, dtype=np.float32))

    if "nc" not in _NC_CACHE:
        _NC_CACHE["nc"] = build_nc()
    nc = _NC_CACHE["nc"]

    in_maps = []
    for c in range(NCORES):
        in_maps.append({
            "k_x": x[c * BL:(c + 1) * BL],
            "k_wqkv": wqkv, "k_qg": qg, "k_kg": kg, "k_wo": wo,
            "k_wqg": wqg, "k_sg": sg, "k_wqu": wqu, "k_su": su,
            "k_wqd": wqd, "k_sd": sd,
        })
    res = run_bass_kernel_spmd(nc, in_maps, core_ids=list(range(NCORES)))
    out = np.concatenate([np.asarray(r["k_y"], dtype=np.float32)
                          for r in res.results], axis=0)
    return out


# version 12 (final)
# speedup vs baseline: 1.3098x; 1.0060x over previous
import numpy as np
from collections import defaultdict
from contextlib import ExitStack

import concourse.bass as bass
import concourse.mybir as mybir
from concourse.bass_utils import run_bass_kernel_spmd

F32 = mybir.dt.float32
BF16 = mybir.dt.bfloat16
I32 = mybir.dt.int32
I8 = mybir.dt.int8
AF = mybir.ActivationFunctionType
ALU = mybir.AluOpType
AX = mybir.AxisListType

NCORES = 8
B, S, D = 16, 2048, 2048
BL = B // NCORES
T = BL * S
NT = T // 128
HID = 5632
NHC = HID // 128
NH = 16
E3 = 3 * D
EPS1 = 1e-5
EPS2 = 1e-6
ISQ = float(1.0 / np.sqrt(128.0))
KB = 1024
ARENA_B = 204 * KB

ENGINES = ("sync", "gpsimd", "tensor", "vector", "scalar")
DT_SIZE = {F32: 4, BF16: 2, I32: 4, I8: 1}


def _prod(xs):
    p = 1
    for x in xs:
        p *= x
    return p


class Plan:
    def __init__(self, nc):
        self.nc = nc
        self.q = {e: [] for e in ENGINES}
        self.cnt = defaultdict(int)
        self.hwm = {e: defaultdict(int) for e in ENGINES}
        self.sems = {}
        self.pending = {e: [] for e in ENGINES}
        self.dma_sems = {}

    def tok(self, s):
        return (s, self.cnt[s])

    def dma(self, eng, key, fn, waits=()):
        name = "d_" + key
        if name not in self.dma_sems:
            self.dma_sems[name] = eng
        return self.op(eng, fn, waits, inc=name, amt=16)

    def op(self, eng, fn, waits=(), inc=None, amt=1):
        wl = []
        allw = list(self.pending[eng]) + [w for w in waits if w is not None]
        self.pending[eng] = []
        for (s, v) in allw:
            if v <= 0:
                continue
            if v > self.hwm[eng][s]:
                self.hwm[eng][s] = v
                wl.append((s, v))
        t = None
        if inc is not None:
            self.cnt[inc] += amt
            t = (inc, self.cnt[inc])
        self.q[eng].append((tuple(wl), fn, inc, amt))
        return t

    def barrier(self):
        nop = lambda e: e.nop()
        sync_w = [(n, self.cnt[n]) for n, eng in self.dma_sems.items() if eng == "sync"]
        gps_w = [(n, self.cnt[n]) for n, eng in self.dma_sems.items() if eng == "gpsimd"]
        self.op("sync", nop, waits=sync_w, inc="b")
        self.op("gpsimd", nop, waits=gps_w, inc="b")
        self.op("tensor", nop, inc="b")
        self.op("vector", nop, inc="b")
        self.op("scalar", nop, inc="b")
        tgt = self.cnt["b"]
        for e in ENGINES:
            self.pending[e].append(("b", tgt))

    def realize(self, ctx):
        for name in ("pe", "ve", "se", "ge", "b"):
            self.sems[name] = ctx.enter_context(self.nc.semaphore("ksem_" + name))
        for name in self.dma_sems:
            self.sems[name] = ctx.enter_context(self.nc.semaphore("ksem" + name))
        block = ctx.enter_context(self.nc.Block())
        sems = self.sems

        def runner(qlist):
            def run(e):
                for wl, fn, inc, amt in qlist:
                    for s, v in wl:
                        e.wait_ge(sems[s], v)
                    ins = fn(e)
                    if inc is not None:
                        ins.then_inc(sems[inc], amt)
            return run

        block.sync(runner(self.q["sync"]))
        block.gpsimd(runner(self.q["gpsimd"]))
        block.tensor(runner(self.q["tensor"]))
        block.vector(runner(self.q["vector"]))
        block.scalar(runner(self.q["scalar"]))


class Arena:
    def __init__(self, ap):
        self.ap = ap

    def at(self, off_b, shape, dtype):
        n = _prod(shape[1:]) * DT_SIZE[dtype]
        assert off_b % 64 == 0, off_b
        assert off_b + n <= ARENA_B, (off_b, n)
        v = self.ap[:, off_b:off_b + n].bitcast(dtype)
        if len(shape) == 2:
            return v
        if len(shape) == 3:
            return v.rearrange("p (a b) -> p a b", a=shape[1])
        if len(shape) == 4:
            return v.rearrange("p (a b c) -> p a b c", a=shape[1], b=shape[2])
        raise ValueError(shape)


DBG = False


def build_nc():
    nc = bass.Bass("TRN2", target_bir_lowering=False)

    X = nc.dram_tensor("k_x", [BL, S, D], F32, kind="ExternalInput").ap()
    WQKV = nc.dram_tensor("k_wqkv", [D, E3], F32, kind="ExternalInput").ap()
    QG = nc.dram_tensor("k_qg", [D], F32, kind="ExternalInput").ap()
    KG = nc.dram_tensor("k_kg", [D], F32, kind="ExternalInput").ap()
    WO = nc.dram_tensor("k_wo", [D, D], F32, kind="ExternalInput").ap()
    WQGATE = nc.dram_tensor("k_wqg", [HID, D], I32, kind="ExternalInput").ap()
    SG = nc.dram_tensor("k_sg", [HID], F32, kind="ExternalInput").ap()
    WQUP = nc.dram_tensor("k_wqu", [HID, D], I32, kind="ExternalInput").ap()
    SU = nc.dram_tensor("k_su", [HID], F32, kind="ExternalInput").ap()
    WQDOWN = nc.dram_tensor("k_wqd", [D, HID], I32, kind="ExternalInput").ap()
    SD = nc.dram_tensor("k_sd", [D], F32, kind="ExternalInput").ap()
    Y = nc.dram_tensor("k_y", [BL, S, D], F32, kind="ExternalOutput").ap()

    scr_hT = nc.dram_tensor("k_scr_ht", [16, 128, T], BF16, kind="Internal").ap()
    scr_qT = nc.dram_tensor("k_scr_qt", [BL, 16, 128, S], BF16, kind="Internal").ap()
    scr_kT = nc.dram_tensor("k_scr_kt", [BL, 16, 128, S], BF16, kind="Internal").ap()
    scr_v = nc.dram_tensor("k_scr_v", [BL, 16, 128, D], BF16, kind="Internal").ap()
    scr_cT = nc.dram_tensor("k_scr_ct", [BL, 16, 128, S], BF16, kind="Internal").ap()
    scr_xp = nc.dram_tensor("k_scr_xp", [T, D], F32, kind="Internal").ap()
    scr_h2T = nc.dram_tensor("k_scr_h2t", [16, 128, T], BF16, kind="Internal").ap()
    scr_tT = nc.dram_tensor("k_scr_tt", [NHC, 128, T], BF16, kind="Internal").ap()
    scr_wdT = nc.dram_tensor("k_scr_wdt", [NHC, 128, D], BF16, kind="Internal").ap()
    scr_wgT = nc.dram_tensor("k_scr_wgt", [NHC, 128, D], BF16, kind="Internal").ap()
    scr_wuT = nc.dram_tensor("k_scr_wut", [NHC, 128, D], BF16, kind="Internal").ap()

    with ExitStack() as ctx:
        ident_bf = ctx.enter_context(nc.sbuf_tensor("k_identb", [128, 128], BF16))
        ident_f = ctx.enter_context(nc.sbuf_tensor("k_identf", [128, 128], F32))
        cmask = ctx.enter_context(nc.sbuf_tensor("k_cmask", [128, 128], F32))
        vecs = ctx.enter_context(nc.sbuf_tensor("k_vecs", [128, 240], F32))
        smalls = ctx.enter_context(nc.sbuf_tensor("k_smalls", [128, 64], F32))
        vraw = ctx.enter_context(nc.sbuf_tensor("k_vraw", [44, 128], F32))
        arena_t = ctx.enter_context(nc.sbuf_tensor("k_arena", [128, ARENA_B], I8))
        psA_t = ctx.enter_context(nc.psum_tensor("k_psA", [128, 2048], F32))
        psB_t = ctx.enter_context(nc.psum_tensor("k_psB", [128, 2048], F32))
        A = Arena(arena_t[:, :])
        psA = psA_t[:, :]
        psB = psB_t[:, :]
        psAb = psA.bitcast(BF16)
        psBb = psB.bitcast(BF16)

        qg_sb = vecs[:, 0:16]
        kg_sb = vecs[:, 16:32]
        sd_sb = vecs[:, 32:48]
        sg_sb = vecs[:, 64:108]
        su_sb = vecs[:, 152:196]

        s12 = [smalls[:, 0:2], smalls[:, 2:4]]
        meanex = [smalls[:, 4:6], smalls[:, 6:8]]
        m2 = [smalls[:, 8:9], smalls[:, 9:10]]
        varv = [smalls[:, 10:11], smalls[:, 11:12]]
        sqv = [smalls[:, 12:13], smalls[:, 13:14]]
        rstd = [smalls[:, 14:15], smalls[:, 15:16]]
        pmax = [smalls[:, 16 + i:17 + i] for i in range(4)]
        negmax = [smalls[:, 20 + i:21 + i] for i in range(4)]
        zacc = [smalls[:, 24 + i:25 + i] for i in range(8)]
        rz = [smalls[:, 32 + i:33 + i] for i in range(4)]
        eps_t = {EPS1: smalls[:, 36:37], EPS2: smalls[:, 37:38]}
        warm_in = smalls[:, 40:44]
        warm_out = smalls[:, 44:48]

        P = Plan(nc)

        ln_state = {}

        def ln_stage1(tag, sl, src_ap, junk_ap, eps, src_tok=None):
            st = ln_state.setdefault(tag, {
                "th": [None, None], "t5": [None, None], "t6": [None, None],
                "t7": [None, None], "junk": None})
            t1 = P.op("vector", lambda e: e.reduce_sum(
                s12[sl][:, 0:1], src_ap, axis=AX.X),
                waits=[src_tok, st["th"][sl]], inc="ve")
            t2 = P.op("vector", lambda e: e.scalar_tensor_tensor(
                out=junk_ap, in0=src_ap, scalar=1.0, in1=src_ap,
                op0=ALU.mult, op1=ALU.mult, accum_out=s12[sl][:, 1:2]),
                waits=[st["junk"]], inc="ve")
            st["junk"] = t2
            t3 = P.op("vector", lambda e: e.tensor_scalar(
                out=meanex[sl], in0=s12[sl], scalar1=1.0 / D, scalar2=None,
                op0=ALU.mult), waits=[t1, t2], inc="ve")
            t4 = P.op("vector", lambda e: e.tensor_mul(
                m2[sl], meanex[sl][:, 0:1], meanex[sl][:, 0:1]),
                waits=[t3, st["t5"][sl]], inc="ve")
            t5 = P.op("vector", lambda e: e.scalar_tensor_tensor(
                out=varv[sl], in0=m2[sl], scalar=-1.0, in1=meanex[sl][:, 1:2],
                op0=ALU.mult, op1=ALU.add), waits=[t4, st["t6"][sl]], inc="ve")
            st["t5"][sl] = t5
            t6 = P.op("scalar", lambda e, eps=eps: e.activation(
                out=sqv[sl], in_=varv[sl], func=AF.Sqrt, bias=eps_t[eps], scale=1.0),
                waits=[t5, st["th"][sl], st["t7"][sl]], inc="se")
            st["t6"][sl] = t6
            return t6

        def ln_stage2(tag, sl, src_ap, dst_ap, t_sqrt, extra=()):
            st = ln_state[tag]
            t7 = P.op("vector", lambda e: e.reciprocal(rstd[sl], sqv[sl]),
                      waits=[t_sqrt], inc="ve")
            st["t7"][sl] = t7
            th = P.op("vector", lambda e: e.tensor_scalar(
                out=dst_ap, in0=src_ap, scalar1=meanex[sl][:, 0:1],
                scalar2=rstd[sl], op0=ALU.subtract, op1=ALU.mult),
                waits=[t7] + [w for w in extra if w is not None], inc="ve")
            st["th"][sl] = th
            return th

        def emit_memsets(e):
            e.memset(smalls[:, 0:36], 1.0)
            e.memset(smalls[:, 38:64], 1.0)
            e.memset(eps_t[EPS1], EPS1)
            e.memset(eps_t[EPS2], EPS2)
            e.memset(ident_bf[:, :], 0.0)
            e.memset(ident_f[:, :], 0.0)
            return e.memset(cmask[:, :], 0.0)
        t_ms = P.op("gpsimd", emit_memsets, inc="ge")

        def emit_selects(e):
            e.affine_select(out=ident_bf[:, :], in_=ident_bf[:, :],
                            compare_op=ALU.not_equal, fill=1.0, base=0,
                            pattern=[[-1, 128]], channel_multiplier=1)
            e.affine_select(out=ident_f[:, :], in_=ident_f[:, :],
                            compare_op=ALU.not_equal, fill=1.0, base=0,
                            pattern=[[-1, 128]], channel_multiplier=1)
            return e.affine_select(out=cmask[:, :], in_=cmask[:, :],
                                   compare_op=ALU.is_ge, fill=-1e30, base=0,
                                   pattern=[[-1, 128]], channel_multiplier=1)
        t_ident = P.op("gpsimd", emit_selects, waits=[t_ms], inc="ge")
        P.pending["vector"].append(t_ident)
        P.pending["scalar"].append(t_ident)

        warm_prev = [None]

        def warm_act(func):
            def warm(e, func=func):
                ins = None
                for j in range(10):
                    ins = e.activation(out=smalls[:, 44 + j:45 + j],
                                       in_=smalls[:, 40:41],
                                       func=func, bias=0.0, scale=0.0)
                return ins
            t = P.op("scalar", warm, waits=[warm_prev[0]], inc="se")
            warm_prev[0] = t
            return t

        warm_act(AF.Sqrt)

        vec_tp_prev = [None]

        def load_vec(src_ap, nch, dst, scale=None):
            tl = P.dma("sync", "vr", lambda e, ap=src_ap, n=nch: e.dma_start(
                out=vraw[0:n, :], in_=ap.rearrange("(c p) -> c p", p=128)),
                waits=[vec_tp_prev[0]])
            tp = P.op("tensor", lambda e, n=nch: e.transpose(
                psB[:, 1536:1536 + n], vraw[0:n, :], ident_f[0:n, 0:n]),
                waits=[tl, t_ident, P.tok("ve")], inc="pe")
            vec_tp_prev[0] = tp
            if scale is None:
                tv = P.op("vector", lambda e, d=dst, n=nch: e.tensor_copy(
                    out=d, in_=psB[:, 1536:1536 + n]), waits=[tp], inc="ve")
            else:
                tv = P.op("vector", lambda e, d=dst, n=nch, sc=scale: e.tensor_scalar(
                    out=d, in0=psB[:, 1536:1536 + n], scalar1=sc, scalar2=None,
                    op0=ALU.mult), waits=[tp], inc="ve")
            return tv

        t_qg = load_vec(QG, 16, qg_sb, scale=ISQ)
        t_kg = load_vec(KG, 16, kg_sb)
        t_sd = load_vec(SD, 16, sd_sb)
        t_sg = load_vec(SG, 44, sg_sb)
        t_su = load_vec(SU, 44, su_sb)

        w_i32 = [A.at(132 * KB, [128, HID], I32), A.at(80 * KB, [128, HID], I32)]
        w_bf = [A.at(156 * KB, [128, HID], BF16), A.at(104 * KB, [128, HID], BF16)]
        wT_stage = [A.at(168 * KB, [128, NHC, 128], BF16),
                    A.at(116 * KB, [128, NHC, 128], BF16)]
        pp_pt = [psB[:, 0:512], psB[:, 512:1024], psB[:, 1024:1536], psB[:, 1536:2048]]
        war_ppi = [None, None]
        war_ppbf = [None, None]
        war_ppstage = [None, None]
        pp_ctr = [0]
        pp_gctr = [0]
        pp_pt_war = [None, None, None, t_su]
        t_ppst = {"wd": None, "wg": None, "wu": None}

        def preprocess_w(tag, W_ap, ncols, scale_sb, t_scale, ntiles, store_fn):
            nblk = ncols // 128
            for tl_i in range(ntiles):
                ts_ = pp_ctr[0] % 2
                pp_ctr[0] += 1
                tl = P.dma("sync", "ppw%d" % ts_, lambda e, tl_i=tl_i, ncols=ncols, ts_=ts_: e.dma_start(
                    out=w_i32[ts_][:, 0:ncols], in_=W_ap[tl_i * 128:(tl_i + 1) * 128, :]),
                    waits=[war_ppi[ts_]])
                tdq = P.op("gpsimd", lambda e, tl_i=tl_i, ncols=ncols, ts_=ts_: e.tensor_scalar(
                    out=w_bf[ts_][:, 0:ncols], in0=w_i32[ts_][:, 0:ncols], scalar1=-128.0,
                    scalar2=scale_sb[:, tl_i:tl_i + 1], op0=ALU.add, op1=ALU.mult),
                    waits=[tl, t_scale, war_ppbf[ts_]], inc="ge")
                war_ppi[ts_] = tdq
                tev_toks = []
                tp_last = None
                for g in range(0, nblk, 4):
                    cnt = min(4, nblk - g)
                    gsl = pp_gctr[0] % 4
                    pp_gctr[0] += 1

                    def tpf(e, g=g, cnt=cnt, gsl=gsl, ts_=ts_):
                        ins = None
                        for j in range(cnt):
                            ins = e.matmul(pp_pt[gsl][:, j * 128:(j + 1) * 128],
                                           w_bf[ts_][:, (g + j) * 128:(g + j + 1) * 128],
                                           ident_bf[:, :], start=True, stop=True)
                        return ins
                    tp = P.op("tensor", tpf,
                              waits=[tdq, t_ident, pp_pt_war[gsl]], inc="pe")
                    tp_last = tp
                    eng = "vector" if pp_gctr[0] % 2 == 0 else "scalar"
                    if eng == "vector":
                        tev = P.op("vector", lambda e, g=g, cnt=cnt, gsl=gsl, ts_=ts_: e.tensor_copy(
                            out=wT_stage[ts_][:, g:g + cnt, :],
                            in_=pp_pt[gsl][:, 0:cnt * 128]
                            .rearrange("p (c t) -> p c t", c=cnt)),
                            waits=[tp, war_ppstage[ts_]], inc="ve")
                    else:
                        tev = P.op("scalar", lambda e, g=g, cnt=cnt, gsl=gsl, ts_=ts_: e.activation(
                            out=wT_stage[ts_][:, g:g + cnt, :],
                            in_=pp_pt[gsl][:, 0:cnt * 128]
                            .rearrange("p (c t) -> p c t", c=cnt),
                            func=AF.Identity),
                            waits=[tp, war_ppstage[ts_]], inc="se")
                    pp_pt_war[gsl] = tev
                    tev_toks.append(tev)
                war_ppbf[ts_] = tp_last
                tst = P.dma("gpsimd", "pps%d" % ts_, store_fn(tl_i, ts_), waits=tev_toks)
                war_ppstage[ts_] = tst
                t_ppst[tag] = tst

        def wd_store(tl_i, ts_):
            return lambda e, tl_i=tl_i, ts_=ts_: e.dma_start(
                out=scr_wdT[:, :, tl_i * 128:(tl_i + 1) * 128].rearrange("c p t -> p c t"),
                in_=wT_stage[ts_][:, 0:NHC, :])

        def wg_store(tl_i, ts_):
            return lambda e, tl_i=tl_i, ts_=ts_: e.dma_start(
                out=scr_wgT[tl_i],
                in_=wT_stage[ts_][:, 0:16, :].rearrange("p c t -> p (c t)"))

        def wu_store(tl_i, ts_):
            return lambda e, tl_i=tl_i, ts_=ts_: e.dma_start(
                out=scr_wuT[tl_i],
                in_=wT_stage[ts_][:, 0:16, :].rearrange("p c t -> p (c t)"))

        preprocess_w("wd", WQDOWN, HID, sd_sb, t_sd, 16, wd_store)
        preprocess_w("wg", WQGATE, D, sg_sb, t_sg, NHC, wg_store)
        preprocess_w("wu", WQUP, D, su_sb, t_su, NHC, wu_store)
        t_wdT_done = t_ppst["wd"]
        t_wgT_done = t_ppst["wg"]
        t_wuT_done = t_ppst["wu"]

        xa = [A.at(180 * KB, [128, D], F32), A.at(188 * KB, [128, D], F32)]
        h_bf = [A.at(196 * KB, [128, D], BF16), A.at(200 * KB, [128, D], BF16)]
        hT_stage = [A.at(68 * KB, [128, 16, 128], BF16), A.at(76 * KB, [128, 16, 128], BF16)]
        junk_bf = A.at(72 * KB, [128, D], BF16)
        war_x = [None, None]
        war_hstage = [None, None]
        pre_sqrt = {}
        pre_th = {}
        tp_pre = {}
        tev_pre = {}
        t_sthT = [None] * NT

        def pre_stage2(i):
            sl = i % 2
            th = ln_stage2("pre", sl, xa[sl], h_bf[sl], pre_sqrt[i],
                           extra=[tp_pre.get(i - 2)])
            pre_th[i] = th
            war_x[sl] = th

        def pre_T(i):
            sl = i % 2

            def tpf(e, sl=sl):
                ins = None
                for c in range(16):
                    ins = e.matmul(psA[:, c * 128:(c + 1) * 128],
                                   h_bf[sl][:, c * 128:(c + 1) * 128],
                                   ident_bf[:, :], start=True, stop=True)
                return ins
            tp = P.op("tensor", tpf, waits=[pre_th[i], t_ident, tev_pre.get(i - 1)],
                      inc="pe")
            tp_pre[i] = tp

        def pre_evac(i):
            sl = i % 2
            tev = P.op("vector", lambda e, sl=sl: e.tensor_copy(
                out=hT_stage[sl][:, :, :],
                in_=psA[:, 0:2048].rearrange("p (c t) -> p c t", c=16)),
                waits=[tp_pre[i], war_hstage[sl]], inc="ve")
            tev_pre[i] = tev
            tst = P.dma("gpsimd", "hts%d" % sl, lambda e, i=i, sl=sl: e.dma_start(
                out=scr_hT[:, :, i * 128:(i + 1) * 128].rearrange("c p t -> p c t"),
                in_=hT_stage[sl][:, :, :]),
                waits=[tev])
            war_hstage[sl] = tst
            t_sthT[i] = tst

        for i in range(NT):
            sl = i % 2
            s, blk = divmod(i, 16)
            row0 = blk * 128
            tx = P.dma("sync", "xa%d" % sl, lambda e, s=s, row0=row0, sl=sl: e.dma_start(
                out=xa[sl], in_=X[s, row0:row0 + 128, :]),
                waits=[war_x[sl]])
            if i >= 2:
                pre_evac(i - 2)
            if i >= 1:
                pre_stage2(i - 1)
                pre_T(i - 1)
            P.pending["vector"].append(tx)
            pre_sqrt[i] = ln_stage1("pre", sl, xa[sl], junk_bf, EPS1)
        pre_evac(NT - 2)
        pre_stage2(NT - 1)
        pre_T(NT - 1)
        pre_evac(NT - 1)

        P.barrier()

        wr_sb = [A.at(4 * KB, [128, 16, 2048], BF16), A.at(68 * KB, [128, 16, 2048], BF16)]
        hT_t = [A.at(132 * KB, [128, 16, 256], BF16), A.at(140 * KB, [128, 16, 256], BF16)]
        q32s = [A.at(148 * KB, [128, D], F32), A.at(156 * KB, [128, D], F32)]
        qkvT_stage = [A.at(164 * KB, [128, 16, 128], BF16), A.at(168 * KB, [128, 16, 128], BF16)]
        junk2 = A.at(172 * KB, [128, D], BF16)
        q_bf = [A.at(176 * KB, [128, D], BF16), A.at(180 * KB, [128, D], BF16)]

        t_w = [None, None, None]
        t_w[0] = P.dma("gpsimd", "wr0", lambda e: e.dma_start(
            out=wr_sb[0], in_=WQKV[:, 0:2048].rearrange("(c p) e -> p c e", p=128)))
        t_w[1] = P.dma("gpsimd", "wr1", lambda e: e.dma_start(
            out=wr_sb[1], in_=WQKV[:, 2048:4096].rearrange("(c p) e -> p c e", p=128)))

        last_st_q = [None] * BL
        last_st_k = [None] * BL
        last_st_v = [None] * BL
        war_hT = [None, None]
        war_ps = [None, None]
        war_qstage = [None, None]
        war_vbf = [None, None]
        qk_tp = {}
        qk_th = {}
        qk_sqrt = {}
        t_ht = {}

        def qk_stage2(r, i):
            sl = i % 2
            th = ln_stage2("qk", sl, q32s[sl], q_bf[sl],
                           qk_sqrt[(r, i)], extra=[qk_tp.get((r, i - 2)),
                                                   qk_tp.get((r - 1, NT - 2 + sl)) if i < 2 else None])
            qk_th[(r, i)] = th

        def qk_T(r, i):
            sl = i % 2
            ps = psA if sl == 0 else psB

            def tpf(e, sl=sl, ps=ps):
                ins = None
                for c in range(16):
                    ins = e.matmul(ps[:, c * 128:(c + 1) * 128],
                                   q_bf[sl][:, c * 128:(c + 1) * 128],
                                   ident_bf[:, :], start=True, stop=True)
                return ins
            tp = P.op("tensor", tpf, waits=[qk_th[(r, i)]], inc="pe")
            qk_tp[(r, i)] = tp

        def qk_gevac(r, i):
            sl = i % 2
            ps = psA if sl == 0 else psB
            s, blk = divmod(i, 16)
            row0 = blk * 128
            gam = qg_sb if r == 0 else kg_sb
            gam_b = gam.unsqueeze(2).broadcast_to([128, 16, 128])

            def gevf(e, sl=sl, ps=ps, gam_b=gam_b):
                return e.tensor_tensor(
                    out=qkvT_stage[sl][:, :, :],
                    in0=ps[:, 0:2048].rearrange("p (c t) -> p c t", c=16),
                    in1=gam_b, op=ALU.mult)
            tgev = P.op("vector", gevf,
                        waits=[qk_tp[(r, i)], war_qstage[sl],
                               (t_qg if r == 0 else t_kg)], inc="ve")
            war_ps[sl] = tgev
            scr = scr_qT if r == 0 else scr_kT
            tst = P.dma("gpsimd", "qks%d" % sl, lambda e, s=s, row0=row0, sl=sl, scr=scr: e.dma_start(
                out=scr[s, :, :, row0:row0 + 128].rearrange("c p t -> p c t"),
                in_=qkvT_stage[sl][:, :, :]),
                waits=[tgev])
            war_qstage[sl] = tst
            if r == 0:
                last_st_q[s] = tst
            else:
                last_st_k[s] = tst

        for r in range(3):
            for i in range(NT):
                sl = i % 2
                s, blk = divmod(i, 16)
                jp = i // 2
                hsl = jp % 2
                if i % 2 == 0:
                    w_pre = [t_sthT[i], t_sthT[i + 1]] if r == 0 else []
                    t_ht[(r, jp)] = P.dma("sync", "ht%d" % hsl, lambda e, i=i, hsl=hsl: e.dma_start(
                        out=hT_t[hsl], in_=scr_hT[:, :, i * 128:(i + 2) * 128]
                        .rearrange("c p t -> p c t")),
                        waits=w_pre + [war_hT[hsl]])
                ps = psA if sl == 0 else psB
                toff = (i % 2) * 128

                if r <= 1:
                    if i >= 1:
                        qk_stage2(r, i - 1)

                def mmf(e, hsl=hsl, toff=toff, r=r, ps=ps):
                    ins = None
                    for d in range(16):
                        for es in range(4):
                            ins = e.matmul(ps[:, es * 512:(es + 1) * 512],
                                           hT_t[hsl][:, d, toff:toff + 128],
                                           wr_sb[r % 2][:, d, es * 512:(es + 1) * 512],
                                           start=(d == 0), stop=(d == 15))
                    return ins
                tmm = P.op("tensor", mmf,
                           waits=[t_ht[(r, jp)], t_w[r], war_ps[sl]], inc="pe")
                if i % 2 == 1:
                    war_hT[hsl] = tmm
                if r <= 1:
                    if i >= 1:
                        qk_T(r, i - 1)
                        qk_gevac(r, i - 1)
                    tcp = P.op("vector", lambda e, ps=ps, sl=sl: e.tensor_copy(
                        out=q32s[sl], in_=ps),
                        waits=[tmm, ln_state.get("qk", {}).get("th", [None, None])[sl]],
                        inc="ve")
                    qk_sqrt[(r, i)] = ln_stage1("qk", sl, q32s[sl], junk2, EPS2,
                                                src_tok=tcp)
                else:
                    tcp = P.op("vector", lambda e, ps=ps, sl=sl: e.tensor_copy(
                        out=q_bf[sl], in_=ps),
                        waits=[tmm, war_vbf[sl], qk_tp.get((1, NT - 2 + sl))],
                        inc="ve")
                    war_ps[sl] = tcp
                    tstv = P.dma("gpsimd", "vs%d" % sl, lambda e, s=s, blk=blk, sl=sl: e.dma_start(
                        out=scr_v[s, blk, :, :], in_=q_bf[sl]),
                        waits=[tcp])
                    war_vbf[sl] = tstv
                    last_st_v[s] = tstv
            if r <= 1:
                qk_stage2(r, NT - 1)
                qk_T(r, NT - 1)
                qk_gevac(r, NT - 1)
            if r == 0:
                t_w[2] = P.dma("gpsimd", "wr0", lambda e: e.dma_start(
                    out=wr_sb[0], in_=WQKV[:, 4096:6144]
                    .rearrange("(c p) e -> p c e", p=128)),
                    waits=[P.tok("pe")])

        P.barrier()

        warm_act(AF.Exp)
        kT_sb = A.at(4 * KB, [128, 16, S], BF16)
        vT_sb = A.at(68 * KB, [128, 16, D], BF16)
        qT_sb = A.at(132 * KB, [128, 16, 512], BF16)
        pT_sb = A.at(148 * KB, [128, 16, 512], BF16)
        p_bf = [A.at(164 * KB, [128, S], BF16), A.at(168 * KB, [128, S], BF16),
                A.at(182 * KB, [128, S], BF16), A.at(186 * KB, [128, S], BF16)]
        psc_bf = [A.at(172 * KB, [128, S], BF16), A.at(176 * KB, [128, S], BF16),
                  A.at(196 * KB, [128, S], BF16), A.at(200 * KB, [128, S], BF16)]
        cT_h = [A.at(180 * KB, [128, 512], BF16), A.at(181 * KB, [128, 512], BF16)]
        sc_ps = [psA[:, 0:1024], psA[:, 1024:2048]]
        pt_ps = [psB[:, 0:512], psB[:, 512:1024]]
        ctx_ps = [psB[:, 1024:1536], psB[:, 1536:2048]]

        t_stctx = {}
        piece_ctr = [0]
        gctr = [0]
        war_sc = [None, None]
        war_pt = [None, None]
        war_ctx = [None, None]
        war_pbf = [None, None, None, None]
        war_psc = [None, None, None, None]
        war_negm = [None] * 4
        war_cth = [None, None]
        war_qT = None
        war_pT = None

        def emit_softmax(qt, h, sub, tqT, tkT):
            gsub = qt * 4 + sub
            k_len = (gsub + 1) * 128
            pieces = []
            k0 = 0
            while k0 < k_len:
                kw = min(1024, k_len - k0)
                pieces.append((k0, kw))
                k0 += kw
            ptoks = []
            for pidx, (k0, kw) in enumerate(pieces):
                psl = piece_ctr[0] % 2
                piece_ctr[0] += 1

                def scf(e, h=h, sub=sub, k0=k0, kw=kw, psl=psl):
                    ins = None
                    for ks in range(0, kw, 512):
                        w = min(512, kw - ks)
                        ins = e.matmul(
                            sc_ps[psl][:, ks:ks + w],
                            qT_sb[:, h, sub * 128:(sub + 1) * 128],
                            kT_sb[:, h, k0 + ks:k0 + ks + w],
                            start=True, stop=True)
                    return ins
                tsc = P.op("tensor", scf, waits=[tqT, tkT, war_sc[psl]], inc="pe")
                if k0 + kw == k_len:
                    tsc = P.op("vector", lambda e, psl=psl, kw=kw: e.tensor_tensor(
                        out=sc_ps[psl][:, kw - 128:kw],
                        in0=sc_ps[psl][:, kw - 128:kw],
                        in1=cmask[:, :], op=ALU.add),
                        waits=[tsc, t_ident], inc="ve")
                if len(pieces) == 1:
                    tmx = P.op("vector", lambda e, psl=psl, kw=kw, sub=sub: e.reduce_max(
                        negmax[sub], sc_ps[psl][:, 0:kw], axis=AX.X, negate=True),
                        waits=[tsc, war_negm[sub]], inc="ve")
                else:
                    tmx = P.op("vector", lambda e, psl=psl, kw=kw, pidx=pidx: e.reduce_max(
                        pmax[pidx], sc_ps[psl][:, 0:kw], axis=AX.X, negate=True),
                        waits=[tsc], inc="ve")
                ptoks.append((psl, k0, kw, tmx, pidx))
            if len(ptoks) == 1:
                tneg = ptoks[0][3]
            else:
                tneg = P.op("vector", lambda e, sub=sub: e.tensor_tensor(
                    out=negmax[sub], in0=pmax[0], in1=pmax[1], op=ALU.min),
                    waits=[ptoks[0][3], ptoks[1][3], war_negm[sub]], inc="ve")
            exp_toks = []
            for (psl, k0, kw, _tmx, pidx) in ptoks:
                tex = P.op("scalar", lambda e, psl=psl, k0=k0, kw=kw, sub=sub, pidx=pidx: e.activation(
                    out=p_bf[sub][:, k0:k0 + kw], in_=sc_ps[psl][:, 0:kw],
                    func=AF.Exp, bias=negmax[sub], scale=1.0,
                    accum_out=zacc[sub * 2 + pidx]),
                    waits=[tneg, war_pbf[sub]], inc="se")
                war_sc[psl] = tex
                exp_toks.append(tex)
            war_negm[sub] = exp_toks[-1]
            return (sub, k_len, len(ptoks), exp_toks)

        def emit_scale(info):
            sub, k_len, npieces, exp_toks = info
            zw = list(exp_toks)
            if npieces > 1:
                tza = P.op("vector", lambda e, sub=sub: e.tensor_add(
                    zacc[sub * 2], zacc[sub * 2], zacc[sub * 2 + 1]),
                    waits=exp_toks, inc="ve")
                zw = [tza]
            trz = P.op("vector", lambda e, sub=sub: e.reciprocal(
                rz[sub], zacc[sub * 2]), waits=zw, inc="ve")
            tps = P.op("vector", lambda e, sub=sub, k_len=k_len: e.tensor_scalar(
                out=psc_bf[sub][:, 0:k_len], in0=p_bf[sub][:, 0:k_len],
                scalar1=rz[sub], scalar2=None, op0=ALU.mult),
                waits=[trz, war_psc[sub]], inc="ve")
            war_pbf[sub] = tps
            return (sub, k_len, tps)

        def emit_T(info):
            sub, k_len, tps = info
            nkc = k_len // 128
            t_last = None
            t_last_v = None
            t_last_s = None
            for g in range(0, nkc, 4):
                cnt = min(4, nkc - g)
                gsl = gctr[0] % 2
                gctr[0] += 1

                def tpf(e, sub=sub, g=g, cnt=cnt, gsl=gsl):
                    ins = None
                    for j in range(cnt):
                        ins = e.matmul(
                            pt_ps[gsl][:, j * 128:(j + 1) * 128],
                            psc_bf[sub][:, (g + j) * 128:(g + j + 1) * 128],
                            ident_bf[:, :], start=True, stop=True)
                    return ins
                tp = P.op("tensor", tpf, waits=[tps, war_pt[gsl]], inc="pe")
                if gctr[0] % 2 == 0:
                    tev = P.op("vector", lambda e, sub=sub, g=g, cnt=cnt, gsl=gsl: e.tensor_copy(
                        out=pT_sb[:, g:g + cnt, sub * 128:(sub + 1) * 128],
                        in_=pt_ps[gsl][:, 0:cnt * 128]
                        .rearrange("p (c t) -> p c t", c=cnt)),
                        waits=[tp, war_pT], inc="ve")
                else:
                    tev = P.op("scalar", lambda e, sub=sub, g=g, cnt=cnt, gsl=gsl: e.activation(
                        out=pT_sb[:, g:g + cnt, sub * 128:(sub + 1) * 128],
                        in_=pt_ps[gsl][:, 0:cnt * 128]
                        .rearrange("p (c t) -> p c t", c=cnt),
                        func=AF.Identity), waits=[tp, war_pT], inc="se")
                war_pt[gsl] = tev
                t_last = tev
                if tev[0] == "ve":
                    t_last_v = tev
                else:
                    t_last_s = tev
            war_psc[sub] = t_last
            return (t_last_v, t_last_s)

        def emit_ctx(s, qt, h, ev_toks, tvT):
            nonlocal war_qT, war_pT
            nkc_tot = (qt + 1) * 4

            def ctxf(e, h=h, qt=qt, nkc_tot=nkc_tot):
                ins = None
                for kc in range(nkc_tot):
                    jlo = max(0, kc - qt * 4)
                    ins = e.matmul(
                        ctx_ps[h % 2][:, jlo * 128:512],
                        vT_sb[:, kc, h * 128:(h + 1) * 128],
                        pT_sb[:, kc, jlo * 128:512],
                        start=(kc == 0), stop=(kc == nkc_tot - 1))
                return ins
            tctx = P.op("tensor", ctxf, waits=list(ev_toks) + [tvT, war_ctx[h % 2]],
                        inc="pe")
            war_pT = tctx
            tcev = P.op("scalar", lambda e, h=h: e.activation(
                out=cT_h[h % 2], in_=ctx_ps[h % 2],
                func=AF.Identity), waits=[tctx, war_cth[h % 2]], inc="se")
            war_ctx[h % 2] = tcev
            q0 = qt * 512
            tstc = P.dma("gpsimd", "cs%d" % (h % 2), lambda e, s=s, q0=q0, h=h: e.dma_start(
                out=scr_cT[s, h, :, q0:q0 + 512], in_=cT_h[h % 2]),
                waits=[tcev])
            war_cth[h % 2] = tstc
            t_stctx[(s, qt, h)] = tstc

        for s in range(BL):
            tkT = P.dma("sync", "kt", lambda e, s=s: e.dma_start(
                out=kT_sb, in_=scr_kT[s].rearrange("c p t -> p c t")),
                waits=[last_st_k[s], war_qT])
            tvT = P.dma("sync", "vt", lambda e, s=s: e.dma_start(
                out=vT_sb, in_=scr_v[s].rearrange("c p t -> p c t")),
                waits=[last_st_v[s]])
            for qt in range(4):
                q0 = qt * 512
                tqT = P.dma("sync", "qtl", lambda e, s=s, q0=q0: e.dma_start(
                    out=qT_sb, in_=scr_qT[s, :, :, q0:q0 + 512].rearrange("c p t -> p c t")),
                    waits=[last_st_q[s], war_qT])
                steps = [(h, sub) for h in range(NH) for sub in range(4)]
                sm_q = {}
                sc_q = {}
                tev_q = {}

                head_ev = {}

                def do_tail(k2):
                    h2, sub2 = steps[k2]
                    tv, tsx = emit_T(sc_q[k2])
                    cur = head_ev.setdefault(h2, [None, None])
                    if tv is not None:
                        cur[0] = tv
                    if tsx is not None:
                        cur[1] = tsx
                    if sub2 == 3:
                        emit_ctx(s, qt, h2, [t for t in cur if t is not None], tvT)

                for k, (h, sub) in enumerate(steps):
                    sm_q[k] = emit_softmax(qt, h, sub, tqT, tkT)
                    if k >= 2:
                        sc_q[k - 2] = emit_scale(sm_q[k - 2])
                    if k >= 4:
                        do_tail(k - 4)
                k = len(steps)
                sc_q[k - 2] = emit_scale(sm_q[k - 2])
                sc_q[k - 1] = emit_scale(sm_q[k - 1])
                do_tail(k - 4)
                do_tail(k - 3)
                do_tail(k - 2)
                do_tail(k - 1)
                war_qT = P.tok("pe")

        P.barrier()

        warm_act(AF.Sqrt)
        wo_sb = A.at(4 * KB, [128, 16, D], BF16)
        cT_t = [A.at(68 * KB, [128, 16, 128], BF16), A.at(72 * KB, [128, 16, 128], BF16)]
        x2a = [A.at(76 * KB, [128, D], F32), A.at(84 * KB, [128, D], F32)]
        xpa = [A.at(92 * KB, [128, D], F32), A.at(100 * KB, [128, D], F32)]
        h2_bf = [A.at(108 * KB, [128, D], BF16), A.at(112 * KB, [128, D], BF16)]
        h2T_stage = [A.at(116 * KB, [128, 16, 128], BF16), A.at(120 * KB, [128, 16, 128], BF16)]
        junk3 = A.at(124 * KB, [128, D], BF16)

        t_wo = P.dma("gpsimd", "wo", lambda e: e.dma_start(
            out=wo_sb, in_=WO.rearrange("(c p) e -> p c e", p=128)))
        war_ct2 = [None, None]
        war_x2 = [None, None]
        war_xp = [None, None]
        war_h2stage = [None, None]
        war_pso = [None, None]
        b_tp = {}
        b_sqrt = {}
        b_th = {}
        b_txp = {}
        t_stxp = [None] * NT
        t_sth2 = [None] * NT

        def b_stage2(i):
            sl = i % 2
            th = ln_stage2("ln2", sl, xpa[sl], h2_bf[sl], b_sqrt[i],
                           extra=[b_tp.get(i - 2)])
            b_th[i] = th
            war_xp[sl] = th

        def b_T(i):
            sl = i % 2
            ps = psA if sl == 0 else psB

            def tpf(e, sl=sl, ps=ps):
                ins = None
                for c in range(16):
                    ins = e.matmul(ps[:, c * 128:(c + 1) * 128],
                                   h2_bf[sl][:, c * 128:(c + 1) * 128],
                                   ident_bf[:, :], start=True, stop=True)
                return ins
            tp = P.op("tensor", tpf, waits=[b_th[i], b_txp[i]], inc="pe")
            b_tp[i] = tp

        def b_evac(i):
            sl = i % 2
            ps = psA if sl == 0 else psB
            tev = P.op("scalar", lambda e, sl=sl, ps=ps: e.activation(
                out=h2T_stage[sl][:, :, :],
                in_=ps[:, 0:2048].rearrange("p (c t) -> p c t", c=16),
                func=AF.Identity),
                waits=[b_tp[i], war_h2stage[sl]], inc="se")
            war_pso[sl] = tev
            tst2 = P.dma("gpsimd", "h2s%d" % sl, lambda e, i=i, sl=sl: e.dma_start(
                out=scr_h2T[:, :, i * 128:(i + 1) * 128].rearrange("c p t -> p c t"),
                in_=h2T_stage[sl][:, :, :]),
                waits=[tev])
            war_h2stage[sl] = tst2
            t_sth2[i] = tst2

        for i in range(NT):
            sl = i % 2
            s, blk = divmod(i, 16)
            row0 = blk * 128
            if i >= 1:
                b_stage2(i - 1)
            tct = P.dma("sync", "ct%d" % sl, lambda e, s=s, row0=row0, sl=sl, blk=blk: e.dma_start(
                out=cT_t[sl], in_=scr_cT[s, :, :, row0:row0 + 128]
                .rearrange("c p t -> p c t")),
                waits=[t_stctx[(s, blk // 4, NH - 1)], t_stctx[(s, blk // 4, NH - 2)],
                       war_ct2[sl]])
            tx = P.dma("sync", "x2%d" % sl, lambda e, s=s, row0=row0, sl=sl: e.dma_start(
                out=x2a[sl], in_=X[s, row0:row0 + 128, :]),
                waits=[war_x2[sl]])
            ps = psA if sl == 0 else psB

            def mmf(e, sl=sl, ps=ps):
                ins = None
                for d in range(16):
                    for es in range(4):
                        ins = e.matmul(ps[:, es * 512:(es + 1) * 512],
                                       cT_t[sl][:, d, :],
                                       wo_sb[:, d, es * 512:(es + 1) * 512],
                                       start=(d == 0), stop=(d == 15))
                return ins
            tmm = P.op("tensor", mmf, waits=[tct, t_wo, war_pso[sl]], inc="pe")
            if i >= 1:
                b_T(i - 1)
                b_evac(i - 1)
            txp = P.op("vector", lambda e, sl=sl, ps=ps: e.scalar_tensor_tensor(
                out=xpa[sl], in0=ps, scalar=1.0, in1=x2a[sl],
                op0=ALU.mult, op1=ALU.add),
                waits=[tmm, tx, war_xp[sl], t_stxp[i - 2] if i >= 2 else None],
                inc="ve")
            b_txp[i] = txp
            war_ct2[sl] = tmm
            war_x2[sl] = txp
            tstx = P.dma("gpsimd", "xps%d" % sl, lambda e, i=i, sl=sl: e.dma_start(
                out=scr_xp[i * 128:(i + 1) * 128, :], in_=xpa[sl]),
                waits=[txp])
            t_stxp[i] = tstx
            b_sqrt[i] = ln_stage1("ln2", sl, xpa[sl], junk3, EPS1, src_tok=txp)
        b_stage2(NT - 1)
        b_T(NT - 1)
        b_evac(NT - 1)

        P.barrier()

        warm_act(AF.Silu)
        h2T_sb = A.at(4 * KB, [128, 16, 2048], BF16)
        wgT_t = [A.at(68 * KB, [128, 16, 128], BF16), A.at(72 * KB, [128, 16, 128], BF16)]
        wuT_t = [A.at(76 * KB, [128, 16, 128], BF16), A.at(80 * KB, [128, 16, 128], BF16)]
        silu_sb = A.at(84 * KB, [128, 2048], F32)
        tT_sb = [A.at(92 * KB, [128, 2048], BF16), A.at(96 * KB, [128, 2048], BF16)]

        war_wgl = [None, None]
        war_wul = [None, None]
        war_tT = [None, None]
        war_h2T = None
        war_psff = [None, None]
        t_sttT = []
        wld = {}

        def ffa_load(c):
            cs = c % 2
            tg = P.dma("sync", "wgl%d" % cs, lambda e, c=c, cs=cs: e.dma_start(
                out=wgT_t[cs][:, :, :].rearrange("p c t -> p (c t)"), in_=scr_wgT[c]),
                waits=[t_wgT_done, war_wgl[cs]])
            tu = P.dma("sync", "wul%d" % cs, lambda e, c=c, cs=cs: e.dma_start(
                out=wuT_t[cs][:, :, :].rearrange("p c t -> p (c t)"), in_=scr_wuT[c]),
                waits=[t_wuT_done, war_wul[cs]])
            wld[c] = (tg, tu)

        for hf in range(2):
            th2T = P.dma("sync", "h2t", lambda e, hf=hf: e.dma_start(
                out=h2T_sb, in_=scr_h2T[:, :, hf * 2048:(hf + 1) * 2048]
                .rearrange("c p t -> p c t")),
                waits=[t_sth2[NT - 1], t_sth2[NT - 2], war_h2T])
            ffa_load(0)
            ffa_load(1)
            for c in range(NHC):
                cs = c % 2
                tg, tu = wld[c]
                tq_last = None
                for qq in range(2):
                    ps = psA if qq % 2 == 0 else psB

                    def mmf(e, cs=cs, ps=ps, qq=qq):
                        ins = None
                        for d in range(16):
                            for t2 in range(2):
                                ins = e.matmul(
                                    ps[:, t2 * 512:(t2 + 1) * 512],
                                    wgT_t[cs][:, d, :],
                                    h2T_sb[:, d, qq * 1024 + t2 * 512:qq * 1024 + (t2 + 1) * 512],
                                    start=(d == 0), stop=(d == 15))
                            for t2 in range(2):
                                ins = e.matmul(
                                    ps[:, 1024 + t2 * 512:1024 + (t2 + 1) * 512],
                                    wuT_t[cs][:, d, :],
                                    h2T_sb[:, d, qq * 1024 + t2 * 512:qq * 1024 + (t2 + 1) * 512],
                                    start=(d == 0), stop=(d == 15))
                        return ins
                    tmm = P.op("tensor", mmf,
                               waits=[tg, tu, th2T, war_psff[qq % 2]], inc="pe")

                    def siluf(e, ps=ps, qq=qq):
                        ins = None
                        for t2 in range(2):
                            ins = e.activation(
                                out=silu_sb[:, qq * 1024 + t2 * 512:qq * 1024 + (t2 + 1) * 512],
                                in_=ps[:, t2 * 512:(t2 + 1) * 512],
                                func=AF.Silu)
                        return ins
                    tsi = P.op("scalar", siluf, waits=[tmm], inc="se")

                    def multf(e, cs=cs, ps=ps, qq=qq):
                        ins = None
                        for t2 in range(2):
                            ins = e.scalar_tensor_tensor(
                                out=tT_sb[cs][:, qq * 1024 + t2 * 512:qq * 1024 + (t2 + 1) * 512],
                                in0=ps[:, 1024 + t2 * 512:1024 + (t2 + 1) * 512],
                                scalar=1.0,
                                in1=silu_sb[:, qq * 1024 + t2 * 512:qq * 1024 + (t2 + 1) * 512],
                                op0=ALU.mult, op1=ALU.mult)
                        return ins
                    tmu = P.op("vector", multf, waits=[tsi, war_tT[cs]], inc="ve")
                    war_psff[qq % 2] = tmu
                    tq_last = tmu
                tst = P.dma("gpsimd", "tts%d" % cs, lambda e, c=c, hf=hf, cs=cs: e.dma_start(
                    out=scr_tT[c, :, hf * 2048:(hf + 1) * 2048], in_=tT_sb[cs]),
                    waits=[tq_last])
                war_tT[cs] = tst
                war_wgl[cs] = tmm
                war_wul[cs] = tmm
                if c + 2 < NHC:
                    ffa_load(c + 2)
                t_sttT.append(tst)
            war_h2T = P.tok("pe")

        P.barrier()

        tT_t = [A.at(4 * KB, [128, 11, 512], BF16), A.at(16 * KB, [128, 11, 512], BF16)]
        wdT_t = [A.at(28 * KB, [128, 11, D], BF16), A.at(72 * KB, [128, 11, D], BF16)]
        xp_t = [A.at(116 * KB, [128, D], F32), A.at(124 * KB, [128, D], F32)]
        acc = A.at(132 * KB, [128, 4, D], F32)

        war_tTt = [None, None]
        war_wdt = [None, None]
        war_xpt = [None, None]
        war_psd = [None, None]
        war_acc = [None] * 4
        acc_last = [None] * 4
        t_lastw_d = [None] * 4
        for tt in range(8):
            tok0 = tt * 512
            for grp in range(4):
                gs = grp % 2
                ttl = P.dma("sync", "tt%d" % gs, lambda e, grp=grp, gs=gs, tok0=tok0: e.dma_start(
                    out=tT_t[gs],
                    in_=scr_tT[grp * 11:(grp + 1) * 11, :, tok0:tok0 + 512]
                    .rearrange("c p t -> p c t")),
                    waits=[t_sttT[-1], t_sttT[-2], war_tTt[gs]])
                twd = P.dma("sync", "wdl%d" % gs, lambda e, grp=grp, gs=gs: e.dma_start(
                    out=wdT_t[gs],
                    in_=scr_wdT[grp * 11:(grp + 1) * 11, :, :]
                    .rearrange("c p t -> p c t")),
                    waits=[t_wdT_done, war_wdt[gs]])
                for sub in range(4):
                    sl = sub % 2
                    txl = None
                    if grp == 0:
                        txl = P.dma("sync", "xpl%d" % sl, lambda e, tok0=tok0, sub=sub, sl=sl: e.dma_start(
                            out=xp_t[sl],
                            in_=scr_xp[tok0 + sub * 128:tok0 + (sub + 1) * 128, :]),
                            waits=[t_stxp[NT - 1], t_stxp[NT - 2], war_xpt[sl]])
                    ps = psA if sl == 0 else psB

                    def mmf(e, sub=sub, gs=gs, ps=ps):
                        ins = None
                        for j in range(11):
                            for es in range(4):
                                ins = e.matmul(ps[:, es * 512:(es + 1) * 512],
                                               tT_t[gs][:, j, sub * 128:(sub + 1) * 128],
                                               wdT_t[gs][:, j, es * 512:(es + 1) * 512],
                                               start=(j == 0), stop=(j == 10))
                        return ins
                    tmm = P.op("tensor", mmf,
                               waits=[ttl, twd, war_psd[sl]], inc="pe")
                    if grp == 0:
                        tev = P.op("vector", lambda e, sub=sub, sl=sl, ps=ps: e.scalar_tensor_tensor(
                            out=acc[:, sub, :], in0=ps, scalar=1.0, in1=xp_t[sl],
                            op0=ALU.mult, op1=ALU.add),
                            waits=[tmm, txl, war_acc[sub], acc_last[sub]], inc="ve")
                        war_xpt[sl] = tev
                    else:
                        tev = P.op("vector", lambda e, sub=sub, ps=ps: e.tensor_tensor(
                            out=acc[:, sub, :], in0=acc[:, sub, :], in1=ps, op=ALU.add),
                            waits=[tmm, acc_last[sub]], inc="ve")
                    acc_last[sub] = tev
                    war_psd[sl] = tev
                    if grp == 3:
                        i = (tok0 // 128) + sub
                        s, blk = divmod(i, 16)
                        tw = P.dma("gpsimd", "ys%d" % sub, lambda e, s=s, blk=blk, sub=sub: e.dma_start(
                            out=Y[s, blk * 128:(blk + 1) * 128, :], in_=acc[:, sub, :]),
                            waits=[tev])
                        war_acc[sub] = tw
                        t_lastw_d[sub] = tw
                war_tTt[gs] = P.tok("pe")
                war_wdt[gs] = P.tok("pe")

        P.op("sync", lambda e: e.nop(), waits=t_lastw_d)

        P.realize(ctx)
    return nc


_NC_CACHE = {}


def kernel(L0_attn_x, L0_attn_Wqkv, L0_attn_q_ln_scale, L0_attn_k_ln_scale,
           L0_attn_out_W, L0_ffn_wq_gate, L0_ffn_s_gate, L0_ffn_wq_up,
           L0_ffn_s_up, L0_ffn_wq_down, L0_ffn_s_down):
    x = np.ascontiguousarray(np.asarray(L0_attn_x, dtype=np.float32))
    wqkv = np.ascontiguousarray(np.asarray(L0_attn_Wqkv, dtype=np.float32))
    qg = np.ascontiguousarray(np.asarray(L0_attn_q_ln_scale, dtype=np.float32))
    kg = np.ascontiguousarray(np.asarray(L0_attn_k_ln_scale, dtype=np.float32))
    wo = np.ascontiguousarray(np.asarray(L0_attn_out_W, dtype=np.float32))
    wqg = np.ascontiguousarray(np.asarray(L0_ffn_wq_gate, dtype=np.int32))
    sg = np.ascontiguousarray(np.asarray(L0_ffn_s_gate, dtype=np.float32))
    wqu = np.ascontiguousarray(np.asarray(L0_ffn_wq_up, dtype=np.int32))
    su = np.ascontiguousarray(np.asarray(L0_ffn_s_up, dtype=np.float32))
    wqd = np.ascontiguousarray(np.asarray(L0_ffn_wq_down, dtype=np.int32))
    sd = np.ascontiguousarray(np.asarray(L0_ffn_s_down, dtype=np.float32))

    if "nc" not in _NC_CACHE:
        _NC_CACHE["nc"] = build_nc()
    nc = _NC_CACHE["nc"]

    in_maps = []
    for c in range(NCORES):
        in_maps.append({
            "k_x": x[c * BL:(c + 1) * BL],
            "k_wqkv": wqkv, "k_qg": qg, "k_kg": kg, "k_wo": wo,
            "k_wqg": wqg, "k_sg": sg, "k_wqu": wqu, "k_su": su,
            "k_wqd": wqd, "k_sd": sd,
        })
    res = run_bass_kernel_spmd(nc, in_maps, core_ids=list(range(NCORES)))
    out = np.concatenate([np.asarray(r["k_y"], dtype=np.float32)
                          for r in res.results], axis=0)
    return out
